# version 16
# baseline (speedup 1.0000x reference)
import numpy as np
from contextlib import ExitStack

import concourse.bass as bass
import concourse.mybir as mybir
from concourse.bass_utils import run_bass_kernel_spmd

F32 = mybir.dt.float32
BF16 = mybir.dt.bfloat16
U32 = mybir.dt.uint32
AF = mybir.ActivationFunctionType
ALU = mybir.AluOpType
AX = mybir.AxisListType

D = 1024
T = 4096
TC = 512
NCH = T // TC
FT = D // 128
DI = 2048
DS = 16
DTR = 64
DFF = 4096
I0 = 2048
E0 = 8
E1 = 4
NTILE = 256
LN_EPS = 1e-5
RMS_EPS = 1e-6
BC_SKIP_C = False
BC_SKIP_B2 = False
BC_JB_LIMIT = 99
BC_P2_PARTS = "wmce"


class SemT:
    __slots__ = ("h", "n")

    def __init__(self, h):
        self.h = h
        self.n = 0

    def inc(self, inst, k=1):
        inst.then_inc(self.h, k)
        self.n += k
        return self.n


class Builder:
    def __init__(self, upto="D", debug=()):
        self.upto = upto
        self.debug = set(debug)
        self.nc = bass.Bass("TRN2", target_bir_lowering=False)

    def wload(self, eng, sem, dst, src_ap, nk):
        if nk == 1:
            inst = eng.dma_start(out=dst[:], in_=src_ap)
        else:
            m = src_ap.shape[-1]
            inst = eng.dma_start(
                out=dst[:].rearrange("p (k m) -> p k m", k=nk),
                in_=src_ap.rearrange("(k p) m -> p k m", k=nk, p=128))
        sem.inc(inst, 16)

    def adma(self, eng, sem, sbuf_t, dram_t, c, nk, store=False):
        dr = dram_t[:, c * TC:(c + 1) * TC]
        if nk > 1:
            dr = dr.rearrange("(k p) t -> p k t", k=nk, p=128)
            sb = sbuf_t[:].rearrange("p (k t) -> p k t", k=nk)
        else:
            sb = sbuf_t[:]
        inst = eng.dma_start(out=dr, in_=sb) if store else eng.dma_start(out=sb, in_=dr)
        sem.inc(inst, 16)


    def _bc_ev(self, v, u, e, pCH, mbsb, tscr, acb1, fb2,
               M0, PM, V0, PV, RA0, rA, smm, svv, dS, sac):
        ci, mi = u // 8, u % 8
        v.wait_ge(smm.h, M0 + PM[("cm", u)])
        v.wait_ge(svv.h, V0 + PV[("mbe", ci)])
        if e > 0:
            v.wait_ge(rA.h, RA0 + 16 * (2 * ci + 1))
        else:
            v.wait_ge(rA.h, RA0 + 16 * ci)
        acs = acb1[:, mi * (T // 8 // 1 if False else 512):(mi + 1) * 512]
        col = fb2[:, e * 8 + mi:e * 8 + mi + 1]
        if e == 0:
            svv.inc(v.scalar_tensor_tensor(acs, pCH[u % 2][:], col, mbsb[:],
                                           ALU.add, ALU.mult))
        else:
            sl = u % 2
            svv.inc(v.scalar_tensor_tensor(tscr[:, sl * 512:(sl + 1) * 512],
                                           pCH[u % 2][:], col, mbsb[:],
                                           ALU.add, ALU.mult))
            v.wait_ge(svv.h, svv.n)
            svv.inc(v.tensor_add(acs, acs, tscr[:, sl * 512:(sl + 1) * 512]))

    def emit_norm_stats(self, xc, nft, eps, with_mean, pMN, pRS):
        nc, b = self.nc, None
        sac, svv, smm = self.sac, self.svv, self.smm
        sq8, pQ, pS = self.sq8, self.pQ, self.pS
        meanr, varr, rstdr = self.meanr, self.varr, self.rstdr
        onesc, onesr = self.onesc, self.onesr
        inv_n = 1.0 / (nft * 128)
        with nc.Block() as b:
            @b.scalar
            def _(s):
                for ft in range(nft):
                    sac.inc(s.activation(sq8[:, ft * TC:(ft + 1) * TC],
                                         xc[:, ft * TC:(ft + 1) * TC], AF.Square))
        with nc.Block() as b:
            @b.tensor
            def _(t):
                if with_mean:
                    for ft in range(nft):
                        inst = t.matmul(pS[:], onesc[:], xc[:, ft * TC:(ft + 1) * TC],
                                        start=(ft == 0), stop=(ft == nft - 1))
                    smm.inc(inst)
                for ft in range(nft):
                    inst = t.matmul(pQ[:], onesc[:], sq8[:, ft * TC:(ft + 1) * TC],
                                    start=(ft == 0), stop=(ft == nft - 1))
                smm.inc(inst)
        with nc.Block() as b:
            @b.scalar
            def _(s):
                if with_mean:
                    sac.inc(s.activation(meanr[:], pS[:], AF.Copy, scale=inv_n))
                    s.wait_ge(sac.h, sac.n)
                    sac.inc(s.activation(varr[:], meanr[:], AF.Square))
        with nc.Block() as b:
            @b.vector
            def _(v):
                if with_mean:
                    svv.inc(v.scalar_tensor_tensor(varr[:], pQ[:], inv_n, varr[:],
                                                   ALU.mult, ALU.subtract))
                else:
                    svv.inc(v.tensor_scalar_mul(varr[:], pQ[:], inv_n))
        with nc.Block() as b:
            @b.scalar
            def _(s):
                sac.inc(s.activation(rstdr[:], varr[:], AF.Sqrt, bias=eps))
        with nc.Block() as b:
            @b.vector
            def _(v):
                svv.inc(v.reciprocal(rstdr[:], rstdr[:]))
        with nc.Block() as b:
            @b.tensor
            def _(t):
                if with_mean:
                    smm.inc(t.matmul(pMN[:], onesr[:], meanr[:], start=True, stop=True))
                smm.inc(t.matmul(pRS[:], onesr[:], rstdr[:], start=True, stop=True))

    def emit_router(self, xc, wsb, ne, c, mask_dram, with_score, with_bias, mstage):
        nc = self.nc
        sac, svv, smm, dS = self.sac, self.svv, self.smm, self.dS
        pR, pT = self.pR, self.pT
        mx8, max8, idx8, idxf, scoreC, mcols = (self.mx8, self.max8, self.idx8,
                                                self.idxf, self.scoreC, self.mcols)
        for tt in range(TC // 128):
            with nc.Block() as b:
                @b.tensor
                def _(t):
                    for k in range(FT):
                        inst = t.matmul(
                            pR[:, 0:ne],
                            xc[:, k * TC + tt * 128: k * TC + tt * 128 + 128],
                            wsb[:, k * ne:(k + 1) * ne],
                            start=(k == 0), stop=(k == FT - 1))
                    smm.inc(inst)
            with nc.Block() as b:
                @b.vector
                def _(v):
                    if ne < 8:
                        svv.inc(v.memset(mx8[:], -3.0e38))
                        v.wait_ge(svv.h, svv.n)
                    if with_bias:
                        svv.inc(v.tensor_add(mx8[:, 0:ne], pR[:, 0:ne], self.rb1b[:, 0:ne]))
                    else:
                        svv.inc(v.tensor_copy(mx8[:, 0:ne], pR[:, 0:ne]))
                    v.wait_ge(svv.h, svv.n)
                    svv.inc(v.max(max8[:], mx8[:]))
                    v.wait_ge(svv.h, svv.n)
                    svv.inc(v.max_index(idx8[:], max8[:], mx8[:]))
                    v.wait_ge(svv.h, svv.n)
                    svv.inc(v.tensor_copy(idxf[:], idx8[:, 0:1]))

                @b.scalar
                def _(s):
                    if with_score:
                        s.wait_ge(svv.h, svv.n - 2)
                        sac.inc(s.activation(scoreC[:], max8[:, 0:1], AF.Sigmoid))
            with nc.Block() as b:
                @b.vector
                def _(v):
                    for e in range(ne):
                        if with_score:
                            svv.inc(v.scalar_tensor_tensor(
                                mcols[:, e:e + 1], idxf[:], float(e), scoreC[:],
                                ALU.is_equal, ALU.mult))
                        else:
                            svv.inc(v.tensor_single_scalar(
                                mcols[:, e:e + 1], idxf[:], float(e), ALU.is_equal))
            with nc.Block() as b:
                @b.tensor
                def _(t):
                    smm.inc(t.transpose(pT[0:ne, 0:128], mcols[:, 0:ne], self.ident[:]))
            with nc.Block() as b:
                @b.vector
                def _(v):
                    svv.inc(v.tensor_copy(mstage[:, tt * 128:(tt + 1) * 128],
                                          pT[0:ne, 0:128]))
        with nc.Block() as b:
            @b.sync
            def _(s):
                dS.inc(s.dma_start(out=mask_dram[:, c * TC:(c + 1) * TC],
                                   in_=mstage[:]), 16)
                s.wait_ge(dS.h, dS.n)

    def emit_lnrouter_phase(self, d_x, acc_src, attnc, out_xn, out_x1, wsb, ne,
                            with_score, mask_out, do_svec, svec, eps, tag,
                            xn_sbuf=None):
        nc = self.nc
        smm, sac, svv = self.smm, self.sac, self.svv
        rX, rS, rC = self.r8[0], self.r8[1], self.r8[2]
        X0, S0, C0 = rX.n, rS.n, rC.n
        M0, A0, V0 = smm.n, sac.n, svv.n
        a4 = acc_src is not None
        MS = 12
        AS = 27 if a4 else 23
        VS = 66 if a4 else 71
        RSS = 48 if a4 else 32
        TTV = 10 if a4 else 13
        TTB = 26 if a4 else 19
        vV_x1 = 8
        vV_vc = 9 if a4 else 1
        vV_rc = 10 if a4 else 2
        vV_sub = 18 if a4 else 10
        vV_mul = 26 if a4 else 18
        vR_pRcons = 2 if a4 else 1
        vR_max = 3 if a4 else 2
        vR_mce = 9 if a4 else 12
        inv_n = 1.0 / D
        ident, onesc, onesr = self.ident, self.onesc, self.onesr
        rb1b = self.rb1b
        def store_xn(s, cp, xnbf):
            if xn_sbuf is not None:
                rS.inc(s.dma_start(
                    out=xn_sbuf[:].rearrange(
                        "p (k t) -> p k t", k=FT)[:, :, cp * TC:(cp + 1) * TC],
                    in_=xnbf[cp % 2][:].rearrange("p (k t) -> p k t", k=FT)), 16)
            else:
                self.adma(s, rS, xnbf[cp % 2], out_xn, cp, FT, store=True)

        with ExitStack() as ph:
            Ep = ph.enter_context
            x0c = [Ep(nc.sbuf_tensor(f"{tag}x0c{i}", [128, FT * TC], F32))
                   for i in range(2)]
            accb = ([Ep(nc.sbuf_tensor(f"{tag}acc{i}", [128, FT * TC], BF16))
                     for i in range(2)] if a4 else None)
            x1c = Ep(nc.sbuf_tensor(f"{tag}x1c", [128, FT * TC], F32)) if a4 else None
            xnc = Ep(nc.sbuf_tensor(f"{tag}xnc", [128, FT * TC], F32))
            sq8 = Ep(nc.sbuf_tensor(f"{tag}sq8", [128, FT * TC], F32))
            xnbf = [Ep(nc.sbuf_tensor(f"{tag}xnb{i}", [128, FT * TC], BF16))
                    for i in range(2)]
            x1bf = ([Ep(nc.sbuf_tensor(f"{tag}x1b{i}", [128, FT * TC], BF16))
                     for i in range(2)] if a4 else None)
            meanr = Ep(nc.sbuf_tensor(f"{tag}mean", [1, TC], F32))
            varr = Ep(nc.sbuf_tensor(f"{tag}varr", [1, TC], F32))
            rstdr = Ep(nc.sbuf_tensor(f"{tag}rstd", [1, TC], F32))
            svst = Ep(nc.sbuf_tensor(f"{tag}svst", [128, FT], F32))
            mx8 = Ep(nc.sbuf_tensor(f"{tag}mx8", [128, 8], F32))
            max8 = Ep(nc.sbuf_tensor(f"{tag}max8", [128, 8], F32))
            idx8 = Ep(nc.sbuf_tensor(f"{tag}idx8", [128, 8], U32))
            idxf = Ep(nc.sbuf_tensor(f"{tag}idxf", [128, 1], F32))
            scoreC = Ep(nc.sbuf_tensor(f"{tag}scor", [128, 1], F32))
            mcols = Ep(nc.sbuf_tensor(f"{tag}mcol", [128, 8], F32))
            mst = [Ep(nc.sbuf_tensor(f"{tag}mst{i}", [ne, TC], BF16))
                   for i in range(2)]
            pS = Ep(nc.psum_tensor(f"{tag}pS", [1, TC], F32))
            pQ = Ep(nc.psum_tensor(f"{tag}pQ", [1, TC], F32))
            pMN = Ep(nc.psum_tensor(f"{tag}pMN", [128, TC], F32))
            pRS = Ep(nc.psum_tensor(f"{tag}pRS", [128, TC], F32))
            pR = Ep(nc.psum_tensor(f"{tag}pR", [128, 8], F32))
            pT = Ep(nc.psum_tensor(f"{tag}pT", [128, 128], F32))

            def nsrc(c):
                return x1c if a4 else x0c[c % 2]

            with nc.Block(no_gpsimd_drain=True) as b:
                @b.sync
                def _(s):
                    for c in range(NCH):
                        if c >= 2:
                            if a4:
                                s.wait_ge(svv.h, V0 + VS * (c - 2) + vV_x1)
                            else:
                                s.wait_ge(sac.h, A0 + AS * (c - 2) + 8)
                                s.wait_ge(svv.h, V0 + VS * (c - 2) + vV_sub)
                                s.wait_ge(smm.h, M0 + MS * (c - 2) + 1)
                        self.adma(s, rX, x0c[c % 2], d_x, c, FT)
                        if c >= 1:
                            cp = c - 1
                            s.wait_ge(sac.h, A0 + AS * cp + 19)
                            store_xn(s, cp, xnbf)
                            if a4:
                                s.wait_ge(sac.h, A0 + AS * cp + 27)
                                self.adma(s, rS, x1bf[cp % 2], out_x1, cp, FT,
                                          store=True)
                            s.wait_ge(svv.h, V0 + VS * cp + VS)
                            rS.inc(s.dma_start(
                                out=mask_out[:, cp * TC:(cp + 1) * TC],
                                in_=mst[cp % 2][:]), 16)
                    cp = NCH - 1
                    s.wait_ge(sac.h, A0 + AS * cp + 19)
                    store_xn(s, cp, xnbf)
                    if a4:
                        s.wait_ge(sac.h, A0 + AS * cp + 27)
                        self.adma(s, rS, x1bf[cp % 2], out_x1, cp, FT, store=True)
                    s.wait_ge(svv.h, V0 + VS * cp + VS)
                    rS.inc(s.dma_start(out=mask_out[:, cp * TC:(cp + 1) * TC],
                                       in_=mst[cp % 2][:]), 16)
                    s.wait_ge(rS.h, rS.n)
                    s.wait_ge(rX.h, rX.n)

                if a4:
                    @b.gpsimd
                    def _(g):
                        for c in range(NCH):
                            if c >= 2:
                                g.wait_ge(svv.h, V0 + VS * (c - 2) + vV_x1)
                            self.adma(g, rC, accb[c % 2], acc_src, c, FT)
                        g.wait_ge(rC.h, rC.n)
                elif getattr(self, "late_casts", None) is not None:
                    lc = self.late_casts
                    self.late_casts = None

                    @b.gpsimd
                    def _(g):
                        g.wait_ge(rX.h, X0 + 16 * NCH)
                        lc(g)

                @b.tensor
                def _(t):
                    for c in range(NCH):
                        if a4:
                            t.wait_ge(svv.h, V0 + VS * c + vV_x1)
                        else:
                            t.wait_ge(rX.h, X0 + 16 * (c + 1))
                        if c >= 1:
                            t.wait_ge(sac.h, A0 + AS * (c - 1) + 9)
                        for ft in range(FT):
                            inst = t.matmul(pS[:], onesc[:],
                                            nsrc(c)[:, ft * TC:(ft + 1) * TC],
                                            start=(ft == 0), stop=(ft == FT - 1))
                        smm.inc(inst)
                        t.wait_ge(sac.h, A0 + AS * c + 8)
                        if c >= 1:
                            t.wait_ge(svv.h, V0 + VS * (c - 1) + vV_vc)
                        for ft in range(FT):
                            inst = t.matmul(pQ[:], onesc[:],
                                            sq8[:, ft * TC:(ft + 1) * TC],
                                            start=(ft == 0), stop=(ft == FT - 1))
                        smm.inc(inst)
                        t.wait_ge(sac.h, A0 + AS * c + 9)
                        if c >= 1:
                            t.wait_ge(svv.h, V0 + VS * (c - 1) + vV_sub)
                        smm.inc(t.matmul(pMN[:], onesr[:], meanr[:],
                                         start=True, stop=True))
                        t.wait_ge(svv.h, V0 + VS * c + vV_rc)
                        if c >= 1:
                            t.wait_ge(svv.h, V0 + VS * (c - 1) + vV_mul)
                        smm.inc(t.matmul(pRS[:], onesr[:], rstdr[:],
                                         start=True, stop=True))
                        for tt in range(TC // 128):
                            t.wait_ge(svv.h, V0 + VS * c + vV_mul)
                            if tt >= 1:
                                t.wait_ge(svv.h, V0 + VS * c + TTB
                                          + TTV * (tt - 1) + vR_pRcons)
                            elif c >= 1:
                                t.wait_ge(svv.h, V0 + VS * (c - 1) + TTB
                                          + TTV * 3 + vR_pRcons)
                            for k in range(FT):
                                inst = t.matmul(
                                    pR[:, 0:ne],
                                    xnc[:, k * TC + tt * 128: k * TC + tt * 128 + 128],
                                    wsb[:, k * ne:(k + 1) * ne],
                                    start=(k == 0), stop=(k == FT - 1))
                            smm.inc(inst)
                            t.wait_ge(svv.h, V0 + VS * c + TTB + TTV * tt + vR_mce)
                            if tt >= 1:
                                t.wait_ge(svv.h, V0 + VS * c + TTB + TTV * tt)
                            elif c >= 1:
                                t.wait_ge(svv.h, V0 + VS * c)
                            smm.inc(t.transpose(pT[0:ne, 0:128], mcols[:, 0:ne],
                                                ident[:]))
                        assert smm.n == M0 + MS * (c + 1), (smm.n, M0, c)

                @b.scalar
                def _(s):
                    for c in range(NCH):
                        if a4:
                            s.wait_ge(svv.h, V0 + VS * c + vV_x1)
                        else:
                            s.wait_ge(rX.h, X0 + 16 * (c + 1))
                        if c >= 1:
                            s.wait_ge(smm.h, M0 + MS * (c - 1) + 2)
                        for ft in range(FT):
                            sac.inc(s.activation(sq8[:, ft * TC:(ft + 1) * TC],
                                                 nsrc(c)[:, ft * TC:(ft + 1) * TC],
                                                 AF.Square))
                        s.wait_ge(smm.h, M0 + MS * c + 1)
                        sac.inc(s.activation(meanr[:], pS[:], AF.Copy, scale=inv_n))
                        s.wait_ge(sac.h, sac.n)
                        sac.inc(s.activation(varr[:], meanr[:], AF.Square))
                        s.wait_ge(svv.h, V0 + VS * c + vV_vc)
                        sac.inc(s.activation(rstdr[:], varr[:], AF.Sqrt, bias=eps))
                        s.wait_ge(svv.h, V0 + VS * c + vV_mul)
                        if c >= 2:
                            s.wait_ge(rS.h, S0 + RSS * (c - 2) + 16)
                        if do_svec and c >= 1:
                            s.wait_ge(svv.h, V0 + VS * (c - 1) + 19)
                        for ft in range(FT):
                            if do_svec:
                                sac.inc(s.activation(
                                    xnbf[c % 2][:, ft * TC:(ft + 1) * TC],
                                    xnc[:, ft * TC:(ft + 1) * TC], AF.Copy,
                                    accum_out=svst[:, ft:ft + 1]))
                            else:
                                sac.inc(s.activation(
                                    xnbf[c % 2][:, ft * TC:(ft + 1) * TC],
                                    xnc[:, ft * TC:(ft + 1) * TC], AF.Copy))
                        if a4:
                            if c >= 2:
                                s.wait_ge(rS.h, S0 + RSS * (c - 2) + 32)
                            for ft in range(FT):
                                sac.inc(s.activation(
                                    x1bf[c % 2][:, ft * TC:(ft + 1) * TC],
                                    x1c[:, ft * TC:(ft + 1) * TC], AF.Copy))
                        if with_score:
                            for tt in range(TC // 128):
                                s.wait_ge(svv.h, V0 + VS * c + TTB + TTV * tt + vR_max)
                                if tt >= 1:
                                    s.wait_ge(svv.h, V0 + VS * c + TTB
                                              + TTV * (tt - 1) + vR_mce)
                                elif c >= 1:
                                    s.wait_ge(svv.h, V0 + VS * (c - 1) + TTB
                                              + TTV * 3 + vR_mce)
                                sac.inc(s.activation(scoreC[:], max8[:, 0:1],
                                                     AF.Sigmoid))
                        assert sac.n == A0 + AS * (c + 1), (sac.n, A0, c)

                @b.vector
                def _(v):
                    for c in range(NCH):
                        if a4:
                            v.wait_ge(rX.h, X0 + 16 * (c + 1))
                            v.wait_ge(rC.h, C0 + 16 * (c + 1))
                            if c >= 1:
                                v.wait_ge(sac.h, A0 + AS * (c - 1) + 27)
                                v.wait_ge(smm.h, M0 + MS * (c - 1) + 1)
                            for ft in range(FT):
                                svv.inc(v.scalar_tensor_tensor(
                                    x1c[:, ft * TC:(ft + 1) * TC],
                                    accb[c % 2][:, ft * TC:(ft + 1) * TC],
                                    attnc[:, ft:ft + 1],
                                    x0c[c % 2][:, ft * TC:(ft + 1) * TC],
                                    ALU.add, ALU.add))
                        v.wait_ge(smm.h, M0 + MS * c + 2)
                        v.wait_ge(sac.h, A0 + AS * c + 10)
                        svv.inc(v.scalar_tensor_tensor(varr[:], pQ[:], inv_n,
                                                       varr[:], ALU.mult,
                                                       ALU.subtract))
                        v.wait_ge(sac.h, A0 + AS * c + 11)
                        svv.inc(v.reciprocal(rstdr[:], rstdr[:]))
                        v.wait_ge(smm.h, M0 + MS * c + 3)
                        if c >= 1:
                            v.wait_ge(sac.h, A0 + AS * (c - 1) + 19)
                            v.wait_ge(smm.h, M0 + MS * (c - 1) + 11)
                        for ft in range(FT):
                            svv.inc(v.tensor_sub(xnc[:, ft * TC:(ft + 1) * TC],
                                                 nsrc(c)[:, ft * TC:(ft + 1) * TC],
                                                 pMN[:]))
                        v.wait_ge(smm.h, M0 + MS * c + 4)
                        v.wait_ge(svv.h, svv.n)
                        for ft in range(FT):
                            svv.inc(v.tensor_mul(xnc[:, ft * TC:(ft + 1) * TC],
                                                 xnc[:, ft * TC:(ft + 1) * TC],
                                                 pRS[:]))
                        if do_svec:
                            v.wait_ge(sac.h, A0 + AS * c + 19)
                            svv.inc(v.tensor_add(svec[:], svec[:], svst[:]))
                        for tt in range(TC // 128):
                            v.wait_ge(smm.h, M0 + MS * c + 5 + 2 * tt)
                            if with_score:
                                if tt >= 1:
                                    v.wait_ge(sac.h, A0 + AS * c + 20 + tt - 1)
                                elif c >= 1:
                                    v.wait_ge(sac.h, A0 + AS * (c - 1) + 23)
                            if a4:
                                svv.inc(v.memset(mx8[:], -3.0e38))
                                v.wait_ge(svv.h, svv.n)
                                svv.inc(v.tensor_add(mx8[:, 0:ne], pR[:, 0:ne],
                                                     rb1b[:, 0:ne]))
                            else:
                                svv.inc(v.tensor_copy(mx8[:, 0:ne], pR[:, 0:ne]))
                            v.wait_ge(svv.h, svv.n)
                            svv.inc(v.max(max8[:], mx8[:]))
                            v.wait_ge(svv.h, svv.n)
                            svv.inc(v.max_index(idx8[:], max8[:], mx8[:]))
                            v.wait_ge(svv.h, svv.n)
                            svv.inc(v.tensor_copy(idxf[:], idx8[:, 0:1]))
                            v.wait_ge(svv.h, svv.n)
                            if with_score:
                                v.wait_ge(sac.h, A0 + AS * c + 20 + tt)
                            for e in range(ne):
                                if with_score:
                                    svv.inc(v.scalar_tensor_tensor(
                                        mcols[:, e:e + 1], idxf[:], float(e),
                                        scoreC[:], ALU.is_equal, ALU.mult))
                                else:
                                    svv.inc(v.tensor_single_scalar(
                                        mcols[:, e:e + 1], idxf[:], float(e),
                                        ALU.is_equal))
                            v.wait_ge(smm.h, M0 + MS * c + 6 + 2 * tt)
                            if c >= 2 and tt == 0:
                                v.wait_ge(rS.h, S0 + RSS * (c - 2) + RSS)
                            svv.inc(v.tensor_copy(
                                mst[c % 2][:, tt * 128:(tt + 1) * 128],
                                pT[0:ne, 0:128]))
                        assert svv.n == V0 + VS * (c + 1), (svv.n, V0, c)

    def build(self):
        nc = self.nc
        dbg = self.debug

        def dram(name, shape, dtype=F32, kind="ExternalInput"):
            return nc.dram_tensor(name, shape, dtype, kind=kind)

        def scratch(name, shape, dtype):
            kind = "ExternalOutput" if name in dbg else "Internal"
            return nc.dram_tensor(name, shape, dtype, kind=kind)

        d_x = dram("x", [D, T])
        d_wvt = dram("wv_t", [D, D])
        d_wot = dram("wo_t", [D, D])
        d_gatew = dram("gate_w", [D, E0])
        d_shg = dram("sh_gate", [D, I0])
        d_shu = dram("sh_up", [D, I0])
        d_shd = dram("sh_down", [I0, D])
        d_rtg = dram("rt_gate", [E0, D, I0])
        d_rtu = dram("rt_up", [E0, D, I0])
        d_rtd = dram("rt_down", [E0, I0, D])
        d_n1w = dram("n1w", [128, FT])
        d_n2w = dram("n2w", [128, FT])
        d_inpt = dram("inproj_t", [D, 2 * DI])
        d_convw = dram("convw", [128, 64])
        d_convb = dram("convb", [128, 16])
        d_xpt = dram("xproj_t", [DI, 96])
        d_dtt = dram("dtproj_t", [DTR, DI])
        d_dtb = dram("dtb", [128, 16])
        d_alog = dram("alog_s", [128, NTILE])
        d_dcol = dram("dcol", [128, 16])
        d_outt = dram("outproj_t", [DI, D])
        d_w1t = dram("w1_t", [D, DFF])
        d_w3t = dram("w3_t", [D, DFF])
        d_w2t = dram("w2_t", [DFF, D])
        d_rw1 = dram("router_w", [D, E1])
        d_rb1 = dram("router_b", [1, E1])
        d_fw1 = dram("f_w1", [E1, D, DFF])
        d_fb1 = dram("f_b1", [128, E1 * 32])
        d_fw2 = dram("f_w2", [E1, DFF, D])
        d_fb2 = dram("f_b2", [128, E1 * 8])
        d_e8sel = dram("e8sel", [8, 8 * 128])
        d_ident = dram("ident", [128, 128])
        d_onesc = dram("ones_col", [128, 1])
        d_onesr = dram("ones_row", [1, 128])
        d_p16 = dram("p16", [16, 128])
        d_sel = dram("selmat", [128, 16 * 128])
        d_gmat = dram("gmat", [128, 8])

        s_xn0 = scratch("s_xn0", [D, T], BF16)
        s_mask0 = scratch("s_mask0", [E0, T], BF16)
        s_cg = scratch("s_cg", [E0 + 1, D, I0], BF16)
        s_cu = scratch("s_cu", [E0 + 1, D, I0], BF16)
        s_cd = scratch("s_cd", [E0 + 1, I0, D], BF16)
        s_acc = scratch("s_acc", [D, T], BF16)
        s_x1 = scratch("s_x1", [D, T], BF16)
        s_xn1 = scratch("s_xn1", [D, T], BF16)
        s_mask1 = scratch("s_mask1", [E1, T], BF16)
        s_xinc = scratch("s_xinc", [DI, T], BF16)
        s_sz = scratch("s_sz", [DI, T], BF16)
        s_delta = scratch("s_delta", [DI, T], BF16)
        s_bc = scratch("s_bc", [32, T], F32)
        s_y = scratch("s_y", [DI, T], BF16)
        s_x1a = scratch("s_x1a", [D, T], BF16)
        s_ab = scratch("s_ab", [DFF, T], BF16)
        s_l1a = scratch("s_l1a", [D, T], BF16)
        s_h1 = scratch("s_h1", [DFF, T], BF16)
        s_acc1 = scratch("s_acc1", [D, T], BF16)
        d_out = dram("out", [D, T], F32, kind="ExternalOutput")

        st = ExitStack()
        with st:
            E = st.enter_context
            self.dS = dS = SemT(E(nc.semaphore("dS")))
            self.dG = dG = SemT(E(nc.semaphore("dG")))
            self.smm = smm = SemT(E(nc.semaphore("smm")))
            self.sac = sac = SemT(E(nc.semaphore("sac")))
            self.svv = svv = SemT(E(nc.semaphore("svv")))
            self.sgp = sgp = SemT(E(nc.semaphore("sgp")))
            r8 = [SemT(E(nc.semaphore(f"r8_{i}"))) for i in range(8)]
            self.r8 = r8
            rB = SemT(E(nc.semaphore("rB")))
            rH = SemT(E(nc.semaphore("rH")))
            rM = SemT(E(nc.semaphore("rM")))
            rA = SemT(E(nc.semaphore("rA")))

            self.ident = ident = E(nc.sbuf_tensor("identS", [128, 128], F32))
            self.onesc = onesc = E(nc.sbuf_tensor("onescS", [128, 1], F32))
            self.onesr = onesr = E(nc.sbuf_tensor("onesrS", [1, 128], F32))
            n1w = E(nc.sbuf_tensor("n1wS", [128, FT], F32))
            n2w = E(nc.sbuf_tensor("n2wS", [128, FT], F32))
            convw = E(nc.sbuf_tensor("convwS", [128, 64], F32))
            convb = E(nc.sbuf_tensor("convbS", [128, 16], F32))
            dtb = E(nc.sbuf_tensor("dtbS", [128, 16], F32))
            dcol = E(nc.sbuf_tensor("dcolS", [128, 16], F32))
            fb1 = E(nc.sbuf_tensor("fb1S", [128, E1 * 32], F32))
            fb2 = E(nc.sbuf_tensor("fb2S", [128, E1 * 8], F32))
            gatew = E(nc.sbuf_tensor("gatewS", [128, FT * E0], F32))
            rw1 = E(nc.sbuf_tensor("rw1S", [128, FT * E1], F32))
            self.rb1b = rb1b = E(nc.sbuf_tensor("rb1bS", [128, E1], F32))
            onesrb = E(nc.sbuf_tensor("onesrbS", [1, 128], BF16))
            cvals = [LN_EPS, RMS_EPS, -1.0, 1.0 / D, 0.70710678118654752]
            cbuf = E(nc.sbuf_tensor("cbufS", [128, len(cvals)], F32))
            for ci, cv in enumerate(cvals):
                nc.const_aps.register(F32, cv, cbuf[:, ci:ci + 1])
            svec = E(nc.sbuf_tensor("svecS", [128, FT], F32))
            attnc = E(nc.sbuf_tensor("attncS", [128, FT], F32))
            rbrow = E(nc.sbuf_tensor("rbrowS", [1, E1], F32))

            stxn = ExitStack()
            e8selS = stxn.enter_context(nc.sbuf_tensor("e8selS", [8, 8 * 128], BF16))
            with nc.Block(no_gpsimd_drain=True) as b:
                @b.sync
                def _(s):
                    for dst, src in [(ident, d_ident), (onesc, d_onesc), (onesr, d_onesr),
                                     (n1w, d_n1w),
                                     (n2w, d_n2w), (convw, d_convw), (convb, d_convb),
                                     (dtb, d_dtb), (dcol, d_dcol), (fb1, d_fb1),
                                     (fb2, d_fb2), (rbrow, d_rb1)]:
                        dS.inc(s.dma_start(out=dst[:], in_=src.ap()), 16)
                    self.wload(s, dS, gatew, d_gatew.ap(), FT)
                    self.wload(s, dS, rw1, d_rw1.ap(), FT)
                    s.wait_ge(dS.h, dS.n)

                @b.gpsimd
                def _(g):
                    dG.inc(g.dma_start(out=e8selS[:], in_=d_e8sel.ap()), 16)
                    self.CAST0 = dG.n
                    dG.inc(g.dma_start(out=s_cg[0], in_=d_shg.ap()), 16)
                    dG.inc(g.dma_start(out=s_cu[0], in_=d_shu.ap()), 16)
                    dG.inc(g.dma_start(out=s_cd[0], in_=d_shd.ap()), 16)

            def _late_casts(g):
                for q in range(1, E0 + 1):
                    dG.inc(g.dma_start(out=s_cg[q], in_=d_rtg[q - 1]), 16)
                    dG.inc(g.dma_start(out=s_cu[q], in_=d_rtu[q - 1]), 16)
                    dG.inc(g.dma_start(out=s_cd[q], in_=d_rtd[q - 1]), 16)
            self.late_casts = _late_casts

            with nc.Block() as b:
                @b.vector
                def _(v):
                    svv.inc(v.memset(svec[:], 0.0))
                    svv.inc(v.memset(onesrb[:], 1.0))
                    for ci, cv in enumerate(cvals):
                        svv.inc(v.memset(cbuf[:, ci:ci + 1], cv))
            with ExitStack() as ph:
                p_rb = ph.enter_context(nc.psum_tensor("p_rb", [128, E1], F32))
                with nc.Block() as b:
                    @b.tensor
                    def _(t):
                        smm.inc(t.matmul(p_rb[:], onesr[:], rbrow[:], start=True, stop=True))
                with nc.Block() as b:
                    @b.vector
                    def _(v):
                        svv.inc(v.tensor_copy(rb1b[:], p_rb[:]))

            XN = stxn.enter_context(nc.sbuf_tensor("XN", [128, FT * T], BF16))
            mrowAll = stxn.enter_context(nc.sbuf_tensor("mrowAll", [E0, T], BF16))
            self.emit_lnrouter_phase(d_x, None, None, s_xn0, None, gatew, E0,
                                     True, mrowAll, True, svec, LN_EPS, "a1",
                                     xn_sbuf=XN)
            if self.upto == "A1":
                return nc

            with ExitStack() as ph2:
                wbig = ph2.enter_context(nc.sbuf_tensor("wbig", [128, FT * D], F32))
                vsum = ph2.enter_context(nc.sbuf_tensor("vsum", [128, FT], F32))
                pV = ph2.enter_context(nc.psum_tensor("pV", [128, FT], F32))
                for wd_, src, dst in [(d_wvt, svec, vsum), (d_wot, vsum, attnc)]:
                    with nc.Block(no_gpsimd_drain=True) as b:
                        @b.sync
                        def _(s):
                            self.wload(s, dS, wbig, wd_.ap(), FT)
                            s.wait_ge(dS.h, dS.n)
                    with nc.Block(no_gpsimd_drain=True) as b:
                        @b.tensor
                        def _(t):
                            for mt in range(FT):
                                for k in range(FT):
                                    inst = t.matmul(
                                        pV[:, mt:mt + 1],
                                        wbig[:, k * D + mt * 128: k * D + (mt + 1) * 128],
                                        src[:, k:k + 1],
                                        start=(k == 0), stop=(k == FT - 1))
                            smm.inc(inst)
                    with nc.Block(no_gpsimd_drain=True) as b:
                        @b.vector
                        def _(v):
                            svv.inc(v.tensor_copy(dst[:], pV[:]))
            if self.upto == "A2":
                return nc

            with ExitStack() as ph2:
                E2 = ph2.enter_context
                HK = 8
                wgh = [E2(nc.sbuf_tensor(f"wgh{i}", [128, FT * 1024], BF16))
                       for i in range(2)]
                wuh = [E2(nc.sbuf_tensor(f"wuh{i}", [128, FT * 1024], BF16))
                       for i in range(2)]
                wdh = E2(nc.sbuf_tensor("wdh", [128, HK * D], BF16))
                xsb = E2(nc.sbuf_tensor("xsb", [128, FT * TC], BF16))
                actb = E2(nc.sbuf_tensor("actb", [128, HK * TC], BF16))
                gs2 = E2(nc.sbuf_tensor("gs2", [128, 2 * TC], BF16))
                accb = [E2(nc.sbuf_tensor(f"accb{i}", [128, FT * TC], BF16))
                        for i in range(2)]
                pG = [E2(nc.psum_tensor(f"pG{i}", [128, TC], F32)) for i in range(2)]
                pU = [E2(nc.psum_tensor(f"pU{i}", [128, TC], F32)) for i in range(2)]
                pD_ = [E2(nc.psum_tensor(f"pDn{i}", [128, TC], F32)) for i in range(2)]
                pMB = E2(nc.psum_tensor("pMB", [128, TC], F32))

                halves = [(e, hf) for e in range(-1, E0) for hf in range(2)]
                NH = len(halves)
                ST0, LD0 = r8[2].n, r8[1].n
                rW = r8[3]
                W0 = rW.n
                CAST0 = self.CAST0
                wpos = {}
                wc = 0
                for h in range(NH):
                    if h == 0:
                        for j in (0, 1):
                            wc += 2
                            wpos[("gu", j)] = wc
                        wc += 1
                        wpos[("d", 0)] = wc
                    else:
                        if h + 1 < NH:
                            wc += 2
                            wpos[("gu", h + 1)] = wc
                        wc += 1
                        wpos[("d", h)] = wc
                Mb, Ab, Vb = [], [], []
                mm_, aa_, vv_ = smm.n, sac.n, svv.n
                for (e_, hf_) in halves:
                    Mb.append(mm_); Ab.append(aa_); Vb.append(vv_)
                    nxs_ = FT if e_ >= 0 else 0
                    mm_ += ((1 if e_ >= 0 else 0) + 3 * HK) * NCH
                    aa_ += HK * NCH
                    vv_ += (nxs_ + 2 * HK) * NCH
                Mb.append(mm_); Ab.append(aa_); Vb.append(vv_)

                def hparm(h):
                    e_, hf_ = halves[h]
                    nxs_ = FT if e_ >= 0 else 0
                    return e_, hf_, nxs_, (1 if e_ >= 0 else 0), (nxs_ + 2 * HK)

                def pA_silu(h, c, mi):
                    return Ab[h] + HK * c + mi + 1

                def pV_xsb(h, c):
                    _, _, nxs_, _, tvh = hparm(h)
                    return Vb[h] + tvh * c + nxs_

                def pV_act(h, c, mi):
                    _, _, nxs_, _, tvh = hparm(h)
                    return Vb[h] + tvh * c + nxs_ + mi + 1

                def pV_acc(h, c, mi):
                    _, _, nxs_, _, tvh = hparm(h)
                    return Vb[h] + tvh * c + nxs_ + HK + mi + 1

                def pM_pMB(h, c):
                    _, _, _, mbh, _ = hparm(h)
                    return Mb[h] + (mbh + 3 * HK) * c + 1

                def pM_G(h, c, mi):
                    _, _, _, mbh, _ = hparm(h)
                    return Mb[h] + (mbh + 3 * HK) * c + mbh + 2 * mi + 1

                def pM_U(h, c, mi):
                    return pM_G(h, c, mi) + 1

                def pM_D(h, c, mi):
                    _, _, _, mbh, _ = hparm(h)
                    return Mb[h] + (mbh + 3 * HK) * c + mbh + 2 * HK + mi + 1

                def st16(h, c):
                    return ST0 + 16 * (NCH * h + c + 1)

                def ld16(h, c):
                    return LD0 + 16 * ((h - 1) * NCH + c + 1)

                def prev_masked(h):
                    for hp in range(h - 1, -1, -1):
                        if halves[hp][0] >= 0:
                            return hp
                    return None

                def wsrc(h):
                    e_, hf_ = halves[h]
                    q = h // 2
                    sl = slice(1024 * hf_, 1024 * (hf_ + 1))
                    return s_cg[q][:, sl], s_cu[q][:, sl], s_cd[q][sl, :]

                with nc.Block() as b:
                    @b.sync
                    def _(s):
                        for h in range(NH):
                            e_, hf_, nxs_, mbh, tvh = hparm(h)
                            for c in range(NCH):
                                s.wait_ge(svv.h, pV_acc(h, c, HK - 1))
                                self.adma(s, r8[2], accb[c % 2], s_acc, c, FT,
                                          store=True)
                                gidx = h * NCH + c + 2
                                th, tc = gidx // NCH, gidx % NCH
                                if 1 <= th < NH:
                                    s.wait_ge(r8[2].h, st16(h, c))
                                    self.adma(s, r8[1], accb[tc % 2], s_acc, tc, FT)
                        s.wait_ge(r8[2].h, r8[2].n)
                        s.wait_ge(r8[1].h, r8[1].n)

                    @b.tensor
                    def _(t):
                        for h in range(NH):
                            e_, hf_, nxs_, mbh, tvh = hparm(h)
                            buf = h % 2
                            t.wait_ge(rW.h, W0 + 16 * wpos[("gu", h)])
                            for c in range(NCH):
                                def xin(k, c=c, e_=e_):
                                    if e_ >= 0:
                                        return xsb[:, k * TC:(k + 1) * TC]
                                    return XN[:, k * T + c * TC: k * T + c * TC + TC]
                                if e_ >= 0:
                                    if c >= 1:
                                        t.wait_ge(svv.h, pV_xsb(h, c - 1))
                                    else:
                                        pm = prev_masked(h)
                                        if pm is not None:
                                            t.wait_ge(svv.h, pV_xsb(pm, NCH - 1))
                                    smm.inc(t.matmul(
                                        pMB[:],
                                        e8selS[:, e_ * 128:(e_ + 1) * 128],
                                        mrowAll[:, c * TC:(c + 1) * TC],
                                        start=True, stop=True))
                                for mi in range(HK):
                                    sidx = HK * c + mi
                                    if e_ >= 0 and mi == 0:
                                        t.wait_ge(svv.h, pV_xsb(h, c))
                                    if sidx >= 2:
                                        cc, m2 = divmod(sidx - 2, HK)
                                        t.wait_ge(sac.h, pA_silu(h, cc, m2))
                                    elif h >= 1:
                                        t.wait_ge(sac.h,
                                                  pA_silu(h - 1, NCH - 1, mi + HK - 2))
                                    for k in range(FT):
                                        inst = t.matmul(
                                            pG[mi % 2][:],
                                            wgh[buf][:, k * 1024 + mi * 128:
                                                     k * 1024 + (mi + 1) * 128],
                                            xin(k), start=(k == 0), stop=(k == FT - 1))
                                    smm.inc(inst)
                                    if sidx >= 2:
                                        cc, m2 = divmod(sidx - 2, HK)
                                        t.wait_ge(svv.h, pV_act(h, cc, m2))
                                    elif h >= 1:
                                        t.wait_ge(svv.h,
                                                  pV_act(h - 1, NCH - 1, mi + HK - 2))
                                    for k in range(FT):
                                        inst = t.matmul(
                                            pU[mi % 2][:],
                                            wuh[buf][:, k * 1024 + mi * 128:
                                                     k * 1024 + (mi + 1) * 128],
                                            xin(k), start=(k == 0), stop=(k == FT - 1))
                                    smm.inc(inst)
                                t.wait_ge(svv.h, pV_act(h, c, HK - 1))
                                if c == 0:
                                    t.wait_ge(rW.h, W0 + 16 * wpos[("d", h)])
                                for mi in range(HK):
                                    sidx = HK * c + mi
                                    if sidx >= 2:
                                        cc, m2 = divmod(sidx - 2, HK)
                                        t.wait_ge(svv.h, pV_acc(h, cc, m2))
                                    elif h >= 1:
                                        t.wait_ge(svv.h,
                                                  pV_acc(h - 1, NCH - 1, mi + HK - 2))
                                    for k in range(HK):
                                        inst = t.matmul(
                                            pD_[mi % 2][:],
                                            wdh[:, k * D + mi * 128:
                                                k * D + (mi + 1) * 128],
                                            actb[:, k * TC:(k + 1) * TC],
                                            start=(k == 0), stop=(k == HK - 1))
                                    smm.inc(inst)
                            assert smm.n == Mb[h + 1], (h, smm.n, Mb[h + 1])

                    @b.scalar
                    def _(s):
                        for h in range(NH):
                            if h == 0:
                                for j in (0, 1):
                                    s.wait_ge(dG.h,
                                              CAST0 + 48 * (j // 2 + 1))
                                    gsrc, usrc, dsrc = wsrc(j)
                                    self.wload(s, rW, wgh[j % 2], gsrc, FT)
                                    self.wload(s, rW, wuh[j % 2], usrc, FT)
                                s.wait_ge(dG.h, CAST0 + 48)
                                _, _, dsrc = wsrc(0)
                                self.wload(s, rW, wdh, dsrc, HK)
                            else:
                                if h + 1 < NH:
                                    j = h + 1
                                    s.wait_ge(smm.h, Mb[j - 1])
                                    s.wait_ge(dG.h, CAST0 + 48 * (j // 2 + 1))
                                    gsrc, usrc, dsrc = wsrc(j)
                                    self.wload(s, rW, wgh[j % 2], gsrc, FT)
                                    self.wload(s, rW, wuh[j % 2], usrc, FT)
                                s.wait_ge(smm.h, Mb[h])
                                s.wait_ge(dG.h, CAST0 + 48 * (h // 2 + 1))
                                _, _, dsrc = wsrc(h)
                                self.wload(s, rW, wdh, dsrc, HK)
                            for c in range(NCH):
                                for mi in range(HK):
                                    sidx = HK * c + mi
                                    s.wait_ge(smm.h, pM_G(h, c, mi))
                                    if sidx >= 2:
                                        cc, m2 = divmod(sidx - 2, HK)
                                        s.wait_ge(svv.h, pV_act(h, cc, m2))
                                    elif h >= 1:
                                        s.wait_ge(svv.h,
                                                  pV_act(h - 1, NCH - 1, mi + HK - 2))
                                    sac.inc(s.activation(
                                        gs2[:, (mi % 2) * TC:(mi % 2 + 1) * TC],
                                        pG[mi % 2][:], AF.Silu))
                            assert sac.n == Ab[h + 1], (h, sac.n, Ab[h + 1])

                    @b.vector
                    def _(v):
                        for h in range(NH):
                            e_, hf_, nxs_, mbh, tvh = hparm(h)
                            for c in range(NCH):
                                if e_ >= 0:
                                    v.wait_ge(smm.h, pM_pMB(h, c))
                                    if c >= 1:
                                        v.wait_ge(smm.h, pM_U(h, c - 1, HK - 1))
                                    else:
                                        pm = prev_masked(h)
                                        if pm is not None:
                                            v.wait_ge(smm.h,
                                                      pM_U(pm, NCH - 1, HK - 1))
                                    for ft in range(FT):
                                        svv.inc(v.tensor_mul(
                                            xsb[:, ft * TC:(ft + 1) * TC],
                                            XN[:, ft * T + c * TC:
                                               ft * T + (c + 1) * TC],
                                            pMB[:]))
                                for mi in range(HK):
                                    v.wait_ge(smm.h, pM_U(h, c, mi))
                                    v.wait_ge(sac.h, pA_silu(h, c, mi))
                                    if mi == 0:
                                        if c >= 1:
                                            v.wait_ge(smm.h, pM_D(h, c - 1, HK - 1))
                                        elif h >= 1:
                                            v.wait_ge(smm.h,
                                                      pM_D(h - 1, NCH - 1, HK - 1))
                                    svv.inc(v.tensor_mul(
                                        actb[:, mi * TC:(mi + 1) * TC],
                                        gs2[:, (mi % 2) * TC:(mi % 2 + 1) * TC],
                                        pU[mi % 2][:]))
                                for mi in range(HK):
                                    v.wait_ge(smm.h, pM_D(h, c, mi))
                                    if mi == 0:
                                        if h >= 1:
                                            v.wait_ge(r8[1].h, ld16(h, c))
                                        if c >= 2:
                                            v.wait_ge(r8[2].h, st16(h, c - 2))
                                        elif h >= 1:
                                            v.wait_ge(r8[2].h,
                                                      st16(h - 1, c + NCH - 2))
                                    accs = accb[c % 2][:, mi * TC:(mi + 1) * TC]
                                    if h == 0:
                                        svv.inc(v.tensor_copy(accs, pD_[mi % 2][:]))
                                    else:
                                        svv.inc(v.tensor_add(accs, accs,
                                                             pD_[mi % 2][:]))
                            assert svv.n == Vb[h + 1], (h, svv.n, Vb[h + 1])
            stxn.close()
            if self.upto == "A3":
                return nc

            self.emit_lnrouter_phase(d_x, s_acc, attnc, s_xn1, s_x1, rw1, E1,
                                     False, s_mask1, False, None, LN_EPS, "a4")
            if self.upto == "A4":
                return nc

            with ExitStack() as ph:
                Ep = ph.enter_context
                winp = Ep(nc.sbuf_tensor("winp", [128, FT * 2 * DI], BF16))
                wxp = Ep(nc.sbuf_tensor("wxp", [128, 16 * 96], BF16))
                wdt = Ep(nc.sbuf_tensor("wdt", [DTR, DI], BF16))
                xn1c = Ep(nc.sbuf_tensor("xn1c", [128, FT * TC], BF16))
                xr = Ep(nc.sbuf_tensor("xr", [128, FT * TC], BF16))
                self.sq8 = Ep(nc.sbuf_tensor("sq8b", [128, FT * TC], F32))
                self.meanr = Ep(nc.sbuf_tensor("meanrb", [1, TC], F32))
                self.varr = Ep(nc.sbuf_tensor("varrb", [1, TC], F32))
                self.rstdr = Ep(nc.sbuf_tensor("rstdrb", [1, TC], F32))
                xp = Ep(nc.sbuf_tensor("xp", [128, 16 * (TC + 3)], F32))
                szst = Ep(nc.sbuf_tensor("szst", [128, 2 * TC], BF16))
                xincb = Ep(nc.sbuf_tensor("xincb", [128, 16 * TC], BF16))
                cacc = Ep(nc.sbuf_tensor("cacc", [128, 2 * TC], F32))
                dtbf = Ep(nc.sbuf_tensor("dtbf", [DTR, TC], BF16))
                bcsb = Ep(nc.sbuf_tensor("bcsb", [32, TC], F32))
                spa = Ep(nc.sbuf_tensor("spa", [128, 2 * TC], F32))
                spb = Ep(nc.sbuf_tensor("spb", [128, 2 * TC], F32))
                spc = Ep(nc.sbuf_tensor("spc", [128, 2 * TC], F32))
                delst = Ep(nc.sbuf_tensor("delst", [128, 2 * TC], BF16))
                pXZ = [Ep(nc.psum_tensor(f"pXZ{i}", [128, TC], F32)) for i in range(2)]
                self.pQ = Ep(nc.psum_tensor("pQb", [1, TC], F32))
                self.pS = None
                pRS = Ep(nc.psum_tensor("pRSb", [128, TC], F32))
                pXD = Ep(nc.psum_tensor("pXD", [96, TC], F32))
                pDl = [Ep(nc.psum_tensor(f"pDl{i}", [128, TC], F32)) for i in range(2)]

                with nc.Block() as b:
                    @b.gpsimd
                    def _(g):
                        self.wload(g, dG, winp, d_inpt.ap(), FT)
                        self.wload(g, dG, wxp, d_xpt.ap(), 16)
                        self.wload(g, dG, wdt, d_dtt.ap(), 1)
                        g.wait_ge(dG.h, dG.n)

                    @b.vector
                    def _(v):
                        svv.inc(v.memset(xp[:], 0.0))

                for c in range(NCH):
                    A0c, M0c, V0c = sac.n, smm.n, svv.n
                    with nc.Block() as b:
                        @b.sync
                        def _(s):
                            self.adma(s, dS, xn1c, s_xn1, c, FT)
                        dSx = dS.n

                        @b.scalar
                        def _(s):
                            s.wait_ge(dS.h, dSx)
                            for ft in range(FT):
                                sac.inc(s.activation(
                                    self.sq8[:, ft * TC:(ft + 1) * TC],
                                    xn1c[:, ft * TC:(ft + 1) * TC], AF.Square))
                            s.wait_ge(svv.h, V0c + 1)
                            sac.inc(s.activation(self.rstdr[:], self.varr[:],
                                                 AF.Sqrt, bias=RMS_EPS))

                        @b.tensor
                        def _(t):
                            t.wait_ge(sac.h, A0c + 8)
                            for ft in range(FT):
                                inst = t.matmul(self.pQ[:], onesc[:],
                                                self.sq8[:, ft * TC:(ft + 1) * TC],
                                                start=(ft == 0), stop=(ft == FT - 1))
                            smm.inc(inst)
                            t.wait_ge(svv.h, V0c + 2)
                            smm.inc(t.matmul(pRS[:], onesr[:], self.rstdr[:],
                                             start=True, stop=True))

                        @b.vector
                        def _(v):
                            v.wait_ge(smm.h, M0c + 1)
                            svv.inc(v.tensor_scalar_mul(self.varr[:], self.pQ[:],
                                                        1.0 / D))
                            v.wait_ge(sac.h, A0c + 9)
                            svv.inc(v.reciprocal(self.rstdr[:], self.rstdr[:]))
                            v.wait_ge(smm.h, M0c + 2)
                            for ft in range(FT):
                                svv.inc(v.scalar_tensor_tensor(
                                    xr[:, ft * TC:(ft + 1) * TC],
                                    xn1c[:, ft * TC:(ft + 1) * TC],
                                    n1w[:, ft:ft + 1], pRS[:],
                                    ALU.mult, ALU.mult))
                    mmI0, acI0 = smm.n, sac.n
                    zb = [r8[0].n, r8[1].n]
                    with nc.Block() as b:
                        @b.tensor
                        def _(t):
                            for mi in range(32):
                                if mi >= 2:
                                    t.wait_ge(sac.h, acI0 + (mi - 1))
                                for k in range(FT):
                                    inst = t.matmul(
                                        pXZ[mi % 2][:],
                                        winp[:, k * 2 * DI + mi * 128: k * 2 * DI + (mi + 1) * 128],
                                        xr[:, k * TC:(k + 1) * TC],
                                        start=(k == 0), stop=(k == FT - 1))
                                smm.inc(inst)

                        @b.scalar
                        def _(s):
                            for mi in range(32):
                                s.wait_ge(smm.h, mmI0 + mi + 1)
                                if mi < 16:
                                    dt = mi
                                    sac.inc(s.activation(
                                        xp[:, dt * (TC + 3) + 3:(dt + 1) * (TC + 3)],
                                        pXZ[mi % 2][:], AF.Copy))
                                else:
                                    sl = mi % 2
                                    s.wait_ge(r8[sl].h,
                                              zb[sl] + 16 * max(0, (mi - 18) // 2 + 1))
                                    sac.inc(s.activation(
                                        szst[:, (mi % 2) * TC:(mi % 2 + 1) * TC],
                                        pXZ[mi % 2][:], AF.Silu))

                        @b.sync
                        def _(s):
                            for mi in range(16, 32):
                                s.wait_ge(sac.h, acI0 + mi + 1)
                                r8[mi % 2].inc(s.dma_start(
                                    out=s_sz[(mi - 16) * 128:(mi - 15) * 128,
                                             c * TC:(c + 1) * TC],
                                    in_=szst[:, (mi % 2) * TC:(mi % 2 + 1) * TC]), 16)
                    vv0, ac0 = svv.n, sac.n
                    with nc.Block() as b:
                        @b.vector
                        def _(v):
                            for dt in range(16):
                                x0_ = dt * (TC + 3)
                                ca = cacc[:, (dt % 2) * TC:(dt % 2 + 1) * TC]
                                if dt >= 2:
                                    v.wait_ge(sac.h, ac0 + (dt - 1))
                                svv.inc(v.tensor_single_scalar(
                                    ca, xp[:, x0_:x0_ + TC],
                                    convw[:, dt * 4:dt * 4 + 1], ALU.mult))
                                for j in range(1, 4):
                                    v.wait_ge(svv.h, svv.n)
                                    svv.inc(v.scalar_tensor_tensor(
                                        ca, xp[:, x0_ + j:x0_ + j + TC],
                                        convw[:, dt * 4 + j:dt * 4 + j + 1],
                                        ca, ALU.mult, ALU.add))
                                v.wait_ge(svv.h, svv.n)
                                svv.inc(v.tensor_copy(xp[:, x0_:x0_ + 3],
                                                      xp[:, x0_ + TC:x0_ + TC + 3]))

                        @b.scalar
                        def _(s):
                            for dt in range(16):
                                s.wait_ge(svv.h, vv0 + dt * 5 + 4)
                                sac.inc(s.activation(
                                    xincb[:, dt * TC:(dt + 1) * TC],
                                    cacc[:, (dt % 2) * TC:(dt % 2 + 1) * TC],
                                    AF.Silu, bias=convb[:, dt:dt + 1]))
                    with nc.Block() as b:
                        @b.sync
                        def _(s):
                            self.adma(s, dS, xincb, s_xinc, c, 16, store=True)
                            s.wait_ge(dS.h, dS.n)
                    with nc.Block() as b:
                        @b.tensor
                        def _(t):
                            for k in range(16):
                                inst = t.matmul(pXD[:], wxp[:, k * 96:(k + 1) * 96],
                                                xincb[:, k * TC:(k + 1) * TC],
                                                start=(k == 0), stop=(k == 15))
                            smm.inc(inst)
                    with nc.Block() as b:
                        @b.scalar
                        def _(s):
                            sac.inc(s.activation(dtbf[:], pXD[0:DTR, :], AF.Copy))

                        @b.vector
                        def _(v):
                            svv.inc(v.tensor_copy(bcsb[:], pXD[DTR:96, :]))

                        @b.sync
                        def _(s):
                            s.wait_ge(svv.h, svv.n)
                            dS.inc(s.dma_start(out=s_bc[:, c * TC:(c + 1) * TC],
                                               in_=bcsb[:]), 16)
                            s.wait_ge(dS.h, dS.n)
                    mmT0, acT0, vvT0 = smm.n, sac.n, svv.n
                    db2 = [r8[2].n, r8[3].n]
                    with nc.Block() as b:
                        @b.tensor
                        def _(t):
                            for mi in range(16):
                                if mi >= 2:
                                    t.wait_ge(svv.h, vvT0 + 2 * (mi - 1))
                                    t.wait_ge(sac.h, acT0 + 3 * (mi - 1))
                                inst = t.matmul(pDl[mi % 2][:],
                                                wdt[:, mi * 128:(mi + 1) * 128], dtbf[:],
                                                start=True, stop=True)
                                smm.inc(inst)

                        @b.scalar
                        def _(s):
                            for mi in range(16):
                                sl = mi % 2
                                s.wait_ge(smm.h, mmT0 + mi + 1)
                                sac.inc(s.activation(spa[:, sl * TC:(sl + 1) * TC],
                                                     pDl[sl][:], AF.Abs,
                                                     bias=dtb[:, mi:mi + 1]))
                                s.wait_ge(sac.h, sac.n)
                                sac.inc(s.activation(spb[:, sl * TC:(sl + 1) * TC],
                                                     spa[:, sl * TC:(sl + 1) * TC],
                                                     AF.Exp, scale=-1.0))
                                s.wait_ge(sac.h, sac.n)
                                sac.inc(s.activation(spa[:, sl * TC:(sl + 1) * TC],
                                                     spb[:, sl * TC:(sl + 1) * TC],
                                                     AF.Ln, bias=1.0))

                        @b.vector
                        def _(v):
                            for mi in range(16):
                                sl = mi % 2
                                v.wait_ge(smm.h, mmT0 + mi + 1)
                                sl2 = mi % 2
                                v.wait_ge(r8[2 + sl2].h,
                                          db2[sl2] + 16 * max(0, (mi - 2) // 2 + 1))
                                svv.inc(v.tensor_scalar(spc[:, sl * TC:(sl + 1) * TC],
                                                        pDl[sl][:], dtb[:, mi:mi + 1],
                                                        0.0, ALU.add, ALU.max))
                                v.wait_ge(sac.h, acT0 + 3 * mi + 3)
                                v.wait_ge(svv.h, svv.n)
                                svv.inc(v.tensor_add(delst[:, sl * TC:(sl + 1) * TC],
                                                     spc[:, sl * TC:(sl + 1) * TC],
                                                     spa[:, sl * TC:(sl + 1) * TC]))

                        @b.sync
                        def _(s):
                            for mi in range(16):
                                s.wait_ge(svv.h, vvT0 + 2 * mi + 2)
                                r8[mi % 2 + 2].inc(s.dma_start(
                                    out=s_delta[mi * 128:(mi + 1) * 128,
                                                c * TC:(c + 1) * TC],
                                    in_=delst[:, (mi % 2) * TC:(mi % 2 + 1) * TC]), 16)
                with nc.Block() as b:
                    @b.sync
                    def _(s):
                        s.wait_ge(dS.h, dS.n)
                        for rr in r8:
                            s.wait_ge(rr.h, rr.n)
            if self.upto == "B1":
                return nc

            with ExitStack() as ph:
                Ep = ph.enter_context
                delc = Ep(nc.sbuf_tensor("delc", [128, 16 * TC], BF16))
                xincc = Ep(nc.sbuf_tensor("xincc", [128, 16 * TC], BF16))
                bcb = Ep(nc.sbuf_tensor("bcb", [16, TC], F32))
                bcc = Ep(nc.sbuf_tensor("bcc", [16, TC], F32))
                brep = Ep(nc.sbuf_tensor("brep", [128, TC], BF16))
                crep = Ep(nc.sbuf_tensor("crep", [128, TC], BF16))
                carry = Ep(nc.sbuf_tensor("carry", [128, NTILE], BF16))
                SS = 4
                dA2 = Ep(nc.sbuf_tensor("dA2", [128, SS * TC], BF16))
                dBx2 = Ep(nc.sbuf_tensor("dBx2", [128, SS * TC], BF16))
                hb2 = Ep(nc.sbuf_tensor("hb2", [128, SS * TC], BF16))
                hc2 = Ep(nc.sbuf_tensor("hc2", [128, SS * TC], BF16))
                yst = Ep(nc.sbuf_tensor("yst", [8, 8 * TC], BF16))
                fw = Ep(nc.sbuf_tensor("fw", [128, FT * DFF], BF16))
                xnC = [Ep(nc.sbuf_tensor(f"xnC{i}", [128, FT * TC], BF16)) for i in range(2)]
                hcC = Ep(nc.sbuf_tensor("hcC", [128, 32 * TC], BF16))
                hstC = Ep(nc.sbuf_tensor("hstC", [128, 4 * TC], BF16))
                acb1 = Ep(nc.sbuf_tensor("acb1", [128, FT * TC], BF16))
                mrowA = Ep(nc.sbuf_tensor("mrowA", [1, T], BF16))
                mbsb = Ep(nc.sbuf_tensor("mbsb", [128, TC], F32))
                tscr = Ep(nc.sbuf_tensor("tscr", [128, 2 * TC], BF16))
                p16 = Ep(nc.sbuf_tensor("p16S", [16, 128], F32))
                selm = Ep(nc.sbuf_tensor("selmS", [128, 16 * 128], BF16))
                gmat = Ep(nc.sbuf_tensor("gmatS", [128, 8], BF16))
                a_s = Ep(nc.sbuf_tensor("a_sS", [128, NTILE], F32))
                pA_ = [Ep(nc.psum_tensor(f"pA_{i}", [128, TC], F32)) for i in range(2)]
                pB_ = [Ep(nc.psum_tensor(f"pB_{i}", [128, TC], F32)) for i in range(2)]
                pY = Ep(nc.psum_tensor("pY", [72, TC], F32))
                pCH = [Ep(nc.psum_tensor(f"pCH{i}", [128, TC], F32)) for i in range(2)]
                pMB1 = Ep(nc.psum_tensor("pMB1", [128, TC], F32))

                def b2_loads(s, c):
                    self.adma(s, rB, delc, s_delta, c, 16)
                    self.adma(s, rB, xincc, s_xinc, c, 16)
                    rB.inc(s.dma_start(out=bcb[:],
                                       in_=s_bc[0:16, c * TC:(c + 1) * TC]), 16)
                    rB.inc(s.dma_start(out=bcc[:],
                                       in_=s_bc[16:32, c * TC:(c + 1) * TC]), 16)

                with nc.Block() as b:
                    @b.sync
                    def _(s):
                        b2_loads(s, 0)
                        dS.inc(s.dma_start(out=p16[:], in_=d_p16.ap()), 16)
                        dS.inc(s.dma_start(out=a_s[:], in_=d_alog.ap()), 16)
                        s.wait_ge(dS.h, dS.n)

                    @b.gpsimd
                    def _(g):
                        dG.inc(g.dma_start(out=selm[:], in_=d_sel.ap()), 16)
                        dG.inc(g.dma_start(out=gmat[:], in_=d_gmat.ap()), 16)
                        g.wait_ge(dG.h, dG.n)

                    @b.vector
                    def _(v):
                        svv.inc(v.memset(carry[:], 0.0))
                with nc.Block() as b:
                    @b.scalar
                    def _(s):
                        sac.inc(s.activation(a_s[:], a_s[:], AF.Exp))
                with nc.Block() as b:
                    @b.vector
                    def _(v):
                        svv.inc(v.tensor_scalar_mul(a_s[:], a_s[:], -1.0))

                LAG = 3

                def emit_p1(e):
                    G16 = dG.n
                    XNB = [r8[6].n, r8[7].n]
                    HSB = [r8[i].n for i in range(4)]
                    M0, A0 = smm.n, sac.n
                    with nc.Block() as b:
                        @b.gpsimd
                        def _(g):
                            self.wload(g, dG, fw, d_fw1[e], FT)

                        @b.tensor
                        def _(t):
                            for u in range(256):
                                ci, mi = u // 32, u % 32
                                t.wait_ge(dG.h, G16 + 16)
                                t.wait_ge(r8[6 + ci % 2].h,
                                          XNB[ci % 2] + 16 * (ci // 2 + 1))
                                if u >= 2:
                                    t.wait_ge(sac.h, A0 + u - 1)
                                for k in range(FT):
                                    inst = t.matmul(
                                        pCH[u % 2][:],
                                        fw[:, k * DFF + mi * 128: k * DFF + (mi + 1) * 128],
                                        xnC[ci % 2][:, k * TC:(k + 1) * TC],
                                        start=(k == 0), stop=(k == FT - 1))
                                smm.inc(inst)

                        @b.scalar
                        def _(s):
                            for u in range(256):
                                s.wait_ge(smm.h, M0 + u + 1)
                                s.wait_ge(r8[u % 4].h,
                                          HSB[u % 4] + 16 * max(0, (u - 4) // 4 + 1))
                                sac.inc(s.activation(
                                    hstC[:, (u % 4) * TC:(u % 4 + 1) * TC],
                                    pCH[u % 2][:], AF.Gelu,
                                    bias=fb1[:, e * 32 + (u % 32):e * 32 + (u % 32) + 1]))

                        @b.sync
                        def _(s):
                            self.adma(s, r8[6], xnC[0], s_xn1, 0, FT)
                            self.adma(s, r8[7], xnC[1], s_xn1, 1, FT)
                            for u in range(256):
                                ci = u // 32
                                s.wait_ge(sac.h, A0 + u + 1)
                                r8[u % 4].inc(s.dma_start(
                                    out=s_h1[(u % 32) * 128:(u % 32 + 1) * 128,
                                             ci * TC:(ci + 1) * TC],
                                    in_=hstC[:, (u % 4) * TC:(u % 4 + 1) * TC]), 16)
                                if u % 32 == 31 and ci + 2 < NCH:
                                    s.wait_ge(smm.h, M0 + ci * 32 + 32)
                                    self.adma(s, r8[6 + (ci + 2) % 2], xnC[(ci + 2) % 2],
                                              s_xn1, ci + 2, FT)

                def emit_jb(jb):
                    c = jb
                    e = jb // 2
                    cbase = 4 * (jb % 2)
                    PM, PA, PV, PG = {}, {}, {}, {}
                    n = [0]

                    def put(dic, tag, k=1):
                        n[0] += k
                        dic[tag] = n[0]
                    n[0] = 0
                    put(PM, "repB"); put(PM, "repC")
                    for i in range(NTILE):
                        if i >= LAG:
                            put(PM, ("y", i - LAG))
                        put(PM, ("a", i)); put(PM, ("b", i))
                        if i % 64 == 0:
                            put(PM, ("pmb", i // 64))
                        if i % 8 == 7:
                            put(PM, ("cm", i // 8))
                    for i in range(NTILE - LAG, NTILE):
                        put(PM, ("y", i))
                    n[0] = 0
                    for i in range(NTILE):
                        put(PA, ("dA", i))
                        if i >= LAG:
                            put(PA, ("yev", i - LAG))
                    for i in range(NTILE - LAG, NTILE):
                        put(PA, ("yev", i))
                    n[0] = 0
                    put(PV, "repBe"); put(PV, "repCe")
                    for i in range(NTILE):
                        if i % 64 == 16:
                            put(PV, ("mbe", i // 64))
                        put(PV, ("dBx", i)); put(PV, ("scan", i))
                        if i >= 16 and (i - 16) % 8 == 0:
                            put(PV, ("ev", (i - 16) // 8), 1 if e == 0 else 2)
                    for u in range(30, 32):
                        put(PV, ("ev", u), 1 if e == 0 else 2)
                    n[0] = 0
                    for dt in range(16):
                        put(PG, ("dx", dt))
                    for i in range(NTILE):
                        put(PG, ("hc", i))
                        if i % 4 == 3:
                            put(PG, ("cy", i))

                    M0, A0, V0, G0 = smm.n, sac.n, svv.n, sgp.n
                    G16 = dG.n
                    RB0 = rB.n
                    RH0, RM0, RA0 = rH.n, rM.n, rA.n
                    HSB = [r8[i].n for i in range(4)]
                    YS0 = [r8[4].n, r8[5].n]
                    with nc.Block() as b:
                        @b.gpsimd
                        def _(g):
                            if jb % 2 == 0:
                                self.wload(g, dG, fw, d_fw2[e], 32)
                            g.wait_ge(rB.h, RB0)
                            for dt in range(16):
                                sgp.inc(g.tensor_mul(xincc[:, dt * TC:(dt + 1) * TC],
                                                     delc[:, dt * TC:(dt + 1) * TC],
                                                     xincc[:, dt * TC:(dt + 1) * TC]))
                            for i in range(NTILE):
                                ss4 = i % 4
                                g.wait_ge(svv.h, V0 + PV[("scan", i)])
                                if i >= 4:
                                    g.wait_ge(smm.h, M0 + PM[("y", i - 4)])
                                sgp.inc(g.tensor_mul(hc2[:, ss4 * TC:(ss4 + 1) * TC],
                                                     hb2[:, ss4 * TC:(ss4 + 1) * TC],
                                                     crep[:]))
                                if i % 4 == 3:
                                    sgp.inc(g.tensor_copy(
                                        carry[:, i - 3:i + 1],
                                        hb2[:].rearrange(
                                            "p (s t) -> p s t", s=4)[:, :, TC - 1]))

                        @b.tensor
                        def _(t):
                            t.wait_ge(rB.h, RB0)
                            smm.inc(t.matmul(pA_[0][:], p16[:], bcb[:],
                                             start=True, stop=True))
                            smm.inc(t.matmul(pB_[0][:], p16[:], bcc[:],
                                             start=True, stop=True))

                            def mmY(bp):
                                t.wait_ge(sgp.h, G0 + PG[("hc", bp)])
                                if bp >= 3:
                                    t.wait_ge(sac.h, A0 + PA[("yev", bp - 3)])
                                yo = 32 * (bp % 3)
                                smm.inc(t.matmul(
                                    pY[yo:yo + 8, :], gmat[:],
                                    hc2[:, (bp % 4) * TC:(bp % 4 + 1) * TC],
                                    start=True, stop=True))

                            for i in range(NTILE):
                                if i >= LAG:
                                    mmY(i - LAG)
                                sp = i % 2
                                j = i % 16
                                dt = i // 16
                                if i < 2:
                                    t.wait_ge(svv.h, V0 + PV["repBe"])
                                if i >= 2:
                                    t.wait_ge(sac.h, A0 + PA[("dA", i - 2)])
                                smm.inc(t.matmul(pA_[sp][:],
                                                 selm[:, j * 128:(j + 1) * 128],
                                                 delc[:, dt * TC:(dt + 1) * TC],
                                                 start=True, stop=True))
                                if i < 2:
                                    t.wait_ge(svv.h, V0 + PV["repCe"])
                                if i >= 2:
                                    t.wait_ge(svv.h, V0 + PV[("dBx", i - 2)])
                                t.wait_ge(sgp.h, G0 + PG[("dx", dt)])
                                smm.inc(t.matmul(pB_[sp][:],
                                                 selm[:, j * 128:(j + 1) * 128],
                                                 xincc[:, dt * TC:(dt + 1) * TC],
                                                 start=True, stop=True))
                                if i % 64 == 0:
                                    lci = i // 64
                                    t.wait_ge(rM.h, RM0 + 16)
                                    if lci >= 1:
                                        t.wait_ge(svv.h, V0 + PV[("mbe", lci - 1)])
                                    smm.inc(t.matmul(
                                        pMB1[:], onesrb[:],
                                        mrowA[:, (cbase + lci) * TC:(cbase + lci + 1) * TC],
                                        start=True, stop=True))
                                u = i // 8
                                jj = i % 8
                                lci, mi = u // 8, u % 8
                                if jj == 0:
                                    if jb % 2 == 0:
                                        t.wait_ge(dG.h, G16 + 16)
                                    t.wait_ge(rH.h, RH0 + 16 * (lci + 1))
                                    if u >= 2:
                                        t.wait_ge(svv.h, V0 + PV[("ev", u - 2)])
                                for k in range(4 * jj, 4 * jj + 4):
                                    inst = t.matmul(
                                        pCH[u % 2][:],
                                        fw[:, k * D + mi * 128: k * D + (mi + 1) * 128],
                                        hcC[:, k * TC:(k + 1) * TC],
                                        start=(k == 0), stop=(k == 31))
                                if jj == 7:
                                    smm.inc(inst)
                            for i in range(NTILE - LAG, NTILE):
                                mmY(i)

                        @b.scalar
                        def _(s):
                            def yev(bp):
                                s.wait_ge(smm.h, M0 + PM[("y", bp)])
                                if bp >= 8:
                                    gprev = bp // 8 - 1
                                    s.wait_ge(r8[4 + gprev % 2].h,
                                              YS0[gprev % 2] + 16 * (gprev // 2 + 1))
                                else:
                                    s.wait_ge(r8[4].h, YS0[0])
                                    s.wait_ge(r8[5].h, YS0[1])
                                yo = 32 * (bp % 3)
                                sac.inc(s.activation(
                                    yst[:, (bp % 8) * TC:(bp % 8 + 1) * TC],
                                    pY[yo:yo + 8, :], AF.Copy))
                            for i in range(NTILE):
                                sp, ss4 = i % 2, i % 4
                                s.wait_ge(smm.h, M0 + PM[("a", i)])
                                if i >= 4:
                                    s.wait_ge(svv.h, V0 + PV[("scan", i - 4)])
                                sac.inc(s.activation(dA2[:, ss4 * TC:(ss4 + 1) * TC],
                                                     pA_[sp][:], AF.Exp,
                                                     scale=a_s[:, i:i + 1]))
                                if i >= LAG:
                                    yev(i - LAG)
                            for i in range(NTILE - LAG, NTILE):
                                yev(i)

                        @b.vector
                        def _(v):
                            v.wait_ge(smm.h, M0 + PM["repB"])
                            svv.inc(v.tensor_copy(brep[:], pA_[0][:]))
                            v.wait_ge(smm.h, M0 + PM["repC"])
                            svv.inc(v.tensor_copy(crep[:], pB_[0][:]))
                            for i in range(NTILE):
                                sp, ss4 = i % 2, i % 4
                                if i % 64 == 16:
                                    lci = i // 64
                                    v.wait_ge(smm.h, M0 + PM[("pmb", lci)])
                                    svv.inc(v.tensor_copy(mbsb[:], pMB1[:]))
                                v.wait_ge(smm.h, M0 + PM[("b", i)])
                                svv.inc(v.tensor_mul(dBx2[:, ss4 * TC:(ss4 + 1) * TC],
                                                     pB_[sp][:], brep[:]))
                                v.wait_ge(sac.h, A0 + PA[("dA", i)])
                                v.wait_ge(svv.h, svv.n)
                                if i >= 4:
                                    v.wait_ge(sgp.h,
                                              G0 + PG[("cy", i - 4 - (i % 4) + 3)])
                                svv.inc(v.tensor_tensor_scan(
                                    hb2[:, ss4 * TC:(ss4 + 1) * TC],
                                    dA2[:, ss4 * TC:(ss4 + 1) * TC],
                                    dBx2[:, ss4 * TC:(ss4 + 1) * TC],
                                    carry[:, i:i + 1], ALU.mult, ALU.add))
                                if i >= 16 and (i - 16) % 8 == 0:
                                    u = (i - 16) // 8
                                    self._bc_ev(v, u, e, pCH, mbsb, tscr, acb1,
                                                fb2, M0, PM, V0, PV, RA0, rA,
                                                smm, svv, dS, sac)
                            for u in range(30, 32):
                                self._bc_ev(v, u, e, pCH, mbsb, tscr, acb1,
                                            fb2, M0, PM, V0, PV, RA0, rA,
                                            smm, svv, dS, sac)

                        @b.sync
                        def _(s):
                            rM.inc(s.dma_start(out=mrowA[:],
                                               in_=s_mask1[e:e + 1, :]), 16)
                            for _ri in range(4):
                                s.wait_ge(r8[_ri].h, HSB[_ri])
                            self.adma(s, rH, hcC, s_h1, cbase, 32)
                            if e > 0:
                                s.wait_ge(rA.h, RA0)
                                self.adma(s, rA, acb1, s_acc1, cbase, FT)
                            for i in range(NTILE):
                                if i % 64 == 57:
                                    lci = i // 64
                                    if lci + 1 < 4:
                                        s.wait_ge(smm.h, M0 + PM[("cm", 8 * lci + 7)])
                                        self.adma(s, rH, hcC, s_h1,
                                                  cbase + lci + 1, 32)
                                if i % 64 == 7 and i >= 64:
                                    lci = i // 64 - 1
                                    s.wait_ge(svv.h, V0 + PV[("ev", 8 * lci + 7)] + 2)
                                    self.adma(s, rA, acb1, s_acc1, cbase + lci, FT,
                                              store=True)
                                    if e > 0:
                                        s.wait_ge(rA.h, RA0 + 16 * (2 * lci + 2))
                                        self.adma(s, rA, acb1, s_acc1,
                                                  cbase + lci + 1, FT)
                                if i % 8 == 7:
                                    g8 = i // 8
                                    s.wait_ge(sac.h, A0 + PA[("yev", 8 * g8 + 7)])
                                    r8[4 + g8 % 2].inc(s.dma_start(
                                        out=s_y[64 * g8:64 * g8 + 64,
                                                c * TC:(c + 1) * TC].rearrange(
                                            "(j r) t -> r j t", j=8, r=8),
                                        in_=yst[:].rearrange("p (j t) -> p j t", j=8)), 16)
                            s.wait_ge(svv.h, svv.n)
                            self.adma(s, rA, acb1, s_acc1, cbase + 3, FT, store=True)
                            if c < NCH - 1:
                                s.wait_ge(smm.h, M0 + PM[("b", NTILE - 1)])
                                b2_loads(s, c + 1)

                for e in range(E1):
                    emit_p1(e)
                    emit_jb(2 * e)
                    emit_jb(2 * e + 1)
                with nc.Block() as b:
                    @b.sync
                    def _(s):
                        s.wait_ge(dS.h, dS.n)
                        for rr in r8 + [rB, rH, rM, rA]:
                            s.wait_ge(rr.h, rr.n)

            if self.upto in ("B2", "C"):
                return nc

            with ExitStack() as ph:
                Ep = ph.enter_context
                wout = Ep(nc.sbuf_tensor("wout", [128, 16 * D], BF16))
                yc = [Ep(nc.sbuf_tensor(f"yc{i}", [128, 16 * TC], BF16))
                      for i in range(2)]
                xincc = [Ep(nc.sbuf_tensor(f"xic2{i}", [128, 16 * TC], BF16))
                         for i in range(2)]
                szc = [Ep(nc.sbuf_tensor(f"szc{i}", [128, 16 * TC], BF16))
                       for i in range(2)]
                xn1c2 = [Ep(nc.sbuf_tensor(f"xn1c2{i}", [128, FT * TC], BF16))
                         for i in range(2)]
                yf = Ep(nc.sbuf_tensor("yf", [128, 16 * TC], BF16))
                ysc2 = Ep(nc.sbuf_tensor("ysc2", [128, 2 * TC], F32))
                x1ast = [Ep(nc.sbuf_tensor(f"x1ast{i}", [128, FT * TC], BF16))
                         for i in range(2)]
                pO = [Ep(nc.psum_tensor(f"pO{i}", [128, TC], F32)) for i in range(2)]
                rY, rS = r8[0], r8[1]
                Y0, S0 = rY.n, rS.n
                M0, V0 = smm.n, svv.n
                VS2, MS2 = 40, 8
                with nc.Block() as b:
                    @b.gpsimd
                    def _(g):
                        self.wload(g, dG, wout, d_outt.ap(), 16)

                    @b.sync
                    def _(s):
                        for c in range(NCH):
                            if c >= 2:
                                s.wait_ge(svv.h, V0 + VS2 * (c - 1))
                            self.adma(s, rY, yc[c % 2], s_y, c, 16)
                            self.adma(s, rY, xincc[c % 2], s_xinc, c, 16)
                            self.adma(s, rY, szc[c % 2], s_sz, c, 16)
                            self.adma(s, rY, xn1c2[c % 2], s_xn1, c, FT)
                            if c >= 1:
                                s.wait_ge(svv.h, V0 + VS2 * c)
                                self.adma(s, rS, x1ast[(c - 1) % 2], s_x1a, c - 1,
                                          FT, store=True)
                        s.wait_ge(svv.h, V0 + VS2 * NCH)
                        self.adma(s, rS, x1ast[(NCH - 1) % 2], s_x1a, NCH - 1,
                                  FT, store=True)
                        s.wait_ge(rS.h, rS.n)
                        s.wait_ge(rY.h, rY.n)

                    @b.tensor
                    def _(t):
                        t.wait_ge(dG.h, dG.n)
                        for c in range(NCH):
                            t.wait_ge(svv.h, V0 + VS2 * c + 32)
                            for mi in range(FT):
                                if mi >= 2:
                                    t.wait_ge(svv.h, V0 + VS2 * c + 32 + mi - 1)
                                elif c >= 1:
                                    t.wait_ge(svv.h, V0 + VS2 * (c - 1) + 32 + mi + 7)
                                for k in range(16):
                                    inst = t.matmul(
                                        pO[mi % 2][:],
                                        wout[:, k * D + mi * 128: k * D + (mi + 1) * 128],
                                        yf[:, k * TC:(k + 1) * TC],
                                        start=(k == 0), stop=(k == 15))
                                smm.inc(inst)
                        assert smm.n == M0 + MS2 * NCH

                    @b.vector
                    def _(v):
                        for c in range(NCH):
                            v.wait_ge(rY.h, Y0 + 64 * (c + 1))
                            if c >= 1:
                                v.wait_ge(smm.h, M0 + MS2 * c)
                            for dt in range(16):
                                sl = dt % 2
                                if dt >= 2:
                                    v.wait_ge(svv.h, V0 + VS2 * c + 2 * (dt - 2) + 2)
                                svv.inc(v.scalar_tensor_tensor(
                                    ysc2[:, sl * TC:(sl + 1) * TC],
                                    xincc[c % 2][:, dt * TC:(dt + 1) * TC],
                                    dcol[:, dt:dt + 1],
                                    yc[c % 2][:, dt * TC:(dt + 1) * TC],
                                    ALU.mult, ALU.add))
                                v.wait_ge(svv.h, svv.n)
                                svv.inc(v.tensor_mul(yf[:, dt * TC:(dt + 1) * TC],
                                                     ysc2[:, sl * TC:(sl + 1) * TC],
                                                     szc[c % 2][:, dt * TC:(dt + 1) * TC]))
                            for mi in range(FT):
                                v.wait_ge(smm.h, M0 + MS2 * c + mi + 1)
                                if mi == 0 and c >= 2:
                                    v.wait_ge(rS.h, S0 + 16 * (c - 1))
                                svv.inc(v.tensor_add(
                                    x1ast[c % 2][:, mi * TC:(mi + 1) * TC],
                                    pO[mi % 2][:],
                                    xn1c2[c % 2][:, mi * TC:(mi + 1) * TC]))
                        assert svv.n == V0 + VS2 * NCH
            if self.upto == "B2b":
                return nc

            with ExitStack() as ph:
                Ep = ph.enter_context
                w1s = Ep(nc.sbuf_tensor("w1s", [128, FT * DFF], BF16))
                w3s = Ep(nc.sbuf_tensor("w3s", [128, FT * DFF], BF16))
                x1ac = [Ep(nc.sbuf_tensor(f"x1ac{i}", [128, FT * TC], BF16))
                        for i in range(2)]
                h2 = Ep(nc.sbuf_tensor("h2", [128, FT * TC], BF16))
                sq8c = Ep(nc.sbuf_tensor("sq8c", [128, FT * TC], F32))
                varrc = Ep(nc.sbuf_tensor("varrc", [1, TC], F32))
                rstdc = Ep(nc.sbuf_tensor("rstdc", [1, TC], F32))
                gs2c = Ep(nc.sbuf_tensor("gs2c", [128, 2 * TC], BF16))
                abst = Ep(nc.sbuf_tensor("abst", [128, 4 * TC], BF16))
                pQc = Ep(nc.psum_tensor("pQc", [1, TC], F32))
                pRSc = Ep(nc.psum_tensor("pRSc", [128, TC], F32))
                pA1 = [Ep(nc.psum_tensor(f"pA1{i}", [128, TC], F32)) for i in range(2)]
                pB1 = [Ep(nc.psum_tensor(f"pB1{i}", [128, TC], F32)) for i in range(2)]
                rX, rS = r8[0], r8[1]
                X0, S0 = rX.n, rS.n
                M0, A0, V0 = smm.n, sac.n, svv.n
                MS3, AS3, VS3 = 66, 41, 42
                inv_d = 1.0 / D
                with nc.Block() as b:
                    @b.gpsimd
                    def _(g):
                        self.wload(g, dG, w1s, d_w1t.ap(), FT)
                        self.wload(g, dG, w3s, d_w3t.ap(), FT)

                    @b.sync
                    def _(s):
                        for c in range(NCH):
                            if c >= 2:
                                s.wait_ge(sac.h, A0 + AS3 * (c - 2) + 8)
                                s.wait_ge(svv.h, V0 + VS3 * (c - 2) + 10)
                            self.adma(s, rX, x1ac[c % 2], s_x1a, c, FT)
                            if c >= 1:
                                for mi in range(32):
                                    s.wait_ge(svv.h, V0 + VS3 * (c - 1) + 11 + mi)
                                    rS.inc(s.dma_start(
                                        out=s_ab[mi * 128:(mi + 1) * 128,
                                                 (c - 1) * TC:c * TC],
                                        in_=abst[:, (mi % 4) * TC:(mi % 4 + 1) * TC]), 16)
                        cp = NCH - 1
                        for mi in range(32):
                            s.wait_ge(svv.h, V0 + VS3 * cp + 11 + mi)
                            rS.inc(s.dma_start(
                                out=s_ab[mi * 128:(mi + 1) * 128, cp * TC:(cp + 1) * TC],
                                in_=abst[:, (mi % 4) * TC:(mi % 4 + 1) * TC]), 16)
                        s.wait_ge(rS.h, rS.n)
                        s.wait_ge(rX.h, rX.n)

                    @b.tensor
                    def _(t):
                        t.wait_ge(dG.h, dG.n)
                        for c in range(NCH):
                            t.wait_ge(sac.h, A0 + AS3 * c + 8)
                            if c >= 1:
                                t.wait_ge(svv.h, V0 + VS3 * (c - 1) + 1)
                            for ft in range(FT):
                                inst = t.matmul(pQc[:], onesc[:],
                                                sq8c[:, ft * TC:(ft + 1) * TC],
                                                start=(ft == 0), stop=(ft == FT - 1))
                            smm.inc(inst)
                            t.wait_ge(svv.h, V0 + VS3 * c + 2)
                            if c >= 1:
                                t.wait_ge(svv.h, V0 + VS3 * (c - 1) + 10)
                            smm.inc(t.matmul(pRSc[:], onesr[:], rstdc[:],
                                             start=True, stop=True))
                            for mi in range(32):
                                if mi == 0:
                                    t.wait_ge(svv.h, V0 + VS3 * c + 10)
                                if mi >= 2:
                                    t.wait_ge(sac.h, A0 + AS3 * c + 8 + mi)
                                elif c >= 1:
                                    t.wait_ge(sac.h, A0 + AS3 * (c - 1) + 40 + mi)
                                for k in range(FT):
                                    inst = t.matmul(
                                        pA1[mi % 2][:],
                                        w1s[:, k * DFF + mi * 128: k * DFF + (mi + 1) * 128],
                                        h2[:, k * TC:(k + 1) * TC],
                                        start=(k == 0), stop=(k == FT - 1))
                                smm.inc(inst)
                                if mi >= 2:
                                    t.wait_ge(svv.h, V0 + VS3 * c + 9 + mi)
                                elif c >= 1:
                                    t.wait_ge(svv.h, V0 + VS3 * (c - 1) + 41 + mi)
                                for k in range(FT):
                                    inst = t.matmul(
                                        pB1[mi % 2][:],
                                        w3s[:, k * DFF + mi * 128: k * DFF + (mi + 1) * 128],
                                        h2[:, k * TC:(k + 1) * TC],
                                        start=(k == 0), stop=(k == FT - 1))
                                smm.inc(inst)
                        assert smm.n == M0 + MS3 * NCH

                    @b.scalar
                    def _(s):
                        for c in range(NCH):
                            s.wait_ge(rX.h, X0 + 16 * (c + 1))
                            if c >= 1:
                                s.wait_ge(smm.h, M0 + MS3 * (c - 1) + 1)
                            for ft in range(FT):
                                sac.inc(s.activation(sq8c[:, ft * TC:(ft + 1) * TC],
                                                     x1ac[c % 2][:, ft * TC:(ft + 1) * TC],
                                                     AF.Square))
                            s.wait_ge(svv.h, V0 + VS3 * c + 1)
                            sac.inc(s.activation(rstdc[:], varrc[:], AF.Sqrt,
                                                 bias=RMS_EPS))
                            for mi in range(32):
                                s.wait_ge(smm.h, M0 + MS3 * c + 3 + 2 * mi)
                                if mi >= 2:
                                    s.wait_ge(svv.h, V0 + VS3 * c + 9 + mi)
                                elif c >= 1:
                                    s.wait_ge(svv.h, V0 + VS3 * (c - 1) + 41 + mi)
                                sac.inc(s.activation(
                                    gs2c[:, (mi % 2) * TC:(mi % 2 + 1) * TC],
                                    pA1[mi % 2][:], AF.Silu))
                        assert sac.n == A0 + AS3 * NCH

                    @b.vector
                    def _(v):
                        for c in range(NCH):
                            v.wait_ge(smm.h, M0 + MS3 * c + 1)
                            svv.inc(v.tensor_scalar_mul(varrc[:], pQc[:], inv_d))
                            v.wait_ge(sac.h, A0 + AS3 * c + 9)
                            svv.inc(v.reciprocal(rstdc[:], rstdc[:]))
                            v.wait_ge(smm.h, M0 + MS3 * c + 2)
                            for ft in range(FT):
                                svv.inc(v.scalar_tensor_tensor(
                                    h2[:, ft * TC:(ft + 1) * TC],
                                    x1ac[c % 2][:, ft * TC:(ft + 1) * TC],
                                    n2w[:, ft:ft + 1], pRSc[:],
                                    ALU.mult, ALU.mult))
                            for mi in range(32):
                                v.wait_ge(smm.h, M0 + MS3 * c + 4 + 2 * mi)
                                v.wait_ge(sac.h, A0 + AS3 * c + 10 + mi)
                                if mi >= 4:
                                    v.wait_ge(rS.h, S0 + 16 * (32 * c + mi - 3))
                                elif c >= 1:
                                    v.wait_ge(rS.h, S0 + 16 * (32 * (c - 1) + mi + 29))
                                svv.inc(v.tensor_mul(
                                    abst[:, (mi % 4) * TC:(mi % 4 + 1) * TC],
                                    gs2c[:, (mi % 2) * TC:(mi % 2 + 1) * TC],
                                    pB1[mi % 2][:]))
                        assert svv.n == V0 + VS3 * NCH
            if self.upto == "B3a":
                return nc

            with ExitStack() as ph:
                Ep = ph.enter_context
                w2s = Ep(nc.sbuf_tensor("w2s", [128, 32 * D], BF16))
                abc = [Ep(nc.sbuf_tensor(f"abc{i}", [128, 32 * TC], BF16))
                       for i in range(2)]
                x1ac2 = Ep(nc.sbuf_tensor("x1ac2", [128, FT * TC], BF16))
                x1fb = Ep(nc.sbuf_tensor("x1fb", [128, FT * TC], BF16))
                acfb = Ep(nc.sbuf_tensor("acfb", [128, FT * TC], BF16))
                tmpf = Ep(nc.sbuf_tensor("tmpf", [128, FT * TC], F32))
                osum = Ep(nc.sbuf_tensor("osum", [128, FT * TC], F32))
                pM2 = [Ep(nc.psum_tensor(f"pM2{i}", [128, TC], F32)) for i in range(2)]
                rX, rS = r8[0], r8[1]
                X0, S0 = rX.n, rS.n
                M0, V0, G0 = smm.n, svv.n, sgp.n
                MS4, VS4, GS4 = 8, 16, 8
                with nc.Block() as b:
                    @b.gpsimd
                    def _(g):
                        self.wload(g, dG, w2s, d_w2t.ap(), 32)
                        for c in range(NCH):
                            g.wait_ge(rX.h, X0 + 64 * (c + 1))
                            for mi in range(FT):
                                if c >= 1:
                                    g.wait_ge(svv.h, V0 + VS4 * (c - 1) + 2 * mi + 2)
                                sgp.inc(g.tensor_add(
                                    tmpf[:, mi * TC:(mi + 1) * TC],
                                    x1fb[:, mi * TC:(mi + 1) * TC],
                                    acfb[:, mi * TC:(mi + 1) * TC]))

                    @b.sync
                    def _(s):
                        for c in range(NCH):
                            if c >= 2:
                                s.wait_ge(smm.h, M0 + MS4 * (c - 1))
                            self.adma(s, rX, abc[c % 2], s_ab, c, 32)
                            if c >= 1:
                                s.wait_ge(svv.h, V0 + VS4 * (c - 1) + 15)
                                s.wait_ge(sgp.h, G0 + GS4 * c)
                            self.adma(s, rX, x1ac2, s_x1a, c, FT)
                            self.adma(s, rX, x1fb, s_x1, c, FT)
                            self.adma(s, rX, acfb, s_acc1, c, FT)
                            if c >= 1:
                                s.wait_ge(svv.h, V0 + VS4 * c)
                                self.adma(s, rS, osum, d_out, c - 1, FT, store=True)
                        s.wait_ge(svv.h, V0 + VS4 * NCH)
                        self.adma(s, rS, osum, d_out, NCH - 1, FT, store=True)
                        s.wait_ge(rS.h, rS.n)
                        s.wait_ge(rX.h, rX.n)

                    @b.tensor
                    def _(t):
                        t.wait_ge(dG.h, dG.n)
                        for c in range(NCH):
                            t.wait_ge(rX.h, X0 + 64 * c + 16)
                            for mi in range(FT):
                                if mi >= 2:
                                    t.wait_ge(svv.h, V0 + VS4 * c + 2 * mi - 3)
                                elif c >= 1:
                                    t.wait_ge(svv.h, V0 + VS4 * (c - 1) + 2 * mi + 13)
                                for k in range(32):
                                    inst = t.matmul(
                                        pM2[mi % 2][:],
                                        w2s[:, k * D + mi * 128: k * D + (mi + 1) * 128],
                                        abc[c % 2][:, k * TC:(k + 1) * TC],
                                        start=(k == 0), stop=(k == 31))
                                smm.inc(inst)
                        assert smm.n == M0 + MS4 * NCH

                    @b.vector
                    def _(v):
                        for c in range(NCH):
                            for mi in range(FT):
                                v.wait_ge(smm.h, M0 + MS4 * c + mi + 1)
                                if mi == 0:
                                    v.wait_ge(rX.h, X0 + 64 * c + 32)
                                    if c >= 1:
                                        v.wait_ge(rS.h, S0 + 16 * c)
                                svv.inc(v.tensor_add(
                                    osum[:, mi * TC:(mi + 1) * TC],
                                    pM2[mi % 2][:],
                                    x1ac2[:, mi * TC:(mi + 1) * TC]))
                                v.wait_ge(svv.h, svv.n)
                                v.wait_ge(sgp.h, G0 + GS4 * c + mi + 1)
                                svv.inc(v.tensor_add(
                                    osum[:, mi * TC:(mi + 1) * TC],
                                    osum[:, mi * TC:(mi + 1) * TC],
                                    tmpf[:, mi * TC:(mi + 1) * TC]))
                        assert svv.n == V0 + VS4 * NCH
            with nc.Block() as b:
                @b.sync
                def _(s):
                    s.wait_ge(dS.h, dS.n)
        return nc


def build(upto="D", debug=()):
    return Builder(upto, debug).build()


def _host_layouts(inputs):
    f32 = np.float32
    gi = lambda k: np.asarray(inputs[k], f32)
    A = {}
    A["wv_t"] = np.ascontiguousarray(gi("L0_attn_w_qkv")[2048:3072, :].T)
    A["wo_t"] = np.ascontiguousarray(gi("L0_attn_w_o").T)
    A["gate_w"] = np.ascontiguousarray(gi("L0_ffn_gate_w"))
    A["sh_gate"] = np.ascontiguousarray(gi("L0_ffn_sh_gate"))
    A["sh_up"] = np.ascontiguousarray(gi("L0_ffn_sh_up"))
    A["sh_down"] = np.ascontiguousarray(gi("L0_ffn_sh_down"))
    A["rt_gate"] = np.ascontiguousarray(gi("L0_ffn_rt_gate"))
    A["rt_up"] = np.ascontiguousarray(gi("L0_ffn_rt_up"))
    A["rt_down"] = np.ascontiguousarray(gi("L0_ffn_rt_down"))
    A["n1w"] = np.ascontiguousarray(gi("L1_attn_norm1_w").reshape(8, 128).T)
    A["n2w"] = np.ascontiguousarray(gi("L1_attn_norm2_w").reshape(8, 128).T)
    A["inproj_t"] = np.ascontiguousarray(gi("L1_attn_in_proj_w").T)
    cw = gi("L1_attn_conv_w").reshape(2048, 4)
    A["convw"] = np.ascontiguousarray(cw.reshape(16, 128, 4).transpose(1, 0, 2).reshape(128, 64))
    A["convb"] = np.ascontiguousarray(gi("L1_attn_conv_b").reshape(16, 128).T)
    A["xproj_t"] = np.ascontiguousarray(gi("L1_attn_x_proj_w").T)
    A["dtproj_t"] = np.ascontiguousarray(gi("L1_attn_dt_proj_w").T)
    A["dtb"] = np.ascontiguousarray(gi("L1_attn_dt_proj_b").reshape(16, 128).T)
    al = gi("L1_attn_A_log")
    A["alog_s"] = np.ascontiguousarray(al.reshape(256, 8, 16).transpose(1, 2, 0).reshape(128, 256))
    A["dcol"] = np.ascontiguousarray(gi("L1_attn_D").reshape(16, 128).T)
    A["outproj_t"] = np.ascontiguousarray(gi("L1_attn_out_proj_w").T)
    A["w1_t"] = np.ascontiguousarray(gi("L1_attn_w1").T)
    A["w3_t"] = np.ascontiguousarray(gi("L1_attn_w3").T)
    A["w2_t"] = np.ascontiguousarray(gi("L1_attn_w2").T)
    A["router_w"] = np.ascontiguousarray(gi("L1_ffn_router_w"))
    A["router_b"] = np.ascontiguousarray(gi("L1_ffn_router_b").reshape(1, 4))
    A["f_w1"] = np.ascontiguousarray(gi("L1_ffn_w1"))
    A["f_b1"] = np.ascontiguousarray(
        gi("L1_ffn_b1").reshape(4, 32, 128).transpose(2, 0, 1).reshape(128, 128))
    A["f_w2"] = np.ascontiguousarray(gi("L1_ffn_w2"))
    A["f_b2"] = np.ascontiguousarray(
        gi("L1_ffn_b2").reshape(4, 8, 128).transpose(2, 0, 1).reshape(128, 32))
    e8 = np.zeros((8, 8 * 128), f32)
    for e in range(8):
        e8[e, e * 128:(e + 1) * 128] = 1.0
    A["e8sel"] = e8
    A["ident"] = np.eye(128, dtype=f32)
    A["ones_col"] = np.ones((128, 1), f32)
    A["ones_row"] = np.ones((1, 128), f32)
    p16 = np.zeros((16, 128), f32)
    for m in range(128):
        p16[m % 16, m] = 1.0
    A["p16"] = p16
    sel = np.zeros((128, 16 * 128), f32)
    for j in range(16):
        for m in range(128):
            sel[j * 8 + m // 16, j * 128 + m] = 1.0
    A["selmat"] = sel
    gm = np.zeros((128, 8), f32)
    for k in range(128):
        gm[k, k // 16] = 1.0
    A["gmat"] = gm
    return A


def kernel(**inputs):
    x = np.asarray(inputs["L0_attn_x"], np.float32)
    W = _host_layouts(inputs)
    nc = build()
    in_maps = []
    for c in range(8):
        m = dict(W)
        m["x"] = np.ascontiguousarray(x[c].T)
        in_maps.append(m)
    res = run_bass_kernel_spmd(nc, in_maps, core_ids=list(range(8)))
    out = np.stack([np.asarray(r["out"]).T for r in res.results], axis=0)
    return np.ascontiguousarray(out).astype(np.float32)


# version 19 (final)
# speedup vs baseline: 1.0143x; 1.0143x over previous
import numpy as np
from contextlib import ExitStack

import concourse.bass as bass
import concourse.mybir as mybir
from concourse.bass_utils import run_bass_kernel_spmd

F32 = mybir.dt.float32
BF16 = mybir.dt.bfloat16
U32 = mybir.dt.uint32
AF = mybir.ActivationFunctionType
ALU = mybir.AluOpType
AX = mybir.AxisListType

D = 1024
T = 4096
TC = 512
NCH = T // TC
FT = D // 128
DI = 2048
DS = 16
DTR = 64
DFF = 4096
I0 = 2048
E0 = 8
E1 = 4
NTILE = 256
LN_EPS = 1e-5
RMS_EPS = 1e-6
BC_SKIP_C = False
BC_SKIP_B2 = False
BC_JB_LIMIT = 99
BC_P2_PARTS = "wmce"


class SemT:
    __slots__ = ("h", "n")

    def __init__(self, h):
        self.h = h
        self.n = 0

    def inc(self, inst, k=1):
        inst.then_inc(self.h, k)
        self.n += k
        return self.n


class Builder:
    def __init__(self, upto="D", debug=()):
        self.upto = upto
        self.debug = set(debug)
        self.nc = bass.Bass("TRN2", target_bir_lowering=False)

    def wload(self, eng, sem, dst, src_ap, nk):
        if nk == 1:
            inst = eng.dma_start(out=dst[:], in_=src_ap)
        else:
            m = src_ap.shape[-1]
            inst = eng.dma_start(
                out=dst[:].rearrange("p (k m) -> p k m", k=nk),
                in_=src_ap.rearrange("(k p) m -> p k m", k=nk, p=128))
        sem.inc(inst, 16)

    def adma(self, eng, sem, sbuf_t, dram_t, c, nk, store=False):
        dr = dram_t[:, c * TC:(c + 1) * TC]
        if nk > 1:
            dr = dr.rearrange("(k p) t -> p k t", k=nk, p=128)
            sb = sbuf_t[:].rearrange("p (k t) -> p k t", k=nk)
        else:
            sb = sbuf_t[:]
        inst = eng.dma_start(out=dr, in_=sb) if store else eng.dma_start(out=sb, in_=dr)
        sem.inc(inst, 16)


    def _bc_ev(self, v, u, e, pCH, mbsb, tscr, acb1, fb2,
               M0, PM, V0, PV, RA0, rA, smm, svv, dS, sac):
        ci, mi = u // 8, u % 8
        v.wait_ge(smm.h, M0 + PM[("cm", u)])
        v.wait_ge(svv.h, V0 + PV[("mbe", ci)])
        if e > 0:
            v.wait_ge(rA.h, RA0 + 16 * (2 * ci + 1))
        else:
            v.wait_ge(rA.h, RA0 + 16 * ci)
        acs = acb1[:, mi * (T // 8 // 1 if False else 512):(mi + 1) * 512]
        col = fb2[:, e * 8 + mi:e * 8 + mi + 1]
        if e == 0:
            svv.inc(v.scalar_tensor_tensor(acs, pCH[u % 2][:], col, mbsb[:],
                                           ALU.add, ALU.mult))
        else:
            sl = u % 2
            svv.inc(v.scalar_tensor_tensor(tscr[:, sl * 512:(sl + 1) * 512],
                                           pCH[u % 2][:], col, mbsb[:],
                                           ALU.add, ALU.mult))
            v.wait_ge(svv.h, svv.n)
            svv.inc(v.tensor_add(acs, acs, tscr[:, sl * 512:(sl + 1) * 512]))

    def emit_norm_stats(self, xc, nft, eps, with_mean, pMN, pRS):
        nc, b = self.nc, None
        sac, svv, smm = self.sac, self.svv, self.smm
        sq8, pQ, pS = self.sq8, self.pQ, self.pS
        meanr, varr, rstdr = self.meanr, self.varr, self.rstdr
        onesc, onesr = self.onesc, self.onesr
        inv_n = 1.0 / (nft * 128)
        with nc.Block() as b:
            @b.scalar
            def _(s):
                for ft in range(nft):
                    sac.inc(s.activation(sq8[:, ft * TC:(ft + 1) * TC],
                                         xc[:, ft * TC:(ft + 1) * TC], AF.Square))
        with nc.Block() as b:
            @b.tensor
            def _(t):
                if with_mean:
                    for ft in range(nft):
                        inst = t.matmul(pS[:], onesc[:], xc[:, ft * TC:(ft + 1) * TC],
                                        start=(ft == 0), stop=(ft == nft - 1))
                    smm.inc(inst)
                for ft in range(nft):
                    inst = t.matmul(pQ[:], onesc[:], sq8[:, ft * TC:(ft + 1) * TC],
                                    start=(ft == 0), stop=(ft == nft - 1))
                smm.inc(inst)
        with nc.Block() as b:
            @b.scalar
            def _(s):
                if with_mean:
                    sac.inc(s.activation(meanr[:], pS[:], AF.Copy, scale=inv_n))
                    s.wait_ge(sac.h, sac.n)
                    sac.inc(s.activation(varr[:], meanr[:], AF.Square))
        with nc.Block() as b:
            @b.vector
            def _(v):
                if with_mean:
                    svv.inc(v.scalar_tensor_tensor(varr[:], pQ[:], inv_n, varr[:],
                                                   ALU.mult, ALU.subtract))
                else:
                    svv.inc(v.tensor_scalar_mul(varr[:], pQ[:], inv_n))
        with nc.Block() as b:
            @b.scalar
            def _(s):
                sac.inc(s.activation(rstdr[:], varr[:], AF.Sqrt, bias=eps))
        with nc.Block() as b:
            @b.vector
            def _(v):
                svv.inc(v.reciprocal(rstdr[:], rstdr[:]))
        with nc.Block() as b:
            @b.tensor
            def _(t):
                if with_mean:
                    smm.inc(t.matmul(pMN[:], onesr[:], meanr[:], start=True, stop=True))
                smm.inc(t.matmul(pRS[:], onesr[:], rstdr[:], start=True, stop=True))

    def emit_router(self, xc, wsb, ne, c, mask_dram, with_score, with_bias, mstage):
        nc = self.nc
        sac, svv, smm, dS = self.sac, self.svv, self.smm, self.dS
        pR, pT = self.pR, self.pT
        mx8, max8, idx8, idxf, scoreC, mcols = (self.mx8, self.max8, self.idx8,
                                                self.idxf, self.scoreC, self.mcols)
        for tt in range(TC // 128):
            with nc.Block() as b:
                @b.tensor
                def _(t):
                    for k in range(FT):
                        inst = t.matmul(
                            pR[:, 0:ne],
                            xc[:, k * TC + tt * 128: k * TC + tt * 128 + 128],
                            wsb[:, k * ne:(k + 1) * ne],
                            start=(k == 0), stop=(k == FT - 1))
                    smm.inc(inst)
            with nc.Block() as b:
                @b.vector
                def _(v):
                    if ne < 8:
                        svv.inc(v.memset(mx8[:], -3.0e38))
                        v.wait_ge(svv.h, svv.n)
                    if with_bias:
                        svv.inc(v.tensor_add(mx8[:, 0:ne], pR[:, 0:ne], self.rb1b[:, 0:ne]))
                    else:
                        svv.inc(v.tensor_copy(mx8[:, 0:ne], pR[:, 0:ne]))
                    v.wait_ge(svv.h, svv.n)
                    svv.inc(v.max(max8[:], mx8[:]))
                    v.wait_ge(svv.h, svv.n)
                    svv.inc(v.max_index(idx8[:], max8[:], mx8[:]))
                    v.wait_ge(svv.h, svv.n)
                    svv.inc(v.tensor_copy(idxf[:], idx8[:, 0:1]))

                @b.scalar
                def _(s):
                    if with_score:
                        s.wait_ge(svv.h, svv.n - 2)
                        sac.inc(s.activation(scoreC[:], max8[:, 0:1], AF.Sigmoid))
            with nc.Block() as b:
                @b.vector
                def _(v):
                    for e in range(ne):
                        if with_score:
                            svv.inc(v.scalar_tensor_tensor(
                                mcols[:, e:e + 1], idxf[:], float(e), scoreC[:],
                                ALU.is_equal, ALU.mult))
                        else:
                            svv.inc(v.tensor_single_scalar(
                                mcols[:, e:e + 1], idxf[:], float(e), ALU.is_equal))
            with nc.Block() as b:
                @b.tensor
                def _(t):
                    smm.inc(t.transpose(pT[0:ne, 0:128], mcols[:, 0:ne], self.ident[:]))
            with nc.Block() as b:
                @b.vector
                def _(v):
                    svv.inc(v.tensor_copy(mstage[:, tt * 128:(tt + 1) * 128],
                                          pT[0:ne, 0:128]))
        with nc.Block() as b:
            @b.sync
            def _(s):
                dS.inc(s.dma_start(out=mask_dram[:, c * TC:(c + 1) * TC],
                                   in_=mstage[:]), 16)
                s.wait_ge(dS.h, dS.n)

    def emit_lnrouter_phase(self, d_x, acc_src, attnc, out_xn, out_x1, wsb, ne,
                            with_score, mask_out, do_svec, svec, eps, tag,
                            xn_sbuf=None):
        nc = self.nc
        smm, sac, svv = self.smm, self.sac, self.svv
        rX, rS, rC = self.r8[0], self.r8[1], self.r8[2]
        X0, S0, C0 = rX.n, rS.n, rC.n
        M0, A0, V0 = smm.n, sac.n, svv.n
        a4 = acc_src is not None
        MS = 12
        AS = 27 if a4 else 23
        VS = 66 if a4 else 71
        RSS = 48 if a4 else 32
        TTV = 10 if a4 else 13
        TTB = 26 if a4 else 19
        vV_x1 = 8
        vV_vc = 9 if a4 else 1
        vV_rc = 10 if a4 else 2
        vV_sub = 18 if a4 else 10
        vV_mul = 26 if a4 else 18
        vR_pRcons = 2 if a4 else 1
        vR_max = 3 if a4 else 2
        vR_mce = 9 if a4 else 12
        inv_n = 1.0 / D
        ident, onesc, onesr = self.ident, self.onesc, self.onesr
        rb1b = self.rb1b
        def store_xn(s, cp, xnbf):
            if xn_sbuf is not None:
                rS.inc(s.dma_start(
                    out=xn_sbuf[:].rearrange(
                        "p (k t) -> p k t", k=FT)[:, :, cp * TC:(cp + 1) * TC],
                    in_=xnbf[cp % 2][:].rearrange("p (k t) -> p k t", k=FT)), 16)
            else:
                self.adma(s, rS, xnbf[cp % 2], out_xn, cp, FT, store=True)

        with ExitStack() as ph:
            Ep = ph.enter_context
            x0c = [Ep(nc.sbuf_tensor(f"{tag}x0c{i}", [128, FT * TC], F32))
                   for i in range(2)]
            accb = ([Ep(nc.sbuf_tensor(f"{tag}acc{i}", [128, FT * TC], BF16))
                     for i in range(2)] if a4 else None)
            x1c = Ep(nc.sbuf_tensor(f"{tag}x1c", [128, FT * TC], F32)) if a4 else None
            xnc = Ep(nc.sbuf_tensor(f"{tag}xnc", [128, FT * TC], F32))
            sq8 = Ep(nc.sbuf_tensor(f"{tag}sq8", [128, FT * TC], F32))
            xnbf = [Ep(nc.sbuf_tensor(f"{tag}xnb{i}", [128, FT * TC], BF16))
                    for i in range(2)]
            x1bf = ([Ep(nc.sbuf_tensor(f"{tag}x1b{i}", [128, FT * TC], BF16))
                     for i in range(2)] if a4 else None)
            meanr = Ep(nc.sbuf_tensor(f"{tag}mean", [1, TC], F32))
            varr = Ep(nc.sbuf_tensor(f"{tag}varr", [1, TC], F32))
            rstdr = Ep(nc.sbuf_tensor(f"{tag}rstd", [1, TC], F32))
            svst = Ep(nc.sbuf_tensor(f"{tag}svst", [128, FT], F32))
            mx8 = Ep(nc.sbuf_tensor(f"{tag}mx8", [128, 8], F32))
            max8 = Ep(nc.sbuf_tensor(f"{tag}max8", [128, 8], F32))
            idx8 = Ep(nc.sbuf_tensor(f"{tag}idx8", [128, 8], U32))
            idxf = Ep(nc.sbuf_tensor(f"{tag}idxf", [128, 1], F32))
            scoreC = Ep(nc.sbuf_tensor(f"{tag}scor", [128, 1], F32))
            mcols = Ep(nc.sbuf_tensor(f"{tag}mcol", [128, 8], F32))
            mst = [Ep(nc.sbuf_tensor(f"{tag}mst{i}", [ne, TC], BF16))
                   for i in range(2)]
            pS = Ep(nc.psum_tensor(f"{tag}pS", [1, TC], F32))
            pQ = Ep(nc.psum_tensor(f"{tag}pQ", [1, TC], F32))
            pMN = Ep(nc.psum_tensor(f"{tag}pMN", [128, TC], F32))
            pRS = Ep(nc.psum_tensor(f"{tag}pRS", [128, TC], F32))
            pR = Ep(nc.psum_tensor(f"{tag}pR", [128, 8], F32))
            pT = Ep(nc.psum_tensor(f"{tag}pT", [128, 128], F32))

            def nsrc(c):
                return x1c if a4 else x0c[c % 2]

            with nc.Block(no_gpsimd_drain=True) as b:
                @b.sync
                def _(s):
                    for c in range(NCH):
                        if c >= 2:
                            if a4:
                                s.wait_ge(svv.h, V0 + VS * (c - 2) + vV_x1)
                            else:
                                s.wait_ge(sac.h, A0 + AS * (c - 2) + 8)
                                s.wait_ge(svv.h, V0 + VS * (c - 2) + vV_sub)
                                s.wait_ge(smm.h, M0 + MS * (c - 2) + 1)
                        self.adma(s, rX, x0c[c % 2], d_x, c, FT)
                        if c >= 1:
                            cp = c - 1
                            s.wait_ge(sac.h, A0 + AS * cp + 19)
                            store_xn(s, cp, xnbf)
                            if a4:
                                s.wait_ge(sac.h, A0 + AS * cp + 27)
                                self.adma(s, rS, x1bf[cp % 2], out_x1, cp, FT,
                                          store=True)
                            s.wait_ge(svv.h, V0 + VS * cp + VS)
                            rS.inc(s.dma_start(
                                out=mask_out[:, cp * TC:(cp + 1) * TC],
                                in_=mst[cp % 2][:]), 16)
                    cp = NCH - 1
                    s.wait_ge(sac.h, A0 + AS * cp + 19)
                    store_xn(s, cp, xnbf)
                    if a4:
                        s.wait_ge(sac.h, A0 + AS * cp + 27)
                        self.adma(s, rS, x1bf[cp % 2], out_x1, cp, FT, store=True)
                    s.wait_ge(svv.h, V0 + VS * cp + VS)
                    rS.inc(s.dma_start(out=mask_out[:, cp * TC:(cp + 1) * TC],
                                       in_=mst[cp % 2][:]), 16)
                    s.wait_ge(rS.h, rS.n)
                    s.wait_ge(rX.h, rX.n)

                if a4:
                    @b.gpsimd
                    def _(g):
                        for c in range(NCH):
                            if c >= 2:
                                g.wait_ge(svv.h, V0 + VS * (c - 2) + vV_x1)
                            self.adma(g, rC, accb[c % 2], acc_src, c, FT)
                        g.wait_ge(rC.h, rC.n)
                elif getattr(self, "late_casts", None) is not None:
                    lc = self.late_casts
                    self.late_casts = None

                    @b.gpsimd
                    def _(g):
                        g.wait_ge(rX.h, X0 + 16 * NCH)
                        lc(g)

                @b.tensor
                def _(t):
                    for c in range(NCH):
                        if a4:
                            t.wait_ge(svv.h, V0 + VS * c + vV_x1)
                        else:
                            t.wait_ge(rX.h, X0 + 16 * (c + 1))
                        if c >= 1:
                            t.wait_ge(sac.h, A0 + AS * (c - 1) + 9)
                        for ft in range(FT):
                            inst = t.matmul(pS[:], onesc[:],
                                            nsrc(c)[:, ft * TC:(ft + 1) * TC],
                                            start=(ft == 0), stop=(ft == FT - 1))
                        smm.inc(inst)
                        t.wait_ge(sac.h, A0 + AS * c + 8)
                        if c >= 1:
                            t.wait_ge(svv.h, V0 + VS * (c - 1) + vV_vc)
                        for ft in range(FT):
                            inst = t.matmul(pQ[:], onesc[:],
                                            sq8[:, ft * TC:(ft + 1) * TC],
                                            start=(ft == 0), stop=(ft == FT - 1))
                        smm.inc(inst)
                        t.wait_ge(sac.h, A0 + AS * c + 9)
                        if c >= 1:
                            t.wait_ge(svv.h, V0 + VS * (c - 1) + vV_sub)
                        smm.inc(t.matmul(pMN[:], onesr[:], meanr[:],
                                         start=True, stop=True))
                        t.wait_ge(svv.h, V0 + VS * c + vV_rc)
                        if c >= 1:
                            t.wait_ge(svv.h, V0 + VS * (c - 1) + vV_mul)
                        smm.inc(t.matmul(pRS[:], onesr[:], rstdr[:],
                                         start=True, stop=True))
                        for tt in range(TC // 128):
                            t.wait_ge(svv.h, V0 + VS * c + vV_mul)
                            if tt >= 1:
                                t.wait_ge(svv.h, V0 + VS * c + TTB
                                          + TTV * (tt - 1) + vR_pRcons)
                            elif c >= 1:
                                t.wait_ge(svv.h, V0 + VS * (c - 1) + TTB
                                          + TTV * 3 + vR_pRcons)
                            for k in range(FT):
                                inst = t.matmul(
                                    pR[:, 0:ne],
                                    xnc[:, k * TC + tt * 128: k * TC + tt * 128 + 128],
                                    wsb[:, k * ne:(k + 1) * ne],
                                    start=(k == 0), stop=(k == FT - 1))
                            smm.inc(inst)
                            t.wait_ge(svv.h, V0 + VS * c + TTB + TTV * tt + vR_mce)
                            if tt >= 1:
                                t.wait_ge(svv.h, V0 + VS * c + TTB + TTV * tt)
                            elif c >= 1:
                                t.wait_ge(svv.h, V0 + VS * c)
                            smm.inc(t.transpose(pT[0:ne, 0:128], mcols[:, 0:ne],
                                                ident[:]))
                        assert smm.n == M0 + MS * (c + 1), (smm.n, M0, c)

                @b.scalar
                def _(s):
                    for c in range(NCH):
                        if a4:
                            s.wait_ge(svv.h, V0 + VS * c + vV_x1)
                        else:
                            s.wait_ge(rX.h, X0 + 16 * (c + 1))
                        if c >= 1:
                            s.wait_ge(smm.h, M0 + MS * (c - 1) + 2)
                        for ft in range(FT):
                            sac.inc(s.activation(sq8[:, ft * TC:(ft + 1) * TC],
                                                 nsrc(c)[:, ft * TC:(ft + 1) * TC],
                                                 AF.Square))
                        s.wait_ge(smm.h, M0 + MS * c + 1)
                        sac.inc(s.activation(meanr[:], pS[:], AF.Copy, scale=inv_n))
                        s.wait_ge(sac.h, sac.n)
                        sac.inc(s.activation(varr[:], meanr[:], AF.Square))
                        s.wait_ge(svv.h, V0 + VS * c + vV_vc)
                        sac.inc(s.activation(rstdr[:], varr[:], AF.Sqrt, bias=eps))
                        s.wait_ge(svv.h, V0 + VS * c + vV_mul)
                        if c >= 2:
                            s.wait_ge(rS.h, S0 + RSS * (c - 2) + 16)
                        if do_svec and c >= 1:
                            s.wait_ge(svv.h, V0 + VS * (c - 1) + 19)
                        for ft in range(FT):
                            if do_svec:
                                sac.inc(s.activation(
                                    xnbf[c % 2][:, ft * TC:(ft + 1) * TC],
                                    xnc[:, ft * TC:(ft + 1) * TC], AF.Copy,
                                    accum_out=svst[:, ft:ft + 1]))
                            else:
                                sac.inc(s.activation(
                                    xnbf[c % 2][:, ft * TC:(ft + 1) * TC],
                                    xnc[:, ft * TC:(ft + 1) * TC], AF.Copy))
                        if a4:
                            if c >= 2:
                                s.wait_ge(rS.h, S0 + RSS * (c - 2) + 32)
                            for ft in range(FT):
                                sac.inc(s.activation(
                                    x1bf[c % 2][:, ft * TC:(ft + 1) * TC],
                                    x1c[:, ft * TC:(ft + 1) * TC], AF.Copy))
                        if with_score:
                            for tt in range(TC // 128):
                                s.wait_ge(svv.h, V0 + VS * c + TTB + TTV * tt + vR_max)
                                if tt >= 1:
                                    s.wait_ge(svv.h, V0 + VS * c + TTB
                                              + TTV * (tt - 1) + vR_mce)
                                elif c >= 1:
                                    s.wait_ge(svv.h, V0 + VS * (c - 1) + TTB
                                              + TTV * 3 + vR_mce)
                                sac.inc(s.activation(scoreC[:], max8[:, 0:1],
                                                     AF.Sigmoid))
                        assert sac.n == A0 + AS * (c + 1), (sac.n, A0, c)

                @b.vector
                def _(v):
                    for c in range(NCH):
                        if a4:
                            v.wait_ge(rX.h, X0 + 16 * (c + 1))
                            v.wait_ge(rC.h, C0 + 16 * (c + 1))
                            if c >= 1:
                                v.wait_ge(sac.h, A0 + AS * (c - 1) + 27)
                                v.wait_ge(smm.h, M0 + MS * (c - 1) + 1)
                            for ft in range(FT):
                                svv.inc(v.scalar_tensor_tensor(
                                    x1c[:, ft * TC:(ft + 1) * TC],
                                    accb[c % 2][:, ft * TC:(ft + 1) * TC],
                                    attnc[:, ft:ft + 1],
                                    x0c[c % 2][:, ft * TC:(ft + 1) * TC],
                                    ALU.add, ALU.add))
                        v.wait_ge(smm.h, M0 + MS * c + 2)
                        v.wait_ge(sac.h, A0 + AS * c + 10)
                        svv.inc(v.scalar_tensor_tensor(varr[:], pQ[:], inv_n,
                                                       varr[:], ALU.mult,
                                                       ALU.subtract))
                        v.wait_ge(sac.h, A0 + AS * c + 11)
                        svv.inc(v.reciprocal(rstdr[:], rstdr[:]))
                        v.wait_ge(smm.h, M0 + MS * c + 3)
                        if c >= 1:
                            v.wait_ge(sac.h, A0 + AS * (c - 1) + 19)
                            v.wait_ge(smm.h, M0 + MS * (c - 1) + 11)
                        for ft in range(FT):
                            svv.inc(v.tensor_sub(xnc[:, ft * TC:(ft + 1) * TC],
                                                 nsrc(c)[:, ft * TC:(ft + 1) * TC],
                                                 pMN[:]))
                        v.wait_ge(smm.h, M0 + MS * c + 4)
                        v.wait_ge(svv.h, svv.n)
                        for ft in range(FT):
                            svv.inc(v.tensor_mul(xnc[:, ft * TC:(ft + 1) * TC],
                                                 xnc[:, ft * TC:(ft + 1) * TC],
                                                 pRS[:]))
                        if do_svec:
                            v.wait_ge(sac.h, A0 + AS * c + 19)
                            svv.inc(v.tensor_add(svec[:], svec[:], svst[:]))
                        for tt in range(TC // 128):
                            v.wait_ge(smm.h, M0 + MS * c + 5 + 2 * tt)
                            if with_score:
                                if tt >= 1:
                                    v.wait_ge(sac.h, A0 + AS * c + 20 + tt - 1)
                                elif c >= 1:
                                    v.wait_ge(sac.h, A0 + AS * (c - 1) + 23)
                            if a4:
                                svv.inc(v.memset(mx8[:], -3.0e38))
                                v.wait_ge(svv.h, svv.n)
                                svv.inc(v.tensor_add(mx8[:, 0:ne], pR[:, 0:ne],
                                                     rb1b[:, 0:ne]))
                            else:
                                svv.inc(v.tensor_copy(mx8[:, 0:ne], pR[:, 0:ne]))
                            v.wait_ge(svv.h, svv.n)
                            svv.inc(v.max(max8[:], mx8[:]))
                            v.wait_ge(svv.h, svv.n)
                            svv.inc(v.max_index(idx8[:], max8[:], mx8[:]))
                            v.wait_ge(svv.h, svv.n)
                            svv.inc(v.tensor_copy(idxf[:], idx8[:, 0:1]))
                            v.wait_ge(svv.h, svv.n)
                            if with_score:
                                v.wait_ge(sac.h, A0 + AS * c + 20 + tt)
                            for e in range(ne):
                                if with_score:
                                    svv.inc(v.scalar_tensor_tensor(
                                        mcols[:, e:e + 1], idxf[:], float(e),
                                        scoreC[:], ALU.is_equal, ALU.mult))
                                else:
                                    svv.inc(v.tensor_single_scalar(
                                        mcols[:, e:e + 1], idxf[:], float(e),
                                        ALU.is_equal))
                            v.wait_ge(smm.h, M0 + MS * c + 6 + 2 * tt)
                            if c >= 2 and tt == 0:
                                v.wait_ge(rS.h, S0 + RSS * (c - 2) + RSS)
                            svv.inc(v.tensor_copy(
                                mst[c % 2][:, tt * 128:(tt + 1) * 128],
                                pT[0:ne, 0:128]))
                        assert svv.n == V0 + VS * (c + 1), (svv.n, V0, c)

    def build(self):
        nc = self.nc
        dbg = self.debug

        def dram(name, shape, dtype=F32, kind="ExternalInput"):
            return nc.dram_tensor(name, shape, dtype, kind=kind)

        def scratch(name, shape, dtype):
            kind = "ExternalOutput" if name in dbg else "Internal"
            return nc.dram_tensor(name, shape, dtype, kind=kind)

        d_x = dram("x", [D, T])
        d_wvt = dram("wv_t", [D, D])
        d_wot = dram("wo_t", [D, D])
        d_gatew = dram("gate_w", [D, E0])
        d_shg = dram("sh_gate", [D, I0])
        d_shu = dram("sh_up", [D, I0])
        d_shd = dram("sh_down", [I0, D])
        d_rtg = dram("rt_gate", [E0, D, I0])
        d_rtu = dram("rt_up", [E0, D, I0])
        d_rtd = dram("rt_down", [E0, I0, D])
        d_n1w = dram("n1w", [128, FT])
        d_n2w = dram("n2w", [128, FT])
        d_inpt = dram("inproj_t", [D, 2 * DI])
        d_convw = dram("convw", [128, 64])
        d_convb = dram("convb", [128, 16])
        d_xpt = dram("xproj_t", [DI, 96])
        d_dtt = dram("dtproj_t", [DTR, DI])
        d_dtb = dram("dtb", [128, 16])
        d_alog = dram("alog_s", [128, NTILE])
        d_dcol = dram("dcol", [128, 16])
        d_outt = dram("outproj_t", [DI, D])
        d_w1t = dram("w1_t", [D, DFF])
        d_w3t = dram("w3_t", [D, DFF])
        d_w2t = dram("w2_t", [DFF, D])
        d_rw1 = dram("router_w", [D, E1])
        d_rb1 = dram("router_b", [1, E1])
        d_fw1 = dram("f_w1", [E1, D, DFF])
        d_fb1 = dram("f_b1", [128, E1 * 32])
        d_fw2 = dram("f_w2", [E1, DFF, D])
        d_fb2 = dram("f_b2", [128, E1 * 8])
        d_e8sel = dram("e8sel", [8, 8 * 128])
        d_ident = dram("ident", [128, 128])
        d_onesc = dram("ones_col", [128, 1])
        d_onesr = dram("ones_row", [1, 128])
        d_p16 = dram("p16", [16, 128])
        d_sel = dram("selmat", [128, 16 * 128])
        d_gmat = dram("gmat", [128, 8])

        s_xn0 = scratch("s_xn0", [D, T], BF16)
        s_mask0 = scratch("s_mask0", [E0, T], BF16)
        s_cg = scratch("s_cg", [E0 + 1, D, I0], BF16)
        s_cu = scratch("s_cu", [E0 + 1, D, I0], BF16)
        s_cd = scratch("s_cd", [E0 + 1, I0, D], BF16)
        s_cf1 = scratch("s_cf1", [E1, D, DFF], BF16)
        s_cf2 = scratch("s_cf2", [E1, DFF, D], BF16)
        s_acc = scratch("s_acc", [D, T], BF16)
        s_x1 = scratch("s_x1", [D, T], BF16)
        s_xn1 = scratch("s_xn1", [D, T], BF16)
        s_mask1 = scratch("s_mask1", [E1, T], BF16)
        s_xinc = scratch("s_xinc", [DI, T], BF16)
        s_sz = scratch("s_sz", [DI, T], BF16)
        s_delta = scratch("s_delta", [DI, T], BF16)
        s_bc = scratch("s_bc", [32, T], F32)
        s_y = scratch("s_y", [DI, T], BF16)
        s_x1a = scratch("s_x1a", [D, T], BF16)
        s_ab = scratch("s_ab", [DFF, T], BF16)
        s_l1a = scratch("s_l1a", [D, T], BF16)
        s_h1 = scratch("s_h1", [DFF, T], BF16)
        s_acc1 = scratch("s_acc1", [D, T], BF16)
        d_out = dram("out", [D, T], F32, kind="ExternalOutput")

        st = ExitStack()
        with st:
            E = st.enter_context
            self.dS = dS = SemT(E(nc.semaphore("dS")))
            self.dG = dG = SemT(E(nc.semaphore("dG")))
            self.smm = smm = SemT(E(nc.semaphore("smm")))
            self.sac = sac = SemT(E(nc.semaphore("sac")))
            self.svv = svv = SemT(E(nc.semaphore("svv")))
            self.sgp = sgp = SemT(E(nc.semaphore("sgp")))
            r8 = [SemT(E(nc.semaphore(f"r8_{i}"))) for i in range(8)]
            self.r8 = r8
            rB = SemT(E(nc.semaphore("rB")))
            rH = SemT(E(nc.semaphore("rH")))
            rM = SemT(E(nc.semaphore("rM")))
            rA = SemT(E(nc.semaphore("rA")))

            self.ident = ident = E(nc.sbuf_tensor("identS", [128, 128], F32))
            self.onesc = onesc = E(nc.sbuf_tensor("onescS", [128, 1], F32))
            self.onesr = onesr = E(nc.sbuf_tensor("onesrS", [1, 128], F32))
            n1w = E(nc.sbuf_tensor("n1wS", [128, FT], F32))
            n2w = E(nc.sbuf_tensor("n2wS", [128, FT], F32))
            convw = E(nc.sbuf_tensor("convwS", [128, 64], F32))
            convb = E(nc.sbuf_tensor("convbS", [128, 16], F32))
            dtb = E(nc.sbuf_tensor("dtbS", [128, 16], F32))
            dcol = E(nc.sbuf_tensor("dcolS", [128, 16], F32))
            fb1 = E(nc.sbuf_tensor("fb1S", [128, E1 * 32], F32))
            fb2 = E(nc.sbuf_tensor("fb2S", [128, E1 * 8], F32))
            gatew = E(nc.sbuf_tensor("gatewS", [128, FT * E0], F32))
            rw1 = E(nc.sbuf_tensor("rw1S", [128, FT * E1], F32))
            self.rb1b = rb1b = E(nc.sbuf_tensor("rb1bS", [128, E1], F32))
            onesrb = E(nc.sbuf_tensor("onesrbS", [1, 128], BF16))
            cvals = [LN_EPS, RMS_EPS, -1.0, 1.0 / D, 0.70710678118654752]
            cbuf = E(nc.sbuf_tensor("cbufS", [128, len(cvals)], F32))
            for ci, cv in enumerate(cvals):
                nc.const_aps.register(F32, cv, cbuf[:, ci:ci + 1])
            svec = E(nc.sbuf_tensor("svecS", [128, FT], F32))
            attnc = E(nc.sbuf_tensor("attncS", [128, FT], F32))
            rbrow = E(nc.sbuf_tensor("rbrowS", [1, E1], F32))

            stxn = ExitStack()
            e8selS = stxn.enter_context(nc.sbuf_tensor("e8selS", [8, 8 * 128], BF16))
            with nc.Block(no_gpsimd_drain=True) as b:
                @b.sync
                def _(s):
                    for dst, src in [(ident, d_ident), (onesc, d_onesc), (onesr, d_onesr),
                                     (n1w, d_n1w),
                                     (n2w, d_n2w), (convw, d_convw), (convb, d_convb),
                                     (dtb, d_dtb), (dcol, d_dcol), (fb1, d_fb1),
                                     (fb2, d_fb2), (rbrow, d_rb1)]:
                        dS.inc(s.dma_start(out=dst[:], in_=src.ap()), 16)
                    self.wload(s, dS, gatew, d_gatew.ap(), FT)
                    self.wload(s, dS, rw1, d_rw1.ap(), FT)
                    s.wait_ge(dS.h, dS.n)

                @b.gpsimd
                def _(g):
                    dG.inc(g.dma_start(out=e8selS[:], in_=d_e8sel.ap()), 16)
                    self.CAST0 = dG.n
                    dG.inc(g.dma_start(out=s_cg[0], in_=d_shg.ap()), 16)
                    dG.inc(g.dma_start(out=s_cu[0], in_=d_shu.ap()), 16)
                    dG.inc(g.dma_start(out=s_cd[0], in_=d_shd.ap()), 16)

            def _late_casts(g):
                for q in range(1, E0 + 1):
                    dG.inc(g.dma_start(out=s_cg[q], in_=d_rtg[q - 1]), 16)
                    dG.inc(g.dma_start(out=s_cu[q], in_=d_rtu[q - 1]), 16)
                    dG.inc(g.dma_start(out=s_cd[q], in_=d_rtd[q - 1]), 16)
            self.late_casts = _late_casts

            with nc.Block() as b:
                @b.vector
                def _(v):
                    svv.inc(v.memset(svec[:], 0.0))
                    svv.inc(v.memset(onesrb[:], 1.0))
                    for ci, cv in enumerate(cvals):
                        svv.inc(v.memset(cbuf[:, ci:ci + 1], cv))
            with ExitStack() as ph:
                p_rb = ph.enter_context(nc.psum_tensor("p_rb", [128, E1], F32))
                with nc.Block() as b:
                    @b.tensor
                    def _(t):
                        smm.inc(t.matmul(p_rb[:], onesr[:], rbrow[:], start=True, stop=True))
                with nc.Block() as b:
                    @b.vector
                    def _(v):
                        svv.inc(v.tensor_copy(rb1b[:], p_rb[:]))

            XN = stxn.enter_context(nc.sbuf_tensor("XN", [128, FT * T], BF16))
            mrowAll = stxn.enter_context(nc.sbuf_tensor("mrowAll", [E0, T], BF16))
            self.emit_lnrouter_phase(d_x, None, None, s_xn0, None, gatew, E0,
                                     True, mrowAll, True, svec, LN_EPS, "a1",
                                     xn_sbuf=XN)
            if self.upto == "A1":
                return nc

            with ExitStack() as ph2:
                wbig = ph2.enter_context(nc.sbuf_tensor("wbig", [128, FT * D], F32))
                vsum = ph2.enter_context(nc.sbuf_tensor("vsum", [128, FT], F32))
                pV = ph2.enter_context(nc.psum_tensor("pV", [128, FT], F32))
                for wd_, src, dst in [(d_wvt, svec, vsum), (d_wot, vsum, attnc)]:
                    with nc.Block(no_gpsimd_drain=True) as b:
                        @b.sync
                        def _(s):
                            self.wload(s, dS, wbig, wd_.ap(), FT)
                            s.wait_ge(dS.h, dS.n)
                    with nc.Block(no_gpsimd_drain=True) as b:
                        @b.tensor
                        def _(t):
                            for mt in range(FT):
                                for k in range(FT):
                                    inst = t.matmul(
                                        pV[:, mt:mt + 1],
                                        wbig[:, k * D + mt * 128: k * D + (mt + 1) * 128],
                                        src[:, k:k + 1],
                                        start=(k == 0), stop=(k == FT - 1))
                            smm.inc(inst)
                    with nc.Block(no_gpsimd_drain=True) as b:
                        @b.vector
                        def _(v):
                            svv.inc(v.tensor_copy(dst[:], pV[:]))
            if self.upto == "A2":
                return nc

            with ExitStack() as ph2:
                E2 = ph2.enter_context
                HK = 8
                wgh = [E2(nc.sbuf_tensor(f"wgh{i}", [128, FT * 1024], BF16))
                       for i in range(2)]
                wuh = [E2(nc.sbuf_tensor(f"wuh{i}", [128, FT * 1024], BF16))
                       for i in range(2)]
                wdh = E2(nc.sbuf_tensor("wdh", [128, HK * D], BF16))
                xsb = E2(nc.sbuf_tensor("xsb", [128, FT * TC], BF16))
                actb = [E2(nc.sbuf_tensor(f"actb{i}", [128, HK * TC], BF16))
                        for i in range(2)]
                gs2 = E2(nc.sbuf_tensor("gs2", [128, 2 * TC], BF16))
                accb = [E2(nc.sbuf_tensor(f"accb{i}", [128, FT * TC], BF16))
                        for i in range(2)]
                pG = [E2(nc.psum_tensor(f"pG{i}", [128, TC], F32)) for i in range(2)]
                pU = [E2(nc.psum_tensor(f"pU{i}", [128, TC], F32)) for i in range(2)]
                pD_ = [E2(nc.psum_tensor(f"pDn{i}", [128, TC], F32)) for i in range(2)]
                pMB = E2(nc.psum_tensor("pMB", [128, TC], F32))

                halves = [(e, hf) for e in range(-1, E0) for hf in range(2)]
                NH = len(halves)
                ST0, LD0 = r8[2].n, r8[1].n
                rW = r8[3]
                W0 = rW.n
                CAST0 = self.CAST0
                wpos = {}
                wc = 0
                for h in range(NH):
                    if h == 0:
                        for j in (0, 1):
                            wc += 2
                            wpos[("gu", j)] = wc
                        wc += 1
                        wpos[("d", 0)] = wc
                    else:
                        if h + 1 < NH:
                            wc += 2
                            wpos[("gu", h + 1)] = wc
                        wc += 1
                        wpos[("d", h)] = wc
                Mb, Ab, Vb = [], [], []
                mm_, aa_, vv_ = smm.n, sac.n, svv.n
                for (e_, hf_) in halves:
                    Mb.append(mm_); Ab.append(aa_); Vb.append(vv_)
                    nxs_ = FT if e_ >= 0 else 0
                    mm_ += ((1 if e_ >= 0 else 0) + 3 * HK) * NCH
                    aa_ += HK * NCH
                    vv_ += (nxs_ + 2 * HK) * NCH
                Mb.append(mm_); Ab.append(aa_); Vb.append(vv_)

                def hparm(h):
                    e_, hf_ = halves[h]
                    nxs_ = FT if e_ >= 0 else 0
                    return e_, hf_, nxs_, (1 if e_ >= 0 else 0), (nxs_ + 2 * HK)

                def pA_silu(h, c, mi):
                    return Ab[h] + HK * c + mi + 1

                def vS(h, c):
                    _, _, nxs_, _, tvh = hparm(h)
                    return Vb[h] + (nxs_ + HK) * c + HK * max(0, c - 1)

                def pV_xsb(h, c):
                    _, _, nxs_, _, tvh = hparm(h)
                    return vS(h, c) + nxs_

                def pV_act(h, c, mi):
                    _, _, nxs_, _, tvh = hparm(h)
                    return vS(h, c) + nxs_ + mi + 1

                def pV_acc(h, c, mi):
                    _, _, nxs_, _, tvh = hparm(h)
                    if c < NCH - 1:
                        return vS(h, c + 1) + nxs_ + HK + mi + 1
                    return vS(h, NCH - 1) + nxs_ + 2 * HK + mi + 1

                def mS(h, c):
                    _, _, _, mbh, _ = hparm(h)
                    return Mb[h] + (mbh + 2 * HK) * c + HK * max(0, c - 1)

                def pM_pMB(h, c):
                    return mS(h, c) + 1

                def pM_G(h, c, mi):
                    _, _, _, mbh, _ = hparm(h)
                    return mS(h, c) + mbh + 2 * mi + 1

                def pM_U(h, c, mi):
                    return pM_G(h, c, mi) + 1

                def pM_D(h, c, mi):
                    _, _, _, mbh, _ = hparm(h)
                    if c < NCH - 1:
                        return mS(h, c + 1) + mbh + 2 * HK + mi + 1
                    return mS(h, NCH - 1) + mbh + 3 * HK + mi + 1

                def st16(h, c):
                    return ST0 + 16 * (NCH * h + c + 1)

                def ld16(h, c):
                    return LD0 + 16 * ((h - 1) * NCH + c + 1)

                def prev_masked(h):
                    for hp in range(h - 1, -1, -1):
                        if halves[hp][0] >= 0:
                            return hp
                    return None

                def wsrc(h):
                    e_, hf_ = halves[h]
                    q = h // 2
                    sl = slice(1024 * hf_, 1024 * (hf_ + 1))
                    return s_cg[q][:, sl], s_cu[q][:, sl], s_cd[q][sl, :]

                with nc.Block() as b:
                    @b.sync
                    def _(s):
                        for h in range(NH):
                            e_, hf_, nxs_, mbh, tvh = hparm(h)
                            for c in range(NCH):
                                s.wait_ge(svv.h, pV_acc(h, c, HK - 1))
                                self.adma(s, r8[2], accb[c % 2], s_acc, c, FT,
                                          store=True)
                                gidx = h * NCH + c + 2
                                th, tc = gidx // NCH, gidx % NCH
                                if 1 <= th < NH:
                                    s.wait_ge(r8[2].h, st16(h, c))
                                    self.adma(s, r8[1], accb[tc % 2], s_acc, tc, FT)
                        s.wait_ge(r8[2].h, r8[2].n)
                        s.wait_ge(r8[1].h, r8[1].n)

                    @b.tensor
                    def _(t):
                        for h in range(NH):
                            e_, hf_, nxs_, mbh, tvh = hparm(h)
                            buf = h % 2

                            def dgroup(cD, h=h):
                                t.wait_ge(svv.h, pV_act(h, cD, HK - 1))
                                if cD == 0:
                                    t.wait_ge(rW.h, W0 + 16 * wpos[("d", h)])
                                for mi in range(HK):
                                    sidx = HK * cD + mi
                                    if sidx >= 2:
                                        cc, m2 = divmod(sidx - 2, HK)
                                        t.wait_ge(svv.h, pV_acc(h, cc, m2))
                                    elif h >= 1:
                                        t.wait_ge(svv.h,
                                                  pV_acc(h - 1, NCH - 1, mi + HK - 2))
                                    for k in range(HK):
                                        inst = t.matmul(
                                            pD_[mi % 2][:],
                                            wdh[:, k * D + mi * 128:
                                                k * D + (mi + 1) * 128],
                                            actb[cD % 2][:, k * TC:(k + 1) * TC],
                                            start=(k == 0), stop=(k == HK - 1))
                                    smm.inc(inst)

                            t.wait_ge(rW.h, W0 + 16 * wpos[("gu", h)])
                            for c in range(NCH):
                                def xin(k, c=c, e_=e_):
                                    if e_ >= 0:
                                        return xsb[:, k * TC:(k + 1) * TC]
                                    return XN[:, k * T + c * TC: k * T + c * TC + TC]
                                if e_ >= 0:
                                    if c >= 1:
                                        t.wait_ge(svv.h, pV_xsb(h, c - 1))
                                    else:
                                        pm = prev_masked(h)
                                        if pm is not None:
                                            t.wait_ge(svv.h, pV_xsb(pm, NCH - 1))
                                    smm.inc(t.matmul(
                                        pMB[:],
                                        e8selS[:, e_ * 128:(e_ + 1) * 128],
                                        mrowAll[:, c * TC:(c + 1) * TC],
                                        start=True, stop=True))
                                for mi in range(HK):
                                    sidx = HK * c + mi
                                    if e_ >= 0 and mi == 0:
                                        t.wait_ge(svv.h, pV_xsb(h, c))
                                    if sidx >= 2:
                                        cc, m2 = divmod(sidx - 2, HK)
                                        t.wait_ge(sac.h, pA_silu(h, cc, m2))
                                    elif h >= 1:
                                        t.wait_ge(sac.h,
                                                  pA_silu(h - 1, NCH - 1, mi + HK - 2))
                                    for k in range(FT):
                                        inst = t.matmul(
                                            pG[mi % 2][:],
                                            wgh[buf][:, k * 1024 + mi * 128:
                                                     k * 1024 + (mi + 1) * 128],
                                            xin(k), start=(k == 0), stop=(k == FT - 1))
                                    smm.inc(inst)
                                    if sidx >= 2:
                                        cc, m2 = divmod(sidx - 2, HK)
                                        t.wait_ge(svv.h, pV_act(h, cc, m2))
                                    elif h >= 1:
                                        t.wait_ge(svv.h,
                                                  pV_act(h - 1, NCH - 1, mi + HK - 2))
                                    for k in range(FT):
                                        inst = t.matmul(
                                            pU[mi % 2][:],
                                            wuh[buf][:, k * 1024 + mi * 128:
                                                     k * 1024 + (mi + 1) * 128],
                                            xin(k), start=(k == 0), stop=(k == FT - 1))
                                    smm.inc(inst)
                                if c >= 1:
                                    dgroup(c - 1)
                            dgroup(NCH - 1)
                            assert smm.n == Mb[h + 1], (h, smm.n, Mb[h + 1])

                    @b.scalar
                    def _(s):
                        for h in range(NH):
                            if h == 0:
                                for j in (0, 1):
                                    s.wait_ge(dG.h,
                                              CAST0 + 48 * (j // 2 + 1))
                                    gsrc, usrc, dsrc = wsrc(j)
                                    self.wload(s, rW, wgh[j % 2], gsrc, FT)
                                    self.wload(s, rW, wuh[j % 2], usrc, FT)
                                s.wait_ge(dG.h, CAST0 + 48)
                                _, _, dsrc = wsrc(0)
                                self.wload(s, rW, wdh, dsrc, HK)
                            else:
                                if h + 1 < NH:
                                    j = h + 1
                                    s.wait_ge(smm.h, Mb[j - 1])
                                    s.wait_ge(dG.h, CAST0 + 48 * (j // 2 + 1))
                                    gsrc, usrc, dsrc = wsrc(j)
                                    self.wload(s, rW, wgh[j % 2], gsrc, FT)
                                    self.wload(s, rW, wuh[j % 2], usrc, FT)
                                s.wait_ge(smm.h, Mb[h])
                                s.wait_ge(dG.h, CAST0 + 48 * (h // 2 + 1))
                                _, _, dsrc = wsrc(h)
                                self.wload(s, rW, wdh, dsrc, HK)
                            for c in range(NCH):
                                for mi in range(HK):
                                    sidx = HK * c + mi
                                    s.wait_ge(smm.h, pM_G(h, c, mi))
                                    if sidx >= 2:
                                        cc, m2 = divmod(sidx - 2, HK)
                                        s.wait_ge(svv.h, pV_act(h, cc, m2))
                                    elif h >= 1:
                                        s.wait_ge(svv.h,
                                                  pV_act(h - 1, NCH - 1, mi + HK - 2))
                                    sac.inc(s.activation(
                                        gs2[:, (mi % 2) * TC:(mi % 2 + 1) * TC],
                                        pG[mi % 2][:], AF.Silu))
                            assert sac.n == Ab[h + 1], (h, sac.n, Ab[h + 1])

                    @b.vector
                    def _(v):
                        for h in range(NH):
                            e_, hf_, nxs_, mbh, tvh = hparm(h)

                            def vacc(cA, h=h):
                                for mi in range(HK):
                                    v.wait_ge(smm.h, pM_D(h, cA, mi))
                                    if mi == 0:
                                        if h >= 1:
                                            v.wait_ge(r8[1].h, ld16(h, cA))
                                        if cA >= 2:
                                            v.wait_ge(r8[2].h, st16(h, cA - 2))
                                        elif h >= 1:
                                            v.wait_ge(r8[2].h,
                                                      st16(h - 1, cA + NCH - 2))
                                    accs = accb[cA % 2][:, mi * TC:(mi + 1) * TC]
                                    if h == 0:
                                        svv.inc(v.tensor_copy(accs, pD_[mi % 2][:]))
                                    else:
                                        svv.inc(v.tensor_add(accs, accs,
                                                             pD_[mi % 2][:]))

                            for c in range(NCH):
                                if e_ >= 0:
                                    v.wait_ge(smm.h, pM_pMB(h, c))
                                    if c >= 1:
                                        v.wait_ge(smm.h, pM_U(h, c - 1, HK - 1))
                                    else:
                                        pm = prev_masked(h)
                                        if pm is not None:
                                            v.wait_ge(smm.h,
                                                      pM_U(pm, NCH - 1, HK - 1))
                                    for ft in range(FT):
                                        svv.inc(v.tensor_mul(
                                            xsb[:, ft * TC:(ft + 1) * TC],
                                            XN[:, ft * T + c * TC:
                                               ft * T + (c + 1) * TC],
                                            pMB[:]))
                                for mi in range(HK):
                                    v.wait_ge(smm.h, pM_U(h, c, mi))
                                    v.wait_ge(sac.h, pA_silu(h, c, mi))
                                    if mi == 0:
                                        if c >= 2:
                                            v.wait_ge(smm.h, pM_D(h, c - 2, HK - 1))
                                        elif h >= 1:
                                            v.wait_ge(smm.h,
                                                      pM_D(h - 1, NCH - 2 + c, HK - 1))
                                    svv.inc(v.tensor_mul(
                                        actb[c % 2][:, mi * TC:(mi + 1) * TC],
                                        gs2[:, (mi % 2) * TC:(mi % 2 + 1) * TC],
                                        pU[mi % 2][:]))
                                if c >= 1:
                                    vacc(c - 1)
                            vacc(NCH - 1)
                            assert svv.n == Vb[h + 1], (h, svv.n, Vb[h + 1])
            stxn.close()
            if self.upto == "A3":
                return nc

            self.emit_lnrouter_phase(d_x, s_acc, attnc, s_xn1, s_x1, rw1, E1,
                                     False, s_mask1, False, None, LN_EPS, "a4")
            if self.upto == "A4":
                return nc

            with ExitStack() as ph:
                Ep = ph.enter_context
                winp = Ep(nc.sbuf_tensor("winp", [128, FT * 2 * DI], BF16))
                wxp = Ep(nc.sbuf_tensor("wxp", [128, 16 * 96], BF16))
                wdt = Ep(nc.sbuf_tensor("wdt", [DTR, DI], BF16))
                xn1c = Ep(nc.sbuf_tensor("xn1c", [128, FT * TC], BF16))
                xr = Ep(nc.sbuf_tensor("xr", [128, FT * TC], BF16))
                self.sq8 = Ep(nc.sbuf_tensor("sq8b", [128, FT * TC], F32))
                self.meanr = Ep(nc.sbuf_tensor("meanrb", [1, TC], F32))
                self.varr = Ep(nc.sbuf_tensor("varrb", [1, TC], F32))
                self.rstdr = Ep(nc.sbuf_tensor("rstdrb", [1, TC], F32))
                xp = Ep(nc.sbuf_tensor("xp", [128, 16 * (TC + 3)], F32))
                szst = Ep(nc.sbuf_tensor("szst", [128, 2 * TC], BF16))
                xincb = Ep(nc.sbuf_tensor("xincb", [128, 16 * TC], BF16))
                cacc = Ep(nc.sbuf_tensor("cacc", [128, 2 * TC], F32))
                dtbf = Ep(nc.sbuf_tensor("dtbf", [DTR, TC], BF16))
                bcsb = Ep(nc.sbuf_tensor("bcsb", [32, TC], F32))
                spa = Ep(nc.sbuf_tensor("spa", [128, 2 * TC], F32))
                spb = Ep(nc.sbuf_tensor("spb", [128, 2 * TC], F32))
                spc = Ep(nc.sbuf_tensor("spc", [128, 2 * TC], F32))
                delst = Ep(nc.sbuf_tensor("delst", [128, 2 * TC], BF16))
                pXZ = [Ep(nc.psum_tensor(f"pXZ{i}", [128, TC], F32)) for i in range(2)]
                self.pQ = Ep(nc.psum_tensor("pQb", [1, TC], F32))
                self.pS = None
                pRS = Ep(nc.psum_tensor("pRSb", [128, TC], F32))
                pXD = Ep(nc.psum_tensor("pXD", [96, TC], F32))
                pDl = [Ep(nc.psum_tensor(f"pDl{i}", [128, TC], F32)) for i in range(2)]

                with nc.Block(no_gpsimd_drain=True) as b:
                    @b.gpsimd
                    def _(g):
                        self.wload(g, dG, winp, d_inpt.ap(), FT)
                        self.wload(g, dG, wxp, d_xpt.ap(), 16)
                        self.wload(g, dG, wdt, d_dtt.ap(), 1)
                        g.wait_ge(dG.h, dG.n)
                        self.CF0 = dG.n
                        for e in range(E1):
                            dG.inc(g.dma_start(out=s_cf1[e], in_=d_fw1[e]), 16)
                            dG.inc(g.dma_start(out=s_cf2[e], in_=d_fw2[e]), 16)

                    @b.vector
                    def _(v):
                        svv.inc(v.memset(xp[:], 0.0))

                for c in range(NCH):
                    A0c, M0c, V0c = sac.n, smm.n, svv.n
                    with nc.Block(no_gpsimd_drain=True) as b:
                        @b.sync
                        def _(s):
                            self.adma(s, dS, xn1c, s_xn1, c, FT)
                        dSx = dS.n

                        @b.scalar
                        def _(s):
                            s.wait_ge(dS.h, dSx)
                            for ft in range(FT):
                                sac.inc(s.activation(
                                    self.sq8[:, ft * TC:(ft + 1) * TC],
                                    xn1c[:, ft * TC:(ft + 1) * TC], AF.Square))
                            s.wait_ge(svv.h, V0c + 1)
                            sac.inc(s.activation(self.rstdr[:], self.varr[:],
                                                 AF.Sqrt, bias=RMS_EPS))

                        @b.tensor
                        def _(t):
                            t.wait_ge(sac.h, A0c + 8)
                            for ft in range(FT):
                                inst = t.matmul(self.pQ[:], onesc[:],
                                                self.sq8[:, ft * TC:(ft + 1) * TC],
                                                start=(ft == 0), stop=(ft == FT - 1))
                            smm.inc(inst)
                            t.wait_ge(svv.h, V0c + 2)
                            smm.inc(t.matmul(pRS[:], onesr[:], self.rstdr[:],
                                             start=True, stop=True))

                        @b.vector
                        def _(v):
                            v.wait_ge(smm.h, M0c + 1)
                            svv.inc(v.tensor_scalar_mul(self.varr[:], self.pQ[:],
                                                        1.0 / D))
                            v.wait_ge(sac.h, A0c + 9)
                            svv.inc(v.reciprocal(self.rstdr[:], self.rstdr[:]))
                            v.wait_ge(smm.h, M0c + 2)
                            for ft in range(FT):
                                svv.inc(v.scalar_tensor_tensor(
                                    xr[:, ft * TC:(ft + 1) * TC],
                                    xn1c[:, ft * TC:(ft + 1) * TC],
                                    n1w[:, ft:ft + 1], pRS[:],
                                    ALU.mult, ALU.mult))
                    mmI0, acI0 = smm.n, sac.n
                    zb = [r8[0].n, r8[1].n]
                    with nc.Block(no_gpsimd_drain=True) as b:
                        @b.tensor
                        def _(t):
                            for mi in range(32):
                                if mi >= 2:
                                    t.wait_ge(sac.h, acI0 + (mi - 1))
                                for k in range(FT):
                                    inst = t.matmul(
                                        pXZ[mi % 2][:],
                                        winp[:, k * 2 * DI + mi * 128: k * 2 * DI + (mi + 1) * 128],
                                        xr[:, k * TC:(k + 1) * TC],
                                        start=(k == 0), stop=(k == FT - 1))
                                smm.inc(inst)

                        @b.scalar
                        def _(s):
                            for mi in range(32):
                                s.wait_ge(smm.h, mmI0 + mi + 1)
                                if mi < 16:
                                    dt = mi
                                    sac.inc(s.activation(
                                        xp[:, dt * (TC + 3) + 3:(dt + 1) * (TC + 3)],
                                        pXZ[mi % 2][:], AF.Copy))
                                else:
                                    sl = mi % 2
                                    s.wait_ge(r8[sl].h,
                                              zb[sl] + 16 * max(0, (mi - 18) // 2 + 1))
                                    sac.inc(s.activation(
                                        szst[:, (mi % 2) * TC:(mi % 2 + 1) * TC],
                                        pXZ[mi % 2][:], AF.Silu))

                        @b.sync
                        def _(s):
                            for mi in range(16, 32):
                                s.wait_ge(sac.h, acI0 + mi + 1)
                                r8[mi % 2].inc(s.dma_start(
                                    out=s_sz[(mi - 16) * 128:(mi - 15) * 128,
                                             c * TC:(c + 1) * TC],
                                    in_=szst[:, (mi % 2) * TC:(mi % 2 + 1) * TC]), 16)
                    vv0, ac0 = svv.n, sac.n
                    with nc.Block(no_gpsimd_drain=True) as b:
                        @b.vector
                        def _(v):
                            for dt in range(16):
                                x0_ = dt * (TC + 3)
                                ca = cacc[:, (dt % 2) * TC:(dt % 2 + 1) * TC]
                                if dt >= 2:
                                    v.wait_ge(sac.h, ac0 + (dt - 1))
                                svv.inc(v.tensor_single_scalar(
                                    ca, xp[:, x0_:x0_ + TC],
                                    convw[:, dt * 4:dt * 4 + 1], ALU.mult))
                                for j in range(1, 4):
                                    v.wait_ge(svv.h, svv.n)
                                    svv.inc(v.scalar_tensor_tensor(
                                        ca, xp[:, x0_ + j:x0_ + j + TC],
                                        convw[:, dt * 4 + j:dt * 4 + j + 1],
                                        ca, ALU.mult, ALU.add))
                                v.wait_ge(svv.h, svv.n)
                                svv.inc(v.tensor_copy(xp[:, x0_:x0_ + 3],
                                                      xp[:, x0_ + TC:x0_ + TC + 3]))

                        @b.scalar
                        def _(s):
                            for dt in range(16):
                                s.wait_ge(svv.h, vv0 + dt * 5 + 4)
                                sac.inc(s.activation(
                                    xincb[:, dt * TC:(dt + 1) * TC],
                                    cacc[:, (dt % 2) * TC:(dt % 2 + 1) * TC],
                                    AF.Silu, bias=convb[:, dt:dt + 1]))
                    with nc.Block(no_gpsimd_drain=True) as b:
                        @b.sync
                        def _(s):
                            self.adma(s, dS, xincb, s_xinc, c, 16, store=True)
                            s.wait_ge(dS.h, dS.n)
                    with nc.Block(no_gpsimd_drain=True) as b:
                        @b.tensor
                        def _(t):
                            for k in range(16):
                                inst = t.matmul(pXD[:], wxp[:, k * 96:(k + 1) * 96],
                                                xincb[:, k * TC:(k + 1) * TC],
                                                start=(k == 0), stop=(k == 15))
                            smm.inc(inst)
                    with nc.Block(no_gpsimd_drain=True) as b:
                        @b.scalar
                        def _(s):
                            sac.inc(s.activation(dtbf[:], pXD[0:DTR, :], AF.Copy))

                        @b.vector
                        def _(v):
                            svv.inc(v.tensor_copy(bcsb[:], pXD[DTR:96, :]))

                        @b.sync
                        def _(s):
                            s.wait_ge(svv.h, svv.n)
                            dS.inc(s.dma_start(out=s_bc[:, c * TC:(c + 1) * TC],
                                               in_=bcsb[:]), 16)
                            s.wait_ge(dS.h, dS.n)
                    mmT0, acT0, vvT0 = smm.n, sac.n, svv.n
                    db2 = [r8[2].n, r8[3].n]
                    with nc.Block(no_gpsimd_drain=True) as b:
                        @b.tensor
                        def _(t):
                            for mi in range(16):
                                if mi >= 2:
                                    t.wait_ge(svv.h, vvT0 + 2 * (mi - 1))
                                    t.wait_ge(sac.h, acT0 + 3 * (mi - 1))
                                inst = t.matmul(pDl[mi % 2][:],
                                                wdt[:, mi * 128:(mi + 1) * 128], dtbf[:],
                                                start=True, stop=True)
                                smm.inc(inst)

                        @b.scalar
                        def _(s):
                            for mi in range(16):
                                sl = mi % 2
                                s.wait_ge(smm.h, mmT0 + mi + 1)
                                sac.inc(s.activation(spa[:, sl * TC:(sl + 1) * TC],
                                                     pDl[sl][:], AF.Abs,
                                                     bias=dtb[:, mi:mi + 1]))
                                s.wait_ge(sac.h, sac.n)
                                sac.inc(s.activation(spb[:, sl * TC:(sl + 1) * TC],
                                                     spa[:, sl * TC:(sl + 1) * TC],
                                                     AF.Exp, scale=-1.0))
                                s.wait_ge(sac.h, sac.n)
                                sac.inc(s.activation(spa[:, sl * TC:(sl + 1) * TC],
                                                     spb[:, sl * TC:(sl + 1) * TC],
                                                     AF.Ln, bias=1.0))

                        @b.vector
                        def _(v):
                            for mi in range(16):
                                sl = mi % 2
                                v.wait_ge(smm.h, mmT0 + mi + 1)
                                sl2 = mi % 2
                                v.wait_ge(r8[2 + sl2].h,
                                          db2[sl2] + 16 * max(0, (mi - 2) // 2 + 1))
                                svv.inc(v.tensor_scalar(spc[:, sl * TC:(sl + 1) * TC],
                                                        pDl[sl][:], dtb[:, mi:mi + 1],
                                                        0.0, ALU.add, ALU.max))
                                v.wait_ge(sac.h, acT0 + 3 * mi + 3)
                                v.wait_ge(svv.h, svv.n)
                                svv.inc(v.tensor_add(delst[:, sl * TC:(sl + 1) * TC],
                                                     spc[:, sl * TC:(sl + 1) * TC],
                                                     spa[:, sl * TC:(sl + 1) * TC]))

                        @b.sync
                        def _(s):
                            for mi in range(16):
                                s.wait_ge(svv.h, vvT0 + 2 * mi + 2)
                                r8[mi % 2 + 2].inc(s.dma_start(
                                    out=s_delta[mi * 128:(mi + 1) * 128,
                                                c * TC:(c + 1) * TC],
                                    in_=delst[:, (mi % 2) * TC:(mi % 2 + 1) * TC]), 16)
                with nc.Block(no_gpsimd_drain=True) as b:
                    @b.sync
                    def _(s):
                        s.wait_ge(dS.h, dS.n)
                        for rr in r8:
                            s.wait_ge(rr.h, rr.n)
            if self.upto == "B1":
                return nc

            with ExitStack() as ph:
                Ep = ph.enter_context
                delc = Ep(nc.sbuf_tensor("delc", [128, 16 * TC], BF16))
                xincc = Ep(nc.sbuf_tensor("xincc", [128, 16 * TC], BF16))
                bcb = Ep(nc.sbuf_tensor("bcb", [16, TC], F32))
                bcc = Ep(nc.sbuf_tensor("bcc", [16, TC], F32))
                brep = Ep(nc.sbuf_tensor("brep", [128, TC], BF16))
                crep = Ep(nc.sbuf_tensor("crep", [128, TC], BF16))
                carry = Ep(nc.sbuf_tensor("carry", [128, NTILE], BF16))
                SS = 4
                dA2 = Ep(nc.sbuf_tensor("dA2", [128, SS * TC], BF16))
                dBx2 = Ep(nc.sbuf_tensor("dBx2", [128, SS * TC], BF16))
                hb2 = Ep(nc.sbuf_tensor("hb2", [128, SS * TC], BF16))
                hc2 = Ep(nc.sbuf_tensor("hc2", [128, SS * TC], BF16))
                yst = Ep(nc.sbuf_tensor("yst", [8, 8 * TC], BF16))
                fw = Ep(nc.sbuf_tensor("fw", [128, FT * DFF], BF16))
                xnC = [Ep(nc.sbuf_tensor(f"xnC{i}", [128, FT * TC], BF16)) for i in range(2)]
                hcC = Ep(nc.sbuf_tensor("hcC", [128, 32 * TC], BF16))
                hstC = Ep(nc.sbuf_tensor("hstC", [128, 4 * TC], BF16))
                acb1 = Ep(nc.sbuf_tensor("acb1", [128, FT * TC], BF16))
                mrowA = Ep(nc.sbuf_tensor("mrowA", [1, T], BF16))
                mbsb = Ep(nc.sbuf_tensor("mbsb", [128, TC], F32))
                tscr = Ep(nc.sbuf_tensor("tscr", [128, 2 * TC], BF16))
                p16 = Ep(nc.sbuf_tensor("p16S", [16, 128], F32))
                selm = Ep(nc.sbuf_tensor("selmS", [128, 16 * 128], BF16))
                gmat = Ep(nc.sbuf_tensor("gmatS", [128, 8], BF16))
                a_s = Ep(nc.sbuf_tensor("a_sS", [128, NTILE], F32))
                pA_ = [Ep(nc.psum_tensor(f"pA_{i}", [128, TC], F32)) for i in range(2)]
                pB_ = [Ep(nc.psum_tensor(f"pB_{i}", [128, TC], F32)) for i in range(2)]
                pY = Ep(nc.psum_tensor("pY", [72, TC], F32))
                pCH = [Ep(nc.psum_tensor(f"pCH{i}", [128, TC], F32)) for i in range(2)]
                pMB1 = Ep(nc.psum_tensor("pMB1", [128, TC], F32))

                def b2_loads(s, c):
                    self.adma(s, rB, delc, s_delta, c, 16)
                    self.adma(s, rB, xincc, s_xinc, c, 16)
                    rB.inc(s.dma_start(out=bcb[:],
                                       in_=s_bc[0:16, c * TC:(c + 1) * TC]), 16)
                    rB.inc(s.dma_start(out=bcc[:],
                                       in_=s_bc[16:32, c * TC:(c + 1) * TC]), 16)

                with nc.Block() as b:
                    @b.sync
                    def _(s):
                        b2_loads(s, 0)
                        dS.inc(s.dma_start(out=p16[:], in_=d_p16.ap()), 16)
                        dS.inc(s.dma_start(out=a_s[:], in_=d_alog.ap()), 16)
                        s.wait_ge(dS.h, dS.n)

                    @b.gpsimd
                    def _(g):
                        dG.inc(g.dma_start(out=selm[:], in_=d_sel.ap()), 16)
                        dG.inc(g.dma_start(out=gmat[:], in_=d_gmat.ap()), 16)
                        g.wait_ge(dG.h, dG.n)

                    @b.vector
                    def _(v):
                        svv.inc(v.memset(carry[:], 0.0))
                with nc.Block() as b:
                    @b.scalar
                    def _(s):
                        sac.inc(s.activation(a_s[:], a_s[:], AF.Exp))
                with nc.Block() as b:
                    @b.vector
                    def _(v):
                        svv.inc(v.tensor_scalar_mul(a_s[:], a_s[:], -1.0))

                LAG = 3

                def emit_p1(e):
                    DS0 = dS.n
                    XNB = [r8[6].n, r8[7].n]
                    HSB = [r8[i].n for i in range(4)]
                    M0, A0 = smm.n, sac.n
                    with nc.Block() as b:
                        @b.tensor
                        def _(t):
                            for u in range(256):
                                ci, mi = u // 32, u % 32
                                t.wait_ge(dS.h, DS0 + 32)
                                t.wait_ge(r8[6 + ci % 2].h,
                                          XNB[ci % 2] + 16 * (ci // 2 + 1))
                                if u >= 2:
                                    t.wait_ge(sac.h, A0 + u - 1)
                                for k in range(FT):
                                    inst = t.matmul(
                                        pCH[u % 2][:],
                                        fw[:, k * DFF + mi * 128: k * DFF + (mi + 1) * 128],
                                        xnC[ci % 2][:, k * TC:(k + 1) * TC],
                                        start=(k == 0), stop=(k == FT - 1))
                                smm.inc(inst)

                        @b.scalar
                        def _(s):
                            s.wait_ge(dG.h, self.CF0 + 16 * (2 * e + 1))
                            dS.inc(s.dma_start(
                                out=fw[:, 4 * DFF:8 * DFF].rearrange(
                                    "p (k m) -> p k m", k=4),
                                in_=s_cf1[e][512:1024, :].rearrange(
                                    "(k p) m -> p k m", k=4, p=128)), 16)
                            for u in range(256):
                                s.wait_ge(smm.h, M0 + u + 1)
                                s.wait_ge(r8[u % 4].h,
                                          HSB[u % 4] + 16 * max(0, (u - 4) // 4 + 1))
                                sac.inc(s.activation(
                                    hstC[:, (u % 4) * TC:(u % 4 + 1) * TC],
                                    pCH[u % 2][:], AF.Gelu,
                                    bias=fb1[:, e * 32 + (u % 32):e * 32 + (u % 32) + 1]))

                        @b.sync
                        def _(s):
                            s.wait_ge(dG.h, self.CF0 + 16 * (2 * e + 1))
                            dS.inc(s.dma_start(
                                out=fw[:, 0:4 * DFF].rearrange(
                                    "p (k m) -> p k m", k=4),
                                in_=s_cf1[e][0:512, :].rearrange(
                                    "(k p) m -> p k m", k=4, p=128)), 16)
                            self.adma(s, r8[6], xnC[0], s_xn1, 0, FT)
                            self.adma(s, r8[7], xnC[1], s_xn1, 1, FT)
                            for u in range(256):
                                ci = u // 32
                                s.wait_ge(sac.h, A0 + u + 1)
                                r8[u % 4].inc(s.dma_start(
                                    out=s_h1[(u % 32) * 128:(u % 32 + 1) * 128,
                                             ci * TC:(ci + 1) * TC],
                                    in_=hstC[:, (u % 4) * TC:(u % 4 + 1) * TC]), 16)
                                if u % 32 == 31 and ci + 2 < NCH:
                                    s.wait_ge(smm.h, M0 + ci * 32 + 32)
                                    self.adma(s, r8[6 + (ci + 2) % 2], xnC[(ci + 2) % 2],
                                              s_xn1, ci + 2, FT)

                def emit_jb(jb):
                    c = jb
                    e = jb // 2
                    cbase = 4 * (jb % 2)
                    PM, PA, PV, PG = {}, {}, {}, {}
                    n = [0]

                    def put(dic, tag, k=1):
                        n[0] += k
                        dic[tag] = n[0]
                    n[0] = 0
                    put(PM, "repB"); put(PM, "repC")
                    for i in range(NTILE):
                        if i >= LAG:
                            put(PM, ("y", i - LAG))
                        put(PM, ("a", i)); put(PM, ("b", i))
                        if i % 64 == 0:
                            put(PM, ("pmb", i // 64))
                        if i % 8 == 7:
                            put(PM, ("cm", i // 8))
                    for i in range(NTILE - LAG, NTILE):
                        put(PM, ("y", i))
                    n[0] = 0
                    for i in range(NTILE):
                        put(PA, ("dA", i))
                        if i >= LAG:
                            put(PA, ("yev", i - LAG))
                    for i in range(NTILE - LAG, NTILE):
                        put(PA, ("yev", i))
                    n[0] = 0
                    put(PV, "repBe"); put(PV, "repCe")
                    for i in range(NTILE):
                        if i % 64 == 16:
                            put(PV, ("mbe", i // 64))
                        put(PV, ("dBx", i)); put(PV, ("scan", i))
                        if i >= 16 and (i - 16) % 8 == 0:
                            put(PV, ("ev", (i - 16) // 8), 1 if e == 0 else 2)
                    for u in range(30, 32):
                        put(PV, ("ev", u), 1 if e == 0 else 2)
                    n[0] = 0
                    for dt in range(16):
                        put(PG, ("dx", dt))
                    for i in range(NTILE):
                        put(PG, ("hc", i))
                        if i % 4 == 3:
                            put(PG, ("cy", i))

                    M0, A0, V0, G0 = smm.n, sac.n, svv.n, sgp.n
                    DSJ = dS.n
                    RB0 = rB.n
                    RH0, RM0, RA0 = rH.n, rM.n, rA.n
                    HSB = [r8[i].n for i in range(4)]
                    YS0 = [r8[4].n, r8[5].n]
                    with nc.Block() as b:
                        @b.gpsimd
                        def _(g):
                            g.wait_ge(rB.h, RB0)
                            for dt in range(16):
                                sgp.inc(g.tensor_mul(xincc[:, dt * TC:(dt + 1) * TC],
                                                     delc[:, dt * TC:(dt + 1) * TC],
                                                     xincc[:, dt * TC:(dt + 1) * TC]))
                            for i in range(NTILE):
                                ss4 = i % 4
                                g.wait_ge(svv.h, V0 + PV[("scan", i)])
                                if i >= 4:
                                    g.wait_ge(smm.h, M0 + PM[("y", i - 4)])
                                sgp.inc(g.tensor_mul(hc2[:, ss4 * TC:(ss4 + 1) * TC],
                                                     hb2[:, ss4 * TC:(ss4 + 1) * TC],
                                                     crep[:]))
                                if i % 4 == 3:
                                    sgp.inc(g.tensor_copy(
                                        carry[:, i - 3:i + 1],
                                        hb2[:].rearrange(
                                            "p (s t) -> p s t", s=4)[:, :, TC - 1]))

                        @b.tensor
                        def _(t):
                            t.wait_ge(rB.h, RB0)
                            smm.inc(t.matmul(pA_[0][:], p16[:], bcb[:],
                                             start=True, stop=True))
                            smm.inc(t.matmul(pB_[0][:], p16[:], bcc[:],
                                             start=True, stop=True))

                            def mmY(bp):
                                t.wait_ge(sgp.h, G0 + PG[("hc", bp)])
                                if bp >= 3:
                                    t.wait_ge(sac.h, A0 + PA[("yev", bp - 3)])
                                yo = 32 * (bp % 3)
                                smm.inc(t.matmul(
                                    pY[yo:yo + 8, :], gmat[:],
                                    hc2[:, (bp % 4) * TC:(bp % 4 + 1) * TC],
                                    start=True, stop=True))

                            for i in range(NTILE):
                                if i >= LAG:
                                    mmY(i - LAG)
                                sp = i % 2
                                j = i % 16
                                dt = i // 16
                                if i < 2:
                                    t.wait_ge(svv.h, V0 + PV["repBe"])
                                if i >= 2:
                                    t.wait_ge(sac.h, A0 + PA[("dA", i - 2)])
                                smm.inc(t.matmul(pA_[sp][:],
                                                 selm[:, j * 128:(j + 1) * 128],
                                                 delc[:, dt * TC:(dt + 1) * TC],
                                                 start=True, stop=True))
                                if i < 2:
                                    t.wait_ge(svv.h, V0 + PV["repCe"])
                                if i >= 2:
                                    t.wait_ge(svv.h, V0 + PV[("dBx", i - 2)])
                                t.wait_ge(sgp.h, G0 + PG[("dx", dt)])
                                smm.inc(t.matmul(pB_[sp][:],
                                                 selm[:, j * 128:(j + 1) * 128],
                                                 xincc[:, dt * TC:(dt + 1) * TC],
                                                 start=True, stop=True))
                                if i % 64 == 0:
                                    lci = i // 64
                                    t.wait_ge(rM.h, RM0 + 16)
                                    if lci >= 1:
                                        t.wait_ge(svv.h, V0 + PV[("mbe", lci - 1)])
                                    smm.inc(t.matmul(
                                        pMB1[:], onesrb[:],
                                        mrowA[:, (cbase + lci) * TC:(cbase + lci + 1) * TC],
                                        start=True, stop=True))
                                u = i // 8
                                jj = i % 8
                                lci, mi = u // 8, u % 8
                                if jj == 0:
                                    if jb % 2 == 0:
                                        t.wait_ge(dS.h, DSJ + 32)
                                    t.wait_ge(rH.h, RH0 + 16 * (lci + 1))
                                    if u >= 2:
                                        t.wait_ge(svv.h, V0 + PV[("ev", u - 2)])
                                for k in range(4 * jj, 4 * jj + 4):
                                    inst = t.matmul(
                                        pCH[u % 2][:],
                                        fw[:, k * D + mi * 128: k * D + (mi + 1) * 128],
                                        hcC[:, k * TC:(k + 1) * TC],
                                        start=(k == 0), stop=(k == 31))
                                if jj == 7:
                                    smm.inc(inst)
                            for i in range(NTILE - LAG, NTILE):
                                mmY(i)

                        @b.scalar
                        def _(s):
                            if jb % 2 == 0:
                                s.wait_ge(dG.h, self.CF0 + 16 * (2 * e + 2))
                                dS.inc(s.dma_start(
                                    out=fw[:, 16 * D:32 * D].rearrange(
                                        "p (k m) -> p k m", k=16),
                                    in_=s_cf2[e][2048:4096, :].rearrange(
                                        "(k p) m -> p k m", k=16, p=128)), 16)

                            def yev(bp):
                                s.wait_ge(smm.h, M0 + PM[("y", bp)])
                                if bp >= 8:
                                    gprev = bp // 8 - 1
                                    s.wait_ge(r8[4 + gprev % 2].h,
                                              YS0[gprev % 2] + 16 * (gprev // 2 + 1))
                                else:
                                    s.wait_ge(r8[4].h, YS0[0])
                                    s.wait_ge(r8[5].h, YS0[1])
                                yo = 32 * (bp % 3)
                                sac.inc(s.activation(
                                    yst[:, (bp % 8) * TC:(bp % 8 + 1) * TC],
                                    pY[yo:yo + 8, :], AF.Copy))
                            for i in range(NTILE):
                                sp, ss4 = i % 2, i % 4
                                s.wait_ge(smm.h, M0 + PM[("a", i)])
                                if i >= 4:
                                    s.wait_ge(svv.h, V0 + PV[("scan", i - 4)])
                                sac.inc(s.activation(dA2[:, ss4 * TC:(ss4 + 1) * TC],
                                                     pA_[sp][:], AF.Exp,
                                                     scale=a_s[:, i:i + 1]))
                                if i >= LAG:
                                    yev(i - LAG)
                            for i in range(NTILE - LAG, NTILE):
                                yev(i)

                        @b.vector
                        def _(v):
                            v.wait_ge(smm.h, M0 + PM["repB"])
                            svv.inc(v.tensor_copy(brep[:], pA_[0][:]))
                            v.wait_ge(smm.h, M0 + PM["repC"])
                            svv.inc(v.tensor_copy(crep[:], pB_[0][:]))
                            for i in range(NTILE):
                                sp, ss4 = i % 2, i % 4
                                if i % 64 == 16:
                                    lci = i // 64
                                    v.wait_ge(smm.h, M0 + PM[("pmb", lci)])
                                    svv.inc(v.tensor_copy(mbsb[:], pMB1[:]))
                                v.wait_ge(smm.h, M0 + PM[("b", i)])
                                svv.inc(v.tensor_mul(dBx2[:, ss4 * TC:(ss4 + 1) * TC],
                                                     pB_[sp][:], brep[:]))
                                v.wait_ge(sac.h, A0 + PA[("dA", i)])
                                v.wait_ge(svv.h, svv.n)
                                if i >= 4:
                                    v.wait_ge(sgp.h,
                                              G0 + PG[("cy", i - 4 - (i % 4) + 3)])
                                svv.inc(v.tensor_tensor_scan(
                                    hb2[:, ss4 * TC:(ss4 + 1) * TC],
                                    dA2[:, ss4 * TC:(ss4 + 1) * TC],
                                    dBx2[:, ss4 * TC:(ss4 + 1) * TC],
                                    carry[:, i:i + 1], ALU.mult, ALU.add))
                                if i >= 16 and (i - 16) % 8 == 0:
                                    u = (i - 16) // 8
                                    self._bc_ev(v, u, e, pCH, mbsb, tscr, acb1,
                                                fb2, M0, PM, V0, PV, RA0, rA,
                                                smm, svv, dS, sac)
                            for u in range(30, 32):
                                self._bc_ev(v, u, e, pCH, mbsb, tscr, acb1,
                                            fb2, M0, PM, V0, PV, RA0, rA,
                                            smm, svv, dS, sac)

                        @b.sync
                        def _(s):
                            if jb % 2 == 0:
                                s.wait_ge(dG.h, self.CF0 + 16 * (2 * e + 2))
                                dS.inc(s.dma_start(
                                    out=fw[:, 0:16 * D].rearrange(
                                        "p (k m) -> p k m", k=16),
                                    in_=s_cf2[e][0:2048, :].rearrange(
                                        "(k p) m -> p k m", k=16, p=128)), 16)
                            rM.inc(s.dma_start(out=mrowA[:],
                                               in_=s_mask1[e:e + 1, :]), 16)
                            for _ri in range(4):
                                s.wait_ge(r8[_ri].h, HSB[_ri])
                            self.adma(s, rH, hcC, s_h1, cbase, 32)
                            if e > 0:
                                s.wait_ge(rA.h, RA0)
                                self.adma(s, rA, acb1, s_acc1, cbase, FT)
                            for i in range(NTILE):
                                if i % 64 == 57:
                                    lci = i // 64
                                    if lci + 1 < 4:
                                        s.wait_ge(smm.h, M0 + PM[("cm", 8 * lci + 7)])
                                        self.adma(s, rH, hcC, s_h1,
                                                  cbase + lci + 1, 32)
                                if i % 64 == 7 and i >= 64:
                                    lci = i // 64 - 1
                                    s.wait_ge(svv.h, V0 + PV[("ev", 8 * lci + 7)] + 2)
                                    self.adma(s, rA, acb1, s_acc1, cbase + lci, FT,
                                              store=True)
                                    if e > 0:
                                        s.wait_ge(rA.h, RA0 + 16 * (2 * lci + 2))
                                        self.adma(s, rA, acb1, s_acc1,
                                                  cbase + lci + 1, FT)
                                if i % 8 == 7:
                                    g8 = i // 8
                                    s.wait_ge(sac.h, A0 + PA[("yev", 8 * g8 + 7)])
                                    r8[4 + g8 % 2].inc(s.dma_start(
                                        out=s_y[64 * g8:64 * g8 + 64,
                                                c * TC:(c + 1) * TC].rearrange(
                                            "(j r) t -> r j t", j=8, r=8),
                                        in_=yst[:].rearrange("p (j t) -> p j t", j=8)), 16)
                            s.wait_ge(svv.h, svv.n)
                            self.adma(s, rA, acb1, s_acc1, cbase + 3, FT, store=True)
                            if c < NCH - 1:
                                s.wait_ge(smm.h, M0 + PM[("b", NTILE - 1)])
                                b2_loads(s, c + 1)

                for e in range(E1):
                    emit_p1(e)
                    emit_jb(2 * e)
                    emit_jb(2 * e + 1)
                with nc.Block() as b:
                    @b.sync
                    def _(s):
                        s.wait_ge(dS.h, dS.n)
                        for rr in r8 + [rB, rH, rM, rA]:
                            s.wait_ge(rr.h, rr.n)

            if self.upto in ("B2", "C"):
                return nc

            with ExitStack() as ph:
                Ep = ph.enter_context
                wout = Ep(nc.sbuf_tensor("wout", [128, 16 * D], BF16))
                yc = [Ep(nc.sbuf_tensor(f"yc{i}", [128, 16 * TC], BF16))
                      for i in range(2)]
                xincc = [Ep(nc.sbuf_tensor(f"xic2{i}", [128, 16 * TC], BF16))
                         for i in range(2)]
                szc = [Ep(nc.sbuf_tensor(f"szc{i}", [128, 16 * TC], BF16))
                       for i in range(2)]
                xn1c2 = [Ep(nc.sbuf_tensor(f"xn1c2{i}", [128, FT * TC], BF16))
                         for i in range(2)]
                yf = Ep(nc.sbuf_tensor("yf", [128, 16 * TC], BF16))
                ysc2 = Ep(nc.sbuf_tensor("ysc2", [128, 2 * TC], F32))
                x1ast = [Ep(nc.sbuf_tensor(f"x1ast{i}", [128, FT * TC], BF16))
                         for i in range(2)]
                pO = [Ep(nc.psum_tensor(f"pO{i}", [128, TC], F32)) for i in range(2)]
                rY, rS = r8[0], r8[1]
                Y0, S0 = rY.n, rS.n
                M0, V0 = smm.n, svv.n
                VS2, MS2 = 40, 8
                with nc.Block() as b:
                    @b.gpsimd
                    def _(g):
                        self.wload(g, dG, wout, d_outt.ap(), 16)

                    @b.sync
                    def _(s):
                        for c in range(NCH):
                            if c >= 2:
                                s.wait_ge(svv.h, V0 + VS2 * (c - 1))
                            self.adma(s, rY, yc[c % 2], s_y, c, 16)
                            self.adma(s, rY, xincc[c % 2], s_xinc, c, 16)
                            self.adma(s, rY, szc[c % 2], s_sz, c, 16)
                            self.adma(s, rY, xn1c2[c % 2], s_xn1, c, FT)
                            if c >= 1:
                                s.wait_ge(svv.h, V0 + VS2 * c)
                                self.adma(s, rS, x1ast[(c - 1) % 2], s_x1a, c - 1,
                                          FT, store=True)
                        s.wait_ge(svv.h, V0 + VS2 * NCH)
                        self.adma(s, rS, x1ast[(NCH - 1) % 2], s_x1a, NCH - 1,
                                  FT, store=True)
                        s.wait_ge(rS.h, rS.n)
                        s.wait_ge(rY.h, rY.n)

                    @b.tensor
                    def _(t):
                        t.wait_ge(dG.h, dG.n)
                        for c in range(NCH):
                            t.wait_ge(svv.h, V0 + VS2 * c + 32)
                            for mi in range(FT):
                                if mi >= 2:
                                    t.wait_ge(svv.h, V0 + VS2 * c + 32 + mi - 1)
                                elif c >= 1:
                                    t.wait_ge(svv.h, V0 + VS2 * (c - 1) + 32 + mi + 7)
                                for k in range(16):
                                    inst = t.matmul(
                                        pO[mi % 2][:],
                                        wout[:, k * D + mi * 128: k * D + (mi + 1) * 128],
                                        yf[:, k * TC:(k + 1) * TC],
                                        start=(k == 0), stop=(k == 15))
                                smm.inc(inst)
                        assert smm.n == M0 + MS2 * NCH

                    @b.vector
                    def _(v):
                        for c in range(NCH):
                            v.wait_ge(rY.h, Y0 + 64 * (c + 1))
                            if c >= 1:
                                v.wait_ge(smm.h, M0 + MS2 * c)
                            for dt in range(16):
                                sl = dt % 2
                                if dt >= 2:
                                    v.wait_ge(svv.h, V0 + VS2 * c + 2 * (dt - 2) + 2)
                                svv.inc(v.scalar_tensor_tensor(
                                    ysc2[:, sl * TC:(sl + 1) * TC],
                                    xincc[c % 2][:, dt * TC:(dt + 1) * TC],
                                    dcol[:, dt:dt + 1],
                                    yc[c % 2][:, dt * TC:(dt + 1) * TC],
                                    ALU.mult, ALU.add))
                                v.wait_ge(svv.h, svv.n)
                                svv.inc(v.tensor_mul(yf[:, dt * TC:(dt + 1) * TC],
                                                     ysc2[:, sl * TC:(sl + 1) * TC],
                                                     szc[c % 2][:, dt * TC:(dt + 1) * TC]))
                            for mi in range(FT):
                                v.wait_ge(smm.h, M0 + MS2 * c + mi + 1)
                                if mi == 0 and c >= 2:
                                    v.wait_ge(rS.h, S0 + 16 * (c - 1))
                                svv.inc(v.tensor_add(
                                    x1ast[c % 2][:, mi * TC:(mi + 1) * TC],
                                    pO[mi % 2][:],
                                    xn1c2[c % 2][:, mi * TC:(mi + 1) * TC]))
                        assert svv.n == V0 + VS2 * NCH
            if self.upto == "B2b":
                return nc

            with ExitStack() as ph:
                Ep = ph.enter_context
                w1s = Ep(nc.sbuf_tensor("w1s", [128, FT * DFF], BF16))
                w3s = Ep(nc.sbuf_tensor("w3s", [128, FT * DFF], BF16))
                x1ac = [Ep(nc.sbuf_tensor(f"x1ac{i}", [128, FT * TC], BF16))
                        for i in range(2)]
                h2 = Ep(nc.sbuf_tensor("h2", [128, FT * TC], BF16))
                sq8c = Ep(nc.sbuf_tensor("sq8c", [128, FT * TC], F32))
                varrc = Ep(nc.sbuf_tensor("varrc", [1, TC], F32))
                rstdc = Ep(nc.sbuf_tensor("rstdc", [1, TC], F32))
                gs2c = Ep(nc.sbuf_tensor("gs2c", [128, 2 * TC], BF16))
                abst = Ep(nc.sbuf_tensor("abst", [128, 4 * TC], BF16))
                pQc = Ep(nc.psum_tensor("pQc", [1, TC], F32))
                pRSc = Ep(nc.psum_tensor("pRSc", [128, TC], F32))
                pA1 = [Ep(nc.psum_tensor(f"pA1{i}", [128, TC], F32)) for i in range(2)]
                pB1 = [Ep(nc.psum_tensor(f"pB1{i}", [128, TC], F32)) for i in range(2)]
                rX, rS = r8[0], r8[1]
                X0, S0 = rX.n, rS.n
                M0, A0, V0 = smm.n, sac.n, svv.n
                MS3, AS3, VS3 = 66, 41, 42
                inv_d = 1.0 / D
                with nc.Block() as b:
                    @b.gpsimd
                    def _(g):
                        self.wload(g, dG, w1s, d_w1t.ap(), FT)
                        self.wload(g, dG, w3s, d_w3t.ap(), FT)

                    @b.sync
                    def _(s):
                        for c in range(NCH):
                            if c >= 2:
                                s.wait_ge(sac.h, A0 + AS3 * (c - 2) + 8)
                                s.wait_ge(svv.h, V0 + VS3 * (c - 2) + 10)
                            self.adma(s, rX, x1ac[c % 2], s_x1a, c, FT)
                            if c >= 1:
                                for mi in range(32):
                                    s.wait_ge(svv.h, V0 + VS3 * (c - 1) + 11 + mi)
                                    rS.inc(s.dma_start(
                                        out=s_ab[mi * 128:(mi + 1) * 128,
                                                 (c - 1) * TC:c * TC],
                                        in_=abst[:, (mi % 4) * TC:(mi % 4 + 1) * TC]), 16)
                        cp = NCH - 1
                        for mi in range(32):
                            s.wait_ge(svv.h, V0 + VS3 * cp + 11 + mi)
                            rS.inc(s.dma_start(
                                out=s_ab[mi * 128:(mi + 1) * 128, cp * TC:(cp + 1) * TC],
                                in_=abst[:, (mi % 4) * TC:(mi % 4 + 1) * TC]), 16)
                        s.wait_ge(rS.h, rS.n)
                        s.wait_ge(rX.h, rX.n)

                    @b.tensor
                    def _(t):
                        t.wait_ge(dG.h, dG.n)
                        for c in range(NCH):
                            t.wait_ge(sac.h, A0 + AS3 * c + 8)
                            if c >= 1:
                                t.wait_ge(svv.h, V0 + VS3 * (c - 1) + 1)
                            for ft in range(FT):
                                inst = t.matmul(pQc[:], onesc[:],
                                                sq8c[:, ft * TC:(ft + 1) * TC],
                                                start=(ft == 0), stop=(ft == FT - 1))
                            smm.inc(inst)
                            t.wait_ge(svv.h, V0 + VS3 * c + 2)
                            if c >= 1:
                                t.wait_ge(svv.h, V0 + VS3 * (c - 1) + 10)
                            smm.inc(t.matmul(pRSc[:], onesr[:], rstdc[:],
                                             start=True, stop=True))
                            for mi in range(32):
                                if mi == 0:
                                    t.wait_ge(svv.h, V0 + VS3 * c + 10)
                                if mi >= 2:
                                    t.wait_ge(sac.h, A0 + AS3 * c + 8 + mi)
                                elif c >= 1:
                                    t.wait_ge(sac.h, A0 + AS3 * (c - 1) + 40 + mi)
                                for k in range(FT):
                                    inst = t.matmul(
                                        pA1[mi % 2][:],
                                        w1s[:, k * DFF + mi * 128: k * DFF + (mi + 1) * 128],
                                        h2[:, k * TC:(k + 1) * TC],
                                        start=(k == 0), stop=(k == FT - 1))
                                smm.inc(inst)
                                if mi >= 2:
                                    t.wait_ge(svv.h, V0 + VS3 * c + 9 + mi)
                                elif c >= 1:
                                    t.wait_ge(svv.h, V0 + VS3 * (c - 1) + 41 + mi)
                                for k in range(FT):
                                    inst = t.matmul(
                                        pB1[mi % 2][:],
                                        w3s[:, k * DFF + mi * 128: k * DFF + (mi + 1) * 128],
                                        h2[:, k * TC:(k + 1) * TC],
                                        start=(k == 0), stop=(k == FT - 1))
                                smm.inc(inst)
                        assert smm.n == M0 + MS3 * NCH

                    @b.scalar
                    def _(s):
                        for c in range(NCH):
                            s.wait_ge(rX.h, X0 + 16 * (c + 1))
                            if c >= 1:
                                s.wait_ge(smm.h, M0 + MS3 * (c - 1) + 1)
                            for ft in range(FT):
                                sac.inc(s.activation(sq8c[:, ft * TC:(ft + 1) * TC],
                                                     x1ac[c % 2][:, ft * TC:(ft + 1) * TC],
                                                     AF.Square))
                            s.wait_ge(svv.h, V0 + VS3 * c + 1)
                            sac.inc(s.activation(rstdc[:], varrc[:], AF.Sqrt,
                                                 bias=RMS_EPS))
                            for mi in range(32):
                                s.wait_ge(smm.h, M0 + MS3 * c + 3 + 2 * mi)
                                if mi >= 2:
                                    s.wait_ge(svv.h, V0 + VS3 * c + 9 + mi)
                                elif c >= 1:
                                    s.wait_ge(svv.h, V0 + VS3 * (c - 1) + 41 + mi)
                                sac.inc(s.activation(
                                    gs2c[:, (mi % 2) * TC:(mi % 2 + 1) * TC],
                                    pA1[mi % 2][:], AF.Silu))
                        assert sac.n == A0 + AS3 * NCH

                    @b.vector
                    def _(v):
                        for c in range(NCH):
                            v.wait_ge(smm.h, M0 + MS3 * c + 1)
                            svv.inc(v.tensor_scalar_mul(varrc[:], pQc[:], inv_d))
                            v.wait_ge(sac.h, A0 + AS3 * c + 9)
                            svv.inc(v.reciprocal(rstdc[:], rstdc[:]))
                            v.wait_ge(smm.h, M0 + MS3 * c + 2)
                            for ft in range(FT):
                                svv.inc(v.scalar_tensor_tensor(
                                    h2[:, ft * TC:(ft + 1) * TC],
                                    x1ac[c % 2][:, ft * TC:(ft + 1) * TC],
                                    n2w[:, ft:ft + 1], pRSc[:],
                                    ALU.mult, ALU.mult))
                            for mi in range(32):
                                v.wait_ge(smm.h, M0 + MS3 * c + 4 + 2 * mi)
                                v.wait_ge(sac.h, A0 + AS3 * c + 10 + mi)
                                if mi >= 4:
                                    v.wait_ge(rS.h, S0 + 16 * (32 * c + mi - 3))
                                elif c >= 1:
                                    v.wait_ge(rS.h, S0 + 16 * (32 * (c - 1) + mi + 29))
                                svv.inc(v.tensor_mul(
                                    abst[:, (mi % 4) * TC:(mi % 4 + 1) * TC],
                                    gs2c[:, (mi % 2) * TC:(mi % 2 + 1) * TC],
                                    pB1[mi % 2][:]))
                        assert svv.n == V0 + VS3 * NCH
            if self.upto == "B3a":
                return nc

            with ExitStack() as ph:
                Ep = ph.enter_context
                w2s = Ep(nc.sbuf_tensor("w2s", [128, 32 * D], BF16))
                abc = [Ep(nc.sbuf_tensor(f"abc{i}", [128, 32 * TC], BF16))
                       for i in range(2)]
                x1ac2 = Ep(nc.sbuf_tensor("x1ac2", [128, FT * TC], BF16))
                x1fb = Ep(nc.sbuf_tensor("x1fb", [128, FT * TC], BF16))
                acfb = Ep(nc.sbuf_tensor("acfb", [128, FT * TC], BF16))
                tmpf = Ep(nc.sbuf_tensor("tmpf", [128, FT * TC], F32))
                osum = Ep(nc.sbuf_tensor("osum", [128, FT * TC], F32))
                pM2 = [Ep(nc.psum_tensor(f"pM2{i}", [128, TC], F32)) for i in range(2)]
                rX, rS = r8[0], r8[1]
                X0, S0 = rX.n, rS.n
                M0, V0, G0 = smm.n, svv.n, sgp.n
                MS4, VS4, GS4 = 8, 16, 8
                with nc.Block() as b:
                    @b.gpsimd
                    def _(g):
                        self.wload(g, dG, w2s, d_w2t.ap(), 32)
                        for c in range(NCH):
                            g.wait_ge(rX.h, X0 + 64 * (c + 1))
                            for mi in range(FT):
                                if c >= 1:
                                    g.wait_ge(svv.h, V0 + VS4 * (c - 1) + 2 * mi + 2)
                                sgp.inc(g.tensor_add(
                                    tmpf[:, mi * TC:(mi + 1) * TC],
                                    x1fb[:, mi * TC:(mi + 1) * TC],
                                    acfb[:, mi * TC:(mi + 1) * TC]))

                    @b.sync
                    def _(s):
                        for c in range(NCH):
                            if c >= 2:
                                s.wait_ge(smm.h, M0 + MS4 * (c - 1))
                            self.adma(s, rX, abc[c % 2], s_ab, c, 32)
                            if c >= 1:
                                s.wait_ge(svv.h, V0 + VS4 * (c - 1) + 15)
                                s.wait_ge(sgp.h, G0 + GS4 * c)
                            self.adma(s, rX, x1ac2, s_x1a, c, FT)
                            self.adma(s, rX, x1fb, s_x1, c, FT)
                            self.adma(s, rX, acfb, s_acc1, c, FT)
                            if c >= 1:
                                s.wait_ge(svv.h, V0 + VS4 * c)
                                self.adma(s, rS, osum, d_out, c - 1, FT, store=True)
                        s.wait_ge(svv.h, V0 + VS4 * NCH)
                        self.adma(s, rS, osum, d_out, NCH - 1, FT, store=True)
                        s.wait_ge(rS.h, rS.n)
                        s.wait_ge(rX.h, rX.n)

                    @b.tensor
                    def _(t):
                        t.wait_ge(dG.h, dG.n)
                        for c in range(NCH):
                            t.wait_ge(rX.h, X0 + 64 * c + 16)
                            for mi in range(FT):
                                if mi >= 2:
                                    t.wait_ge(svv.h, V0 + VS4 * c + 2 * mi - 3)
                                elif c >= 1:
                                    t.wait_ge(svv.h, V0 + VS4 * (c - 1) + 2 * mi + 13)
                                for k in range(32):
                                    inst = t.matmul(
                                        pM2[mi % 2][:],
                                        w2s[:, k * D + mi * 128: k * D + (mi + 1) * 128],
                                        abc[c % 2][:, k * TC:(k + 1) * TC],
                                        start=(k == 0), stop=(k == 31))
                                smm.inc(inst)
                        assert smm.n == M0 + MS4 * NCH

                    @b.vector
                    def _(v):
                        for c in range(NCH):
                            for mi in range(FT):
                                v.wait_ge(smm.h, M0 + MS4 * c + mi + 1)
                                if mi == 0:
                                    v.wait_ge(rX.h, X0 + 64 * c + 32)
                                    if c >= 1:
                                        v.wait_ge(rS.h, S0 + 16 * c)
                                svv.inc(v.tensor_add(
                                    osum[:, mi * TC:(mi + 1) * TC],
                                    pM2[mi % 2][:],
                                    x1ac2[:, mi * TC:(mi + 1) * TC]))
                                v.wait_ge(svv.h, svv.n)
                                v.wait_ge(sgp.h, G0 + GS4 * c + mi + 1)
                                svv.inc(v.tensor_add(
                                    osum[:, mi * TC:(mi + 1) * TC],
                                    osum[:, mi * TC:(mi + 1) * TC],
                                    tmpf[:, mi * TC:(mi + 1) * TC]))
                        assert svv.n == V0 + VS4 * NCH
            with nc.Block() as b:
                @b.sync
                def _(s):
                    s.wait_ge(dS.h, dS.n)
        return nc


def build(upto="D", debug=()):
    return Builder(upto, debug).build()


def _host_layouts(inputs):
    f32 = np.float32
    gi = lambda k: np.asarray(inputs[k], f32)
    A = {}
    A["wv_t"] = np.ascontiguousarray(gi("L0_attn_w_qkv")[2048:3072, :].T)
    A["wo_t"] = np.ascontiguousarray(gi("L0_attn_w_o").T)
    A["gate_w"] = np.ascontiguousarray(gi("L0_ffn_gate_w"))
    A["sh_gate"] = np.ascontiguousarray(gi("L0_ffn_sh_gate"))
    A["sh_up"] = np.ascontiguousarray(gi("L0_ffn_sh_up"))
    A["sh_down"] = np.ascontiguousarray(gi("L0_ffn_sh_down"))
    A["rt_gate"] = np.ascontiguousarray(gi("L0_ffn_rt_gate"))
    A["rt_up"] = np.ascontiguousarray(gi("L0_ffn_rt_up"))
    A["rt_down"] = np.ascontiguousarray(gi("L0_ffn_rt_down"))
    A["n1w"] = np.ascontiguousarray(gi("L1_attn_norm1_w").reshape(8, 128).T)
    A["n2w"] = np.ascontiguousarray(gi("L1_attn_norm2_w").reshape(8, 128).T)
    A["inproj_t"] = np.ascontiguousarray(gi("L1_attn_in_proj_w").T)
    cw = gi("L1_attn_conv_w").reshape(2048, 4)
    A["convw"] = np.ascontiguousarray(cw.reshape(16, 128, 4).transpose(1, 0, 2).reshape(128, 64))
    A["convb"] = np.ascontiguousarray(gi("L1_attn_conv_b").reshape(16, 128).T)
    A["xproj_t"] = np.ascontiguousarray(gi("L1_attn_x_proj_w").T)
    A["dtproj_t"] = np.ascontiguousarray(gi("L1_attn_dt_proj_w").T)
    A["dtb"] = np.ascontiguousarray(gi("L1_attn_dt_proj_b").reshape(16, 128).T)
    al = gi("L1_attn_A_log")
    A["alog_s"] = np.ascontiguousarray(al.reshape(256, 8, 16).transpose(1, 2, 0).reshape(128, 256))
    A["dcol"] = np.ascontiguousarray(gi("L1_attn_D").reshape(16, 128).T)
    A["outproj_t"] = np.ascontiguousarray(gi("L1_attn_out_proj_w").T)
    A["w1_t"] = np.ascontiguousarray(gi("L1_attn_w1").T)
    A["w3_t"] = np.ascontiguousarray(gi("L1_attn_w3").T)
    A["w2_t"] = np.ascontiguousarray(gi("L1_attn_w2").T)
    A["router_w"] = np.ascontiguousarray(gi("L1_ffn_router_w"))
    A["router_b"] = np.ascontiguousarray(gi("L1_ffn_router_b").reshape(1, 4))
    A["f_w1"] = np.ascontiguousarray(gi("L1_ffn_w1"))
    A["f_b1"] = np.ascontiguousarray(
        gi("L1_ffn_b1").reshape(4, 32, 128).transpose(2, 0, 1).reshape(128, 128))
    A["f_w2"] = np.ascontiguousarray(gi("L1_ffn_w2"))
    A["f_b2"] = np.ascontiguousarray(
        gi("L1_ffn_b2").reshape(4, 8, 128).transpose(2, 0, 1).reshape(128, 32))
    e8 = np.zeros((8, 8 * 128), f32)
    for e in range(8):
        e8[e, e * 128:(e + 1) * 128] = 1.0
    A["e8sel"] = e8
    A["ident"] = np.eye(128, dtype=f32)
    A["ones_col"] = np.ones((128, 1), f32)
    A["ones_row"] = np.ones((1, 128), f32)
    p16 = np.zeros((16, 128), f32)
    for m in range(128):
        p16[m % 16, m] = 1.0
    A["p16"] = p16
    sel = np.zeros((128, 16 * 128), f32)
    for j in range(16):
        for m in range(128):
            sel[j * 8 + m // 16, j * 128 + m] = 1.0
    A["selmat"] = sel
    gm = np.zeros((128, 8), f32)
    for k in range(128):
        gm[k, k // 16] = 1.0
    A["gmat"] = gm
    return A


def kernel(**inputs):
    x = np.asarray(inputs["L0_attn_x"], np.float32)
    W = _host_layouts(inputs)
    nc = build()
    in_maps = []
    for c in range(8):
        m = dict(W)
        m["x"] = np.ascontiguousarray(x[c].T)
        in_maps.append(m)
    res = run_bass_kernel_spmd(nc, in_maps, core_ids=list(range(8)))
    out = np.stack([np.asarray(r["out"]).T for r in res.results], axis=0)
    return np.ascontiguousarray(out).astype(np.float32)
